# version 18
# baseline (speedup 1.0000x reference)
import contextlib
import numpy as np
import concourse.bass as bass
import concourse.mybir as mybir
from concourse.bass_utils import run_bass_kernel_spmd

F32 = mybir.dt.float32
BF16 = mybir.dt.bfloat16
AF = mybir.ActivationFunctionType
ALU = mybir.AluOpType

ENGINES = ("sync", "scalar", "vector", "tensor", "gpsimd")
NCORES = 8
D = 1024
DT = 8
F = 4096
FT = 32
B = 4
LN_EPS = 1e-5
WKV_EPS = 1e-8
ATT_EPS = 1e-6


class CFG:
    def __init__(self, sloc=1024, tca=256, se=128, taps=()):
        self.SLOC = sloc
        self.TOK = B * sloc
        self.TCA = tca
        self.SE = se
        self.TPB = sloc // tca
        self.NTT = B * self.TPB
        self.NTE = sloc // se
        self.taps = tuple(taps)
        assert sloc % tca == 0 and sloc % se == 0


class Prog:
    def __init__(self, nc, stack, semnames):
        self.nc = nc
        self.items = {e: [] for e in ENGINES}
        self.sems = {n: stack.enter_context(nc.semaphore(n)) for n in semnames}
        self.cnt = {n: 0 for n in semnames}
        self.waited = {}
        self.nbar = 0

    CHAIN = {"vector": "v", "scalar": "s", "gpsimd": "g"}

    def emit(self, eng, build):
        ce = self.CHAIN.get(eng)
        if ce is not None:
            self.inc(eng, build, ce, 1)
        else:
            self.items[eng].append(build)

    def inc(self, eng, build, sem, k=1, chain=True):
        ce = self.CHAIN.get(eng)
        if ce is not None and chain:
            lo = getattr(self, "last_op", {}).get(eng)
            if lo:
                self.wait(eng, lo[0], lo[1])
        self.cnt[sem] += k
        v = self.cnt[sem]
        s = self.sems[sem]
        def f(e, b=build, s=s, k=k):
            b(e).then_inc(s, k)
        self.items[eng].append(f)
        if ce is not None and chain:
            self.last_op = getattr(self, "last_op", {})
            self.last_op[eng] = (sem, v)
        return v

    def wait(self, eng, sem, v):
        if v <= 0:
            return
        key = (eng, sem)
        if self.waited.get(key, 0) >= v:
            return
        self.waited[key] = v
        s = self.sems[sem]
        self.items[eng].append(lambda e, s=s, v=v: e.wait_ge(s, v))

    def need(self, eng, toks):
        for sem, v in toks:
            self.wait(eng, sem, v)

    def barrier(self):
        for i in range(4):
            self.wait("sync", f"ld{i}", self.cnt[f"ld{i}"])
            self.wait("sync", f"st{i}", self.cnt[f"st{i}"])
        for i in range(2):
            self.wait("gpsimd", f"gld{i}", self.cnt[f"gld{i}"])
            self.wait("sync", f"hst{i}", self.cnt[f"hst{i}"])
        self.nbar += 1
        target = 5 * self.nbar
        s = self.sems["bar"]
        for eng in ENGINES:
            def f(e, s=s):
                e.drain()
                e.sem_inc(s, 1)
            self.items[eng].append(f)
        self.cnt["bar"] = target
        for eng in ENGINES:
            self.waited[(eng, "bar")] = target
            self.items[eng].append(
                lambda e, s=s, t=target: e.wait_ge(s, t))

    GSIZE = {"ld": 4, "st": 4, "gld": 2, "hst": 2}

    def dma(self, q, group, out, in_):
        self.rot = getattr(self, "rot", {})
        i = self.rot[group] = (self.rot.get(group, -1) + 1) % self.GSIZE[group]
        sem = f"{group}{i}"
        self.wait(q, sem, self.cnt[sem])
        v = self.inc(q, lambda e, o=out, i_=in_: e.dma_start(out=o, in_=i_),
                     sem, 16, chain=False)
        return ((sem, v),)


class Buf:
    def __init__(self, t, off, dts, n):
        self.t = t
        self.off = off
        self.dts = dts
        self.n = n
        self.ready = ()
        self.guard = ()

    def sl(self, d, c0=0, n=None):
        n = self.n if n is None else n
        base = self.off + d * self.n + c0
        return self.t[:, base:base + n]

    def all(self):
        return self.t[:, self.off:self.off + self.dts * self.n]

    def rg(self, lo, w):
        return self.t[:, self.off + lo:self.off + lo + w]

    def v3(self):
        return self.all().rearrange("p (t n) -> p t n", t=self.dts)

    def v4(self, b):
        return self.all().rearrange("p (t b n) -> p t b n", t=self.dts, b=b)


class Alloc:
    def __init__(self, tensor, total):
        self.tensor = tensor
        self.total = total
        self.off = 0

    def take(self, dts, n):
        need = dts * n
        assert self.off + need <= self.total, \
            f"SBUF alloc overflow: {self.off}+{need}>{self.total}"
        b = Buf(self.tensor, self.off, dts, n)
        self.off += need
        return b

    def reset(self):
        self.off = 0


def t_mer(*toks_seq):
    out = {}
    for toks in toks_seq:
        for sem, v in toks:
            if out.get(sem, 0) < v:
                out[sem] = v
    return tuple(out.items())


def build_program(C: CFG):
    nc = bass.Bass("TRN2", target_bir_lowering=False)
    TOK = C.TOK
    N = C.TCA

    def din(name, shape, dt=F32):
        return nc.dram_tensor(name, list(shape), dt, kind="ExternalInput").ap()

    ap = {}
    ap["x0"] = din("x0", [D, TOK])
    for nm in ("WrT", "WwT", "WkT", "WvT", "WoT", "WrecT",
               "bWqT", "bWkT", "bWvT", "bWoT", "fWrT", "fWkT", "fWvT"):
        ap[nm] = din(nm, [D, D])
    ap["WkeyT"] = din("WkeyT", [D, F])
    ap["WvalT"] = din("WvalT", [F, D])
    for nm in ("g1", "b1", "g2", "b2", "td", "mur", "muk", "bq", "bk", "bv",
               "bo"):
        ap[nm] = din(nm, [128, DT])
    ap["pe"] = din("pe", [128, DT * B])
    ap["hmask"] = din("hmask", [128, 8])
    ap["hexpo"] = din("hexpo", [128, 8])
    ap["bmask"] = din("bmask", [128, 8])

    ap["y_out"] = nc.dram_tensor("y_out", [D, TOK], F32,
                                 kind="ExternalOutput").ap()

    def spill(name, rows, dt):
        return nc.dram_tensor(name, [rows, TOK], dt, kind="Internal").ap()

    ap["sp_h0"] = spill("sp_h0", D, F32)
    ap["sp_xn1"] = spill("sp_xn1", D, BF16)
    ap["sp_num"] = spill("sp_num", D, F32)
    ap["sp_den"] = spill("sp_den", D, F32)
    ap["sp_y"] = spill("sp_y", D, F32)
    ap["sp_xn2"] = spill("sp_xn2", D, BF16)
    ap["sp_rcm"] = spill("sp_rcm", D, F32)
    ap["sp_kcm"] = spill("sp_kcm", F, BF16)
    ap["sp_x1"] = spill("sp_x1", D, F32)
    ap["sp_hf1"] = spill("sp_hf1", D, F32)
    ap["sp_x2"] = spill("sp_x2", D, F32)
    ap["sp_hp"] = spill("sp_hp", D, BF16)
    ap["sp_x3"] = spill("sp_x3", D, F32)
    ap["sp_hf2"] = spill("sp_hf2", D, F32)
    ap["sp_wkv"] = spill("sp_wkv", D, F32)
    ap["sp_pq"] = spill("sp_pq", D, BF16)
    ap["sp_pk"] = spill("sp_pk", D, BF16)
    ap["sp_pv"] = spill("sp_pv", D, BF16)

    ap["SC_loc"] = nc.dram_tensor("SC_loc", [128, 64], F32, kind="Internal").ap()
    ap["SC_all"] = nc.dram_tensor("SC_all", [1024, 64], F32, kind="Internal").ap()
    ap["B1_loc"] = nc.dram_tensor("B1_loc", [128, 32], F32, kind="Internal").ap()
    ap["B1_all"] = nc.dram_tensor("B1_all", [1024, 32], F32, kind="Internal").ap()
    ap["B3_loc"] = nc.dram_tensor("B3_loc", [128, 32], F32, kind="Internal").ap()
    ap["B3_all"] = nc.dram_tensor("B3_all", [1024, 32], F32, kind="Internal").ap()

    taps = {}
    for nm in C.taps:
        rows = F if nm == "sp_kcm" else D
        dt = BF16 if nm in ("sp_xn1", "sp_xn2", "sp_kcm", "sp_hp") else F32
        taps[nm] = nc.dram_tensor("tap_" + nm, [rows, TOK], dt,
                                  kind="ExternalOutput").ap()

    def spv(name):
        return ap[name].rearrange("(t p) n -> p t n", p=128)

    with contextlib.ExitStack() as stack:
        E = stack.enter_context
        GENF = E(nc.sbuf_tensor("GENF", [128, 20736], F32))
        GENB = E(nc.sbuf_tensor("GENB", [128, 15360], BF16))
        WP = E(nc.sbuf_tensor("WP", [128, DT * F], BF16))
        MIS = E(nc.sbuf_tensor("MIS", [128, 2496], F32))
        ROWS = E(nc.sbuf_tensor("ROWS", [1, 5760], F32))
        ONES_B = E(nc.sbuf_tensor("ONES_B", [128, 129], BF16))
        OH2 = E(nc.sbuf_tensor("OH2", [128, 160], BF16))
        pb = [E(nc.psum_tensor(f"pb{i}", [128, 512], F32)) for i in range(8)]

        prog = Prog(nc, stack,
                    ["ld0", "ld1", "ld2", "ld3", "st0", "st1", "st2", "st3",
                     "gld0", "gld1", "hst0", "hst1", "pe", "v", "s", "g",
                     "cc", "bar"])

        mis = {}
        _mo = [0]
        def mtake(name, n):
            mis[name] = _mo[0]
            _mo[0] += n
            assert _mo[0] <= 2496
        for nm in ("g1", "b1", "g2", "b2", "td", "mur", "muk", "bq", "bk",
                   "bv", "bo", "bq1", "bk1", "etd"):
            mtake(nm, DT)
        mtake("pe", DT * B)
        mtake("hmask", 8); mtake("hexpo", 8); mtake("bmask", 8)
        mtake("carN", DT * B); mtake("carD", DT * B)
        mtake("gcN", DT * B); mtake("gcD", DT * B)
        mtake("coef", 8 * DT)
        mtake("Sg", 8 * 64)
        mtake("B1g", 8 * 32); mtake("bnd1", 32)
        mtake("B3g", 8 * 32); mtake("bnd3", 32)
        mtake("iota", C.SLOC)
        mtake("cst", 4)

        def M(name, c0=0, n=1):
            return MIS[:, mis[name] + c0: mis[name] + c0 + n]

        rws = {"mu": 0, "ex2": 512, "rstd": 1024, "onesf": 1536,
               "rden": 1664}
        def R(name, c0=0, n=1):
            return ROWS[0:1, rws[name] + c0: rws[name] + c0 + n]

        onesb_col = ONES_B[:, 128:129]
        allones = ONES_B[:, 0:128]

        alloc_f = Alloc(GENF, 20736)
        alloc_b = Alloc(GENB, 15360)

        bankfree = {}
        rowguard = [()]
        wready = [()]
        carry_edge = [0]

        pre_toks = []
        for nm in ("g1", "b1", "g2", "b2", "td", "mur", "muk", "bq", "bk",
                   "bv", "bo"):
            pre_toks.append(prog.dma("sync", "ld", M(nm, 0, DT), ap[nm]))
        pre_toks.append(prog.dma("sync", "ld", M("pe", 0, DT * B), ap["pe"]))
        pre_toks.append(prog.dma("sync", "ld", M("hmask", 0, 8), ap["hmask"]))
        pre_toks.append(prog.dma("sync", "ld", M("hexpo", 0, 8), ap["hexpo"]))
        pre_toks.append(prog.dma("sync", "ld", M("bmask", 0, 8), ap["bmask"]))
        pre_ld = t_mer(*pre_toks)

        gv = prog.inc("gpsimd", lambda e: e.iota(
            M("iota", 0, C.SLOC), pattern=[[1, C.SLOC]], base=1,
            channel_multiplier=0, allow_small_or_imprecise_dtypes=True), "g", 1)
        gv = prog.inc("gpsimd", lambda e: e.memset(ONES_B[:, :], 1.0), "g", 1)
        gv = prog.inc("gpsimd", lambda e: e.memset(OH2[:, :], 0.0), "g", 1)
        prog.wait("gpsimd", "g", gv)
        gv = prog.inc("gpsimd", lambda e: e.memset(OH2[:, 128:129], 1.0), "g", 1)
        gv = prog.inc("gpsimd", lambda e: e.memset(R("onesf", 0, 128), 1.0),
                      "g", 1)
        gv = prog.inc("gpsimd", lambda e: e.memset(M("cst", 0, 1), LN_EPS), "g", 1)
        gv = prog.inc("gpsimd", lambda e: e.memset(M("cst", 1, 1), -0.5), "g", 1)
        gv = prog.inc("gpsimd", lambda e: e.memset(M("cst", 2, 1), -1.0), "g", 1)
        nc.const_aps.register(F32, LN_EPS, M("cst", 0, 1))
        nc.const_aps.register(F32, -0.5, M("cst", 1, 1))
        nc.const_aps.register(F32, -1.0, M("cst", 2, 1))
        pre_g = (("g", gv),)
        prog.need("scalar", pre_g)
        prog.need("vector", pre_g)

        prog.need("vector", t_mer(pre_ld))
        vv = prog.inc("vector", lambda e: e.tensor_scalar(
            out=M("bq1", 0, DT), in0=M("bq", 0, DT), scalar1=1.0, scalar2=None,
            op0=ALU.add), "v", 1)
        vv = prog.inc("vector", lambda e: e.tensor_scalar(
            out=M("bk1", 0, DT), in0=M("bk", 0, DT), scalar1=1.0, scalar2=None,
            op0=ALU.add), "v", 1)
        pre_v = (("v", vv),)

        _wt = []

        def load_w(name, base, rows, cols):
            return prog.dma("gpsimd", "gld",
                            WP[:, base:base + (rows // 128) * cols].rearrange(
                                "p (t e) -> p t e", t=rows // 128),
                            ap[name].rearrange("(t p) e -> p t e", p=128))

        def wdone():
            t = t_mer(*_wt)
            _wt.clear()
            return t

        prog.need("scalar", pre_ld)
        pre_s_etd = prog.inc("scalar", lambda e: e.activation(
            out=M("etd", 0, DT), in_=M("td", 0, DT), func=AF.Exp), "s", 1)

        def ln_stats(x_bf, sq, n, src_toks):
            prog.need("vector", t_mer(src_toks, sq.guard, rowguard[0]))
            vv = prog.inc("vector", lambda e: e.tensor_mul(
                out=sq.all(), in0=x_bf.all(), in1=x_bf.all()), "v", 1)
            sq.ready = (("v", vv),)
            prog.need("tensor", t_mer(src_toks, sq.ready, pre_g,
                                      bankfree.get(4, ()), bankfree.get(7, ())))
            for d in range(DT):
                prog.emit("tensor", lambda e, d=d: e.matmul(
                    pb[4][0:1, 0:n], onesb_col, x_bf.sl(d, 0, n),
                    start=(d == 0), stop=(d == DT - 1)))
            for d in range(DT):
                bld = lambda e, d=d: e.matmul(
                    pb[7][0:1, 0:n], onesb_col, sq.sl(d, 0, n),
                    start=(d == 0), stop=(d == DT - 1))
                if d == DT - 1:
                    pev = prog.inc("tensor", bld, "pe", 1)
                else:
                    prog.emit("tensor", bld)
            sq.guard = (("pe", pev),)
            prog.wait("vector", "pe", pev)
            prog.emit("vector", lambda e: e.tensor_scalar(
                out=R("mu", 0, n), in0=pb[4][0:1, 0:n], scalar1=1.0 / D,
                scalar2=None, op0=ALU.mult))
            prog.emit("vector", lambda e: e.tensor_scalar(
                out=R("ex2", 0, n), in0=pb[7][0:1, 0:n], scalar1=1.0 / D,
                scalar2=None, op0=ALU.mult))
            prog.emit("vector", lambda e: e.tensor_mul(
                out=R("rstd", 0, n), in0=R("mu", 0, n), in1=R("mu", 0, n)))
            vv = prog.inc("vector", lambda e: e.tensor_tensor(
                out=R("ex2", 0, n), in0=R("ex2", 0, n), in1=R("rstd", 0, n),
                op=ALU.subtract), "v", 1)
            bankfree[4] = (("v", vv),)
            bankfree[7] = (("v", vv),)
            prog.need("scalar", t_mer((("v", vv),), rowguard[0]))
            prog.emit("scalar", lambda e: e.activation(
                out=R("rstd", 0, n), in_=R("ex2", 0, n), func=AF.Ln,
                bias=LN_EPS))
            svv = prog.inc("scalar", lambda e: e.activation(
                out=R("rstd", 0, n), in_=R("rstd", 0, n), func=AF.Exp,
                scale=-0.5), "s", 1)
            prog.need("tensor", t_mer((("v", vv),), (("s", svv),), pre_g,
                                      bankfree.get(5, ()), bankfree.get(6, ())))
            prog.emit("tensor", lambda e: e.matmul(
                pb[5][:, 0:n], R("onesf", 0, 128), R("mu", 0, n), start=True,
                stop=True))
            pev = prog.inc("tensor", lambda e: e.matmul(
                pb[6][:, 0:n], R("onesf", 0, 128), R("rstd", 0, n), start=True,
                stop=True), "pe", 1)
            rowguard[0] = (("pe", pev),)
            return pev

        def ln_apply(x_f32, out_buf, pev, n, extra=()):
            prog.need("vector", t_mer((("pe", pev),), x_f32.ready,
                                      out_buf.guard, extra))
            for d in range(DT):
                prog.emit("vector", lambda e, d=d: e.tensor_tensor(
                    out=out_buf.sl(d, 0, n), in0=x_f32.sl(d, 0, n),
                    in1=pb[5][:, 0:n], op=ALU.subtract))
            for d in range(DT):
                bld = lambda e, d=d: e.tensor_tensor(
                    out=out_buf.sl(d, 0, n), in0=out_buf.sl(d, 0, n),
                    in1=pb[6][:, 0:n], op=ALU.mult)
                if d == DT - 1:
                    vv = prog.inc("vector", bld, "v", 1)
                else:
                    prog.emit("vector", bld)
            out_buf.ready = (("v", vv),)
            bankfree[5] = (("v", vv),)
            bankfree[6] = (("v", vv),)
            return vv

        def ln_g(xf, xbf, sq, out, n, mu_sb, rs_sb, xf_toks=None):
            assert 2 * n <= 512
            prog.need("vector", t_mer(xbf.ready, sq.guard))
            gv = prog.inc("vector", lambda e, o=sq.all(), i=xbf.all():
                          e.tensor_mul(out=o, in0=i, in1=i), "v", 1)
            sq.ready = (("v", gv),)
            prog.need("tensor", t_mer(xbf.ready, sq.ready, pre_g,
                                      bankfree.get(4, ())))
            for d in range(DT):
                prog.emit("tensor", lambda e, d=d: e.matmul(
                    pb[4][0:1, 0:n], onesb_col, xbf.sl(d, 0, n),
                    start=(d == 0), stop=(d == DT - 1)))
            for d in range(DT):
                bld = lambda e, d=d: e.matmul(
                    pb[4][0:1, n:2 * n], onesb_col, sq.sl(d, 0, n),
                    start=(d == 0), stop=(d == DT - 1))
                if d == DT - 1:
                    pev = prog.inc("tensor", bld, "pe", 1)
                else:
                    prog.emit("tensor", bld)
            sq.guard = (("pe", pev),)
            xbf.guard = (("pe", pev),)
            prog.need("vector", t_mer((("pe", pev),), rowguard[0]))
            prog.emit("vector", lambda e: e.tensor_scalar(
                out=R("mu", 0, n), in0=pb[4][0:1, 0:n], scalar1=1.0 / D,
                scalar2=None, op0=ALU.mult))
            prog.emit("vector", lambda e: e.tensor_scalar(
                out=R("ex2", 0, n), in0=pb[4][0:1, n:2 * n], scalar1=1.0 / D,
                scalar2=None, op0=ALU.mult))
            prog.emit("vector", lambda e: e.tensor_mul(
                out=R("rstd", 0, n), in0=R("mu", 0, n), in1=R("mu", 0, n)))
            vv = prog.inc("vector", lambda e: e.tensor_tensor(
                out=R("ex2", 0, n), in0=R("ex2", 0, n), in1=R("rstd", 0, n),
                op=ALU.subtract), "v", 1)
            bankfree[4] = (("v", vv),)
            prog.need("scalar", t_mer((("v", vv),), rowguard[0]))
            prog.emit("scalar", lambda e: e.activation(
                out=R("rstd", 0, n), in_=R("ex2", 0, n), func=AF.Ln,
                bias=LN_EPS))
            svv = prog.inc("scalar", lambda e: e.activation(
                out=R("rstd", 0, n), in_=R("rstd", 0, n), func=AF.Exp,
                scale=-0.5), "s", 1)
            prog.need("tensor", t_mer((("v", vv),), (("s", svv),), pre_g,
                                      bankfree.get(5, ())))
            prog.emit("tensor", lambda e: e.matmul(
                pb[5][:, 0:n], R("onesf", 0, 128), R("mu", 0, n), start=True,
                stop=True))
            pev = prog.inc("tensor", lambda e: e.matmul(
                pb[5][:, n:2 * n], R("onesf", 0, 128), R("rstd", 0, n),
                start=True, stop=True), "pe", 1)
            rowguard[0] = (("pe", pev),)
            prog.need("vector", t_mer((("pe", pev),), out.guard,
                                      xf_toks if xf_toks else xf.ready))
            mb = pb[5][:, 0:n].rearrange("p n -> p () n").broadcast_to(
                [128, DT, n])
            rb = pb[5][:, n:2 * n].rearrange("p n -> p () n").broadcast_to(
                [128, DT, n])
            o3 = out.all().rearrange("p (t n) -> p t n", t=DT)
            i3 = xf.all().rearrange("p (t n) -> p t n", t=DT)
            prog.emit("vector", lambda e, o=o3, i=i3, m=mb: e.tensor_tensor(
                out=o, in0=i, in1=m, op=ALU.subtract))
            vv = prog.inc("vector", lambda e, o=o3, r=rb: e.tensor_tensor(
                out=o, in0=o, in1=r, op=ALU.mult), "v", 1)
            out.ready = (("v", vv),)
            bankfree[5] = out.ready
            return out.ready

        def proj(w_base, x_buf, n, n_e, n_k, epilogue, banks, we_stride=None,
                 single=False):
            stride = we_stride if we_stride else n_e * 128
            pair = 2 if (2 * n <= 512 and n_e % 2 == 0 and not single) else 1
            for ep in range(n_e // pair):
                bk = banks[ep % len(banks)]
                prog.need("tensor", t_mer(bankfree.get(bk, ()), x_buf.ready,
                                          wready[0]))
                for half in range(pair):
                    e = ep * pair + half
                    for k in range(n_k):
                        base = w_base + k * stride + e * 128
                        bld = lambda e_, k=k, base=base, bk=bk, half=half: \
                            e_.matmul(pb[bk][:, half * n:(half + 1) * n],
                                      WP[:, base:base + 128],
                                      x_buf.sl(k, 0, n),
                                      start=(k == 0), stop=(k == n_k - 1))
                        if k == n_k - 1 and half == pair - 1:
                            pev = prog.inc("tensor", bld, "pe", 1)
                        else:
                            prog.emit("tensor", bld)
                bankfree[bk] = epilogue(ep, bk, pev, ep * pair * n, pair * n)

        def pass_a():
            _wt.append(load_w("WwT", 0 * D * DT, D, D))
            _wt.append(load_w("WkT", 1 * D * DT, D, D))
            _wt.append(load_w("WvT", 2 * D * DT, D, D))
            wready[0] = wdone()

            x0f = [alloc_f.take(DT, N), alloc_f.take(DT, N)]
            h0 = alloc_f.take(DT, N)
            th = alloc_f.take(DT, N)
            ewk = alloc_f.take(DT, N)
            tb = alloc_f.take(DT, N)
            u = alloc_f.take(DT, N)
            numb = alloc_f.take(DT, N)
            denb = alloc_f.take(DT, N)
            x0b = alloc_b.take(DT, N)
            sqb = alloc_b.take(DT, N)
            h0b = alloc_b.take(DT, N)
            xn1 = [alloc_b.take(DT, N) for _ in range(3)]
            g1b = M("g1", 0, DT).rearrange("p t -> p t ()").broadcast_to(
                [128, DT, N])
            b1b = M("b1", 0, DT).rearrange("p t -> p t ()").broadcast_to(
                [128, DT, N])

            def stats_mm(xbf, sq):
                prog.need("tensor", t_mer(xbf.ready, sq.ready, pre_g,
                                          bankfree.get(4, ())))
                for d in range(DT):
                    prog.emit("tensor", lambda e, d=d, x=xbf: e.matmul(
                        pb[4][0:1, 0:N], onesb_col, x.sl(d), start=(d == 0),
                        stop=(d == DT - 1)))
                for d in range(DT):
                    bld = lambda e, d=d, q=sq: e.matmul(
                        pb[4][0:1, N:2 * N], onesb_col, q.sl(d),
                        start=(d == 0), stop=(d == DT - 1))
                    if d == DT - 1:
                        pev = prog.inc("tensor", bld, "pe", 1)
                    else:
                        prog.emit("tensor", bld)
                sq.guard = (("pe", pev),)
                xbf.guard = (("pe", pev),)
                return pev

            def rows_rstd(pev):
                prog.need("vector", t_mer((("pe", pev),), rowguard[0]))
                prog.emit("vector", lambda e: e.tensor_scalar(
                    out=R("mu", 0, N), in0=pb[4][0:1, 0:N], scalar1=1.0 / D,
                    scalar2=None, op0=ALU.mult))
                prog.emit("vector", lambda e: e.tensor_scalar(
                    out=R("ex2", 0, N), in0=pb[4][0:1, N:2 * N],
                    scalar1=1.0 / D, scalar2=None, op0=ALU.mult))
                prog.emit("vector", lambda e: e.tensor_mul(
                    out=R("rstd", 0, N), in0=R("mu", 0, N),
                    in1=R("mu", 0, N)))
                vv = prog.inc("vector", lambda e: e.tensor_tensor(
                    out=R("ex2", 0, N), in0=R("ex2", 0, N),
                    in1=R("rstd", 0, N), op=ALU.subtract), "v", 1)
                bankfree[4] = (("v", vv),)
                prog.need("scalar", t_mer((("v", vv),), rowguard[0]))
                prog.emit("scalar", lambda e: e.activation(
                    out=R("rstd", 0, N), in_=R("ex2", 0, N), func=AF.Ln,
                    bias=LN_EPS))
                svv = prog.inc("scalar", lambda e: e.activation(
                    out=R("rstd", 0, N), in_=R("rstd", 0, N), func=AF.Exp,
                    scale=-0.5), "s", 1)
                return vv, svv

            def bcast(vv, svv):
                prog.need("tensor", t_mer((("v", vv),), (("s", svv),), pre_g,
                                          bankfree.get(5, ())))
                prog.emit("tensor", lambda e: e.matmul(
                    pb[5][:, 0:N], R("onesf", 0, 128), R("mu", 0, N),
                    start=True, stop=True))
                pev = prog.inc("tensor", lambda e: e.matmul(
                    pb[5][:, N:2 * N], R("onesf", 0, 128), R("rstd", 0, N),
                    start=True, stop=True), "pe", 1)
                rowguard[0] = (("pe", pev),)
                return pev

            def apply_v(xf, out, pev, xf_toks=None):
                prog.need("vector", t_mer((("pe", pev),), out.guard,
                                          xf_toks if xf_toks else xf.ready))
                mb = pb[5][:, 0:N].rearrange("p n -> p () n").broadcast_to(
                    [128, DT, N])
                rb = pb[5][:, N:2 * N].rearrange("p n -> p () n").broadcast_to(
                    [128, DT, N])
                o3 = out.v3()
                i3 = xf.v3()
                prog.emit("vector", lambda e, o=o3, i=i3, m=mb:
                          e.tensor_tensor(out=o, in0=i, in1=m,
                                          op=ALU.subtract))
                vv = prog.inc("vector", lambda e, o=o3, r=rb: e.tensor_tensor(
                    out=o, in0=o, in1=r, op=ALU.mult), "v", 1)
                out.ready = (("v", vv),)
                bankfree[5] = out.ready
                return vv

            st = {}

            def front_p1(i):
                b, j = divmod(i, C.TPB)
                col0 = b * C.SLOC + j * C.TCA
                xf = x0f[i % 2]
                prog.need("sync", xf.guard)
                lv = prog.dma("sync", "ld", xf.v3(),
                              spv("x0")[:, :, col0:col0 + N])
                xf.ready = lv
                prog.need("vector", t_mer(xf.ready, x0b.guard))
                gv = prog.inc("vector", lambda e, o=x0b.all(), i_=xf.all():
                              e.tensor_copy(out=o, in_=i_), "v", 1)
                x0b.ready = (("v", gv),)
                prog.need("vector", t_mer(x0b.ready, sqb.guard))
                gv = prog.inc("vector", lambda e, o=sqb.all(), i_=x0b.all():
                              e.tensor_mul(out=o, in0=i_, in1=i_), "v", 1)
                sqb.ready = (("v", gv),)
                st[i] = dict(col0=col0, b=b, j=j, xf=xf)
                st[i]["pev0"] = stats_mm(x0b, sqb)

            def front_p2(i):
                t = st[i]
                vv, svv = rows_rstd(t["pev0"])
                pev = bcast(vv, svv)
                apply_v(t["xf"], h0, pev)
                t["xf"].guard = h0.ready
                prog.need("gpsimd", h0.ready)
                t["st_h0"] = prog.dma(
                    "gpsimd", "st",
                    spv("sp_h0")[:, :, t["col0"]:t["col0"] + N], h0.v3())
                prog.need("vector", t_mer(h0.ready, h0b.guard))
                gv = prog.inc("vector", lambda e, o=h0b.all(), i_=h0.all():
                              e.tensor_copy(out=o, in_=i_), "v", 1)
                h0b.ready = (("v", gv),)
                prog.need("vector", t_mer(h0b.ready, sqb.guard))
                gv = prog.inc("vector", lambda e, o=sqb.all(), i_=h0b.all():
                              e.tensor_mul(out=o, in0=i_, in1=i_), "v", 1)
                sqb.ready = (("v", gv),)
                t["pev1"] = stats_mm(h0b, sqb)

            def front_p3(i):
                t = st[i]
                xn = xn1[i % 3]
                vv, svv = rows_rstd(t["pev1"])
                pev = bcast(vv, svv)
                apply_v(h0, th, pev)
                h0.guard = t_mer(th.ready, t["st_h0"])
                prog.need("vector", t_mer(th.ready, xn.guard, pre_ld))
                th3 = th.v3()
                prog.emit("vector", lambda e, o=th3, g=g1b: e.tensor_tensor(
                    out=o, in0=o, in1=g, op=ALU.mult))
                vv = prog.inc("vector", lambda e, o=xn.v3(), i_=th3, bb_=b1b:
                              e.tensor_tensor(out=o, in0=i_, in1=bb_,
                                              op=ALU.add), "v", 1)
                xn.ready = (("v", vv),)
                th.guard = xn.ready
                prog.need("gpsimd", xn.ready)
                st_xn = prog.dma(
                    "gpsimd", "st",
                    spv("sp_xn1")[:, :, t["col0"]:t["col0"] + N], xn.v3())
                xn.guard = st_xn

            def back_pw(i):
                xn = xn1[i % 3]
                def epi_w(ep, bk, pev, lo, w):
                    prog.need("scalar", t_mer((("pe", pev),), ewk.guard))
                    sv = prog.inc("scalar", lambda eng, bk=bk, o=ewk.rg(lo, w), w=w:
                                  eng.activation(out=o, in_=pb[bk][:, 0:w],
                                                 func=AF.Exp), "s", 1)
                    return (("s", sv),)
                proj(0 * D * DT, xn, N, DT, DT, epi_w, banks=(0, 1, 2, 3, 6, 7))
                st[i]["sv_w"] = prog.cnt["s"]

            def back_pk(i):
                xn = xn1[i % 3]
                sv_w = st[i]["sv_w"]
                def epi_k(ep, bk, pev, lo, w):
                    prog.need("vector", t_mer((("pe", pev),), (("s", sv_w),),
                                              tb.guard))
                    vv = prog.inc("vector", lambda eng, bk=bk, o=tb.rg(lo, w),
                                  i_=ewk.rg(lo, w), w=w:
                                  eng.tensor_tensor(out=o,
                                                    in0=pb[bk][:, 0:w],
                                                    in1=i_,
                                                    op=ALU.subtract), "v", 1)
                    return (("v", vv),)
                proj(1 * D * DT, xn, N, DT, DT, epi_k, banks=(0, 1, 2, 3, 6, 7))
                vv_k = prog.cnt["v"]
                prog.need("scalar", (("v", vv_k),))
                sv = prog.inc("scalar", lambda e: e.activation(
                    out=ewk.all(), in_=tb.all(), func=AF.Exp), "s", 1)
                ewk.ready = (("s", sv),)
                tb.guard = ewk.ready

            def back_pv(i):
                xn = xn1[i % 3]
                def epi_v(ep, bk, pev, lo, w):
                    prog.need("vector", t_mer((("pe", pev),), ewk.ready,
                                              u.guard))
                    vv = prog.inc("vector", lambda eng, bk=bk, o=u.rg(lo, w),
                                  i_=ewk.rg(lo, w), w=w:
                                  eng.tensor_tensor(out=o, in0=i_,
                                                    in1=pb[bk][:, 0:w],
                                                    op=ALU.mult), "v", 1)
                    return (("v", vv),)
                proj(2 * D * DT, xn, N, DT, DT, epi_v, banks=(0, 1, 2, 3, 6, 7))
                u.ready = (("v", prog.cnt["v"]),)
                xn.guard = t_mer(xn.guard, (("pe", prog.cnt["pe"]),))

            def back_scan(i):
                t = st.pop(i)
                b, j, col0 = t["b"], t["j"], t["col0"]
                prog.need("vector", t_mer(u.ready, ewk.ready, pre_ld,
                                          numb.guard, denb.guard,
                                          (("s", pre_s_etd),)))
                if j > 0:
                    prog.wait("vector", "v", carry_edge[0])
                for d in range(DT):
                    dec_d = M("etd", d, 1).broadcast_to([128, N])
                    init_n = 0.0 if j == 0 else M("carN", d * B + b, 1)
                    init_d = 0.0 if j == 0 else M("carD", d * B + b, 1)
                    prog.emit("vector",
                              lambda e, d=d, dd=dec_d, init=init_n:
                              e.tensor_tensor_scan(
                                  out=numb.sl(d), data0=dd, data1=u.sl(d),
                                  initial=init, op0=ALU.mult, op1=ALU.add))
                    prog.emit("vector",
                              lambda e, d=d, dd=dec_d, init=init_d:
                              e.tensor_tensor_scan(
                                  out=denb.sl(d), data0=dd, data1=ewk.sl(d),
                                  initial=init, op0=ALU.mult, op1=ALU.add))
                carN3 = M("carN", 0, DT * B).rearrange(
                    "p (d b) -> p d b", d=DT)[:, :, b:b + 1]
                carD3 = M("carD", 0, DT * B).rearrange(
                    "p (d b) -> p d b", d=DT)[:, :, b:b + 1]
                prog.emit("vector", lambda e, o=carN3: e.tensor_copy(
                    out=o, in_=numb.v3()[:, :, N - 1:N]))
                vv = prog.inc("vector", lambda e, o=carD3: e.tensor_copy(
                    out=o, in_=denb.v3()[:, :, N - 1:N]), "v", 1)
                carry_edge[0] = vv
                numb.ready = denb.ready = (("v", vv),)
                u.guard = (("v", vv),)
                ewk.guard = (("v", vv),)
                prog.need("gpsimd", numb.ready)
                stv = prog.dma("gpsimd", "st",
                               spv("sp_num")[:, :, col0:col0 + N], numb.v3())
                numb.guard = stv
                stv = prog.dma("gpsimd", "st",
                               spv("sp_den")[:, :, col0:col0 + N], denb.v3())
                denb.guard = stv

            for i in range(C.NTT + 2):
                if i < C.NTT:
                    front_p1(i)
                if i >= 2:
                    back_pw(i - 2)
                if i < C.NTT:
                    front_p2(i)
                if i >= 2:
                    back_pk(i - 2)
                if i < C.NTT:
                    front_p3(i)
                if i >= 2:
                    back_pv(i - 2)
                    back_scan(i - 2)

        pass_a()
        def coll_sc():
            prog.need("sync", (("v", carry_edge[0]),))
            spv_sc = ap["SC_loc"].rearrange("p (t c b) -> p t c b", t=DT, c=2)
            stv1 = prog.dma("sync", "hst", spv_sc[:, :, 0, :],
                            M("carN", 0, DT * B).rearrange("p (t b) -> p t b", t=DT))
            stv2 = prog.dma("sync", "hst", spv_sc[:, :, 1, :],
                            M("carD", 0, DT * B).rearrange("p (t b) -> p t b", t=DT))
            prog.need("gpsimd", t_mer(stv1, stv2))
            ccv = prog.inc("gpsimd", lambda e: e.collective_compute(
                "AllGather", ALU.bypass, replica_groups=[list(range(NCORES))],
                ins=[ap["SC_loc"]], outs=[ap["SC_all"]]), "cc", 1)
            prog.wait("sync", "cc", ccv)
            lv_sg = prog.dma("sync", "ld", M("Sg", 0, 8 * 64),
                             ap["SC_all"].rearrange("(j p) c -> p j c", p=128))
            prog.need("vector", t_mer(pre_ld))
            for jj in range(8):
                bld = lambda e, jj=jj: e.tensor_scalar(
                    out=M("coef", jj * DT, DT), in0=M("td", 0, DT),
                    scalar1=M("hexpo", jj, 1), scalar2=None, op0=ALU.mult)
                if jj == 7:
                    vv = prog.inc("vector", bld, "v", 1)
                else:
                    prog.emit("vector", bld)
            prog.wait("scalar", "v", vv)
            sv = prog.inc("scalar", lambda e: e.activation(
                out=M("coef", 0, 8 * DT), in_=M("coef", 0, 8 * DT), func=AF.Exp),
                "s", 1)
            prog.wait("vector", "s", sv)
            for jj in range(8):
                bld = lambda e, jj=jj: e.tensor_scalar(
                    out=M("coef", jj * DT, DT), in0=M("coef", jj * DT, DT),
                    scalar1=M("hmask", jj, 1), scalar2=None, op0=ALU.mult)
                if jj == 7:
                    vv = prog.inc("vector", bld, "v", 1)
                else:
                    prog.emit("vector", bld)
            prog.wait("vector", "v", vv)
            prog.need("vector", lv_sg)
            last = None
            for jj in range(8):
                for t in range(DT):
                    for cn, tens in (("gcN", 0), ("gcD", 1)):
                        src = M("Sg", jj * 64 + t * 8 + tens * 4, B)
                        dst = M(cn, t * B, B)
                        if jj == 0:
                            bld = lambda e, src=src, dst=dst, t=t: e.tensor_scalar(
                                out=dst, in0=src, scalar1=M("coef", t, 1),
                                scalar2=None, op0=ALU.mult)
                        else:
                            bld = (lambda e, src=src, dst=dst, t=t, jj=jj:
                                   e.scalar_tensor_tensor(
                                       out=dst, in0=src,
                                       scalar=M("coef", jj * DT + t, 1),
                                       in1=dst, op0=ALU.mult, op1=ALU.add))
                        prog.emit("vector", bld)
            gcar_edge = prog.cnt["v"]
            return gcar_edge
        gcar_edge = coll_sc()

        prog.barrier()
        alloc_f.reset()
        alloc_b.reset()

        def pass_b():
            _wt.append(load_w("WrT", 0 * D * DT, D, D))
            _wt.append(load_w("WoT", 1 * D * DT, D, D))
            _wt.append(load_w("WrecT", 2 * D * DT, D, D))
            wready[0] = wdone()

            numb = [alloc_f.take(DT, N), alloc_f.take(DT, N)]
            denb = alloc_f.take(DT, N)
            scl = alloc_f.take(DT, N)
            rden = alloc_f.take(DT, N)
            rscr = scl
            h0 = alloc_f.take(DT, N)
            r1 = alloc_f.take(DT, N)
            yb = alloc_f.take(DT, N)
            th = alloc_f.take(DT, N)
            mu_sb = alloc_f.take(1, N)
            rs_sb = alloc_f.take(1, N)
            xn1 = alloc_b.take(DT, N)
            rw = alloc_b.take(DT, N)
            ybf = alloc_b.take(DT, N)
            sqb = alloc_b.take(DT, N)
            xn2 = alloc_b.take(DT, N)
            wkvb = [alloc_b.take(DT, N), alloc_b.take(DT, N)]
            wsp = {}
            g2b = M("g2", 0, DT).rearrange("p t -> p t ()").broadcast_to(
                [128, DT, N])
            b2b = M("b2", 0, DT).rearrange("p t -> p t ()").broadcast_to(
                [128, DT, N])

            def front(i):
                b, j = divmod(i, C.TPB)
                col0 = b * C.SLOC + j * C.TCA
                s0 = j * C.TCA
                nb = numb[i % 2]
                prog.need("sync", nb.guard)
                lvn = prog.dma("sync", "ld", nb.v3(),
                               spv("sp_num")[:, :, col0:col0 + N])
                nb.ready = lvn
                prog.need("sync", denb.guard)
                lvd = prog.dma("sync", "ld", denb.v3(),
                               spv("sp_den")[:, :, col0:col0 + N])
                denb.ready = lvd
                prog.need("scalar", t_mer(pre_g, pre_ld, scl.guard))
                for d in range(DT):
                    bld = lambda e, d=d, ia=M("iota", s0, N): e.activation(
                        out=scl.sl(d), in_=ia, func=AF.Exp,
                        scale=M("td", d, 1))
                    if d == DT - 1:
                        sv = prog.inc("scalar", bld, "s", 1)
                    else:
                        prog.emit("scalar", bld)
                scl.ready = (("s", sv),)
                gcnb = M("gcN", 0, DT * B).rearrange(
                    "p (d b) -> p d b", d=DT)[:, :, b:b + 1].broadcast_to(
                    [128, DT, N])
                gcdb = M("gcD", 0, DT * B).rearrange(
                    "p (d b) -> p d b", d=DT)[:, :, b:b + 1].broadcast_to(
                    [128, DT, N])
                prog.need("vector", t_mer(nb.ready, denb.ready, scl.ready,
                                          (("v", gcar_edge),), rden.guard,
                                          rscr.guard))
                prog.emit("vector", lambda e, o=rden.v3(), i=scl.v3(), g=gcnb:
                          e.tensor_tensor(out=o, in0=i, in1=g, op=ALU.mult))
                prog.emit("vector", lambda e, o=nb.v3(), i=rden.v3():
                          e.tensor_add(out=o, in0=o, in1=i))
                prog.emit("vector", lambda e, o=rscr.v3(), i=scl.v3(), g=gcdb:
                          e.tensor_tensor(out=o, in0=i, in1=g, op=ALU.mult))
                prog.emit("vector", lambda e, o=denb.v3(), i=rscr.v3():
                          e.tensor_add(out=o, in0=o, in1=i))
                vv = prog.inc("vector", lambda e, o=denb.all():
                              e.tensor_scalar(out=o, in0=o, scalar1=WKV_EPS,
                                              scalar2=None, op0=ALU.add),
                              "v", 1)
                scl.guard = (("v", vv),)
                prog.wait("scalar", "v", vv)
                prog.need("scalar", rscr.guard)
                prog.emit("scalar", lambda e, o=rscr.all(), i=denb.all():
                          e.activation(out=o, in_=i, func=AF.Ln))
                sv = prog.inc("scalar", lambda e, o=rden.all(), i=rscr.all():
                              e.activation(out=o, in_=i, func=AF.Exp,
                                           scale=-1.0), "s", 1)
                rden.ready = (("s", sv),)
                prog.need("vector", t_mer(rden.ready,))
                gv = prog.inc("vector", lambda e, o=nb.all(), r=rden.all():
                              e.tensor_mul(out=o, in0=o, in1=r), "v", 1)
                nb.ready = (("v", gv),)
                denb.guard = nb.ready
                rden.guard = nb.ready
                rscr.guard = nb.ready
                prog.need("gpsimd", nb.ready)
                stw = prog.dma("gpsimd", "st",
                               spv("sp_wkv")[:, :, col0:col0 + N], nb.v3())
                nb.guard = stw
                wsp[i] = stw

            def back(i):
                b, j = divmod(i, C.TPB)
                col0 = b * C.SLOC + j * C.TCA
                wk = wkvb[i % 2]
                prog.need("gpsimd", t_mer(wk.guard, wsp.pop(i)))
                lvw = prog.dma("gpsimd", "gld", wk.v3(),
                               spv("sp_wkv")[:, :, col0:col0 + N])
                wk.ready = lvw
                prog.need("sync", xn1.guard)
                lvx = prog.dma("sync", "ld", xn1.v3(),
                               spv("sp_xn1")[:, :, col0:col0 + N])
                xn1.ready = lvx
                prog.need("sync", h0.guard)
                lvh = prog.dma("sync", "ld", h0.v3(),
                               spv("sp_h0")[:, :, col0:col0 + N])
                h0.ready = lvh

                def epi_r(ep, bk, pev, lo, w):
                    prog.need("scalar", t_mer((("pe", pev),), r1.guard))
                    sv = prog.inc("scalar", lambda eng, bk=bk, o=r1.rg(lo, w),
                                  w=w: eng.activation(out=o,
                                                      in_=pb[bk][:, 0:w],
                                                      func=AF.Sigmoid),
                                  "s", 1)
                    return (("s", sv),)
                proj(0 * D * DT, xn1, N, DT, DT, epi_r, banks=(0, 1, 2, 3, 6, 7))
                r1.ready = (("s", prog.cnt["s"]),)

                prog.need("gpsimd", t_mer(r1.ready, wk.ready, rw.guard))
                gv = prog.inc("gpsimd", lambda e, o=rw.all(), i=wk.all(),
                              r=r1.all(): e.tensor_tensor(
                                  out=o, in0=i, in1=r, op=ALU.mult), "g", 1)
                rw.ready = (("g", gv),)
                wk.guard = rw.ready
                r1.guard = rw.ready

                def epi_o(ep, bk, pev, lo, w):
                    prog.need("vector", t_mer((("pe", pev),), h0.ready,
                                              yb.guard))
                    vv = prog.inc("vector", lambda eng, bk=bk, o=yb.rg(lo, w),
                                  i_=h0.rg(lo, w), w=w:
                                  eng.tensor_tensor(out=o,
                                                    in0=pb[bk][:, 0:w],
                                                    in1=i_, op=ALU.add),
                                  "v", 1)
                    return (("v", vv),)
                proj(1 * D * DT, rw, N, DT, DT, epi_o, banks=(0, 1, 2, 3, 6, 7))
                yb.ready = (("v", prog.cnt["v"]),)
                h0.guard = yb.ready
                rw.guard = (("pe", prog.cnt["pe"]),)
                prog.need("gpsimd", yb.ready)
                st_y = prog.dma("gpsimd", "st",
                                spv("sp_y")[:, :, col0:col0 + N], yb.v3())

                prog.need("vector", t_mer(yb.ready, ybf.guard))
                gv = prog.inc("vector", lambda e, o=ybf.all(), i=yb.all():
                              e.tensor_copy(out=o, in_=i), "v", 1)
                ybf.ready = (("v", gv),)
                ln_g(yb, ybf, sqb, th, N, mu_sb, rs_sb)
                yb.guard = t_mer(th.ready, st_y)
                prog.need("vector", t_mer(th.ready, xn2.guard, pre_ld))
                th3 = th.v3()
                prog.emit("vector", lambda e, o=th3, g=g2b: e.tensor_tensor(
                    out=o, in0=o, in1=g, op=ALU.mult))
                vv = prog.inc("vector", lambda e, o=xn2.v3(), i=th3, bb_=b2b:
                              e.tensor_tensor(out=o, in0=i, in1=bb_,
                                              op=ALU.add), "v", 1)
                xn2.ready = (("v", vv),)
                th.guard = xn2.ready
                prog.need("gpsimd", xn2.ready)
                st_x = prog.dma("gpsimd", "st",
                                spv("sp_xn2")[:, :, col0:col0 + N], xn2.v3())

                def epi_rc(ep, bk, pev, lo, w):
                    prog.need("scalar", t_mer((("pe", pev),), r1.guard))
                    sv = prog.inc("scalar", lambda eng, bk=bk, o=r1.rg(lo, w),
                                  w=w: eng.activation(out=o,
                                                      in_=pb[bk][:, 0:w],
                                                      func=AF.Sigmoid),
                                  "s", 1)
                    return (("s", sv),)
                proj(2 * D * DT, xn2, N, DT, DT, epi_rc, banks=(0, 1, 2, 3, 6, 7))
                r1.ready = (("s", prog.cnt["s"]),)
                prog.need("gpsimd", r1.ready)
                st_r = prog.dma("gpsimd", "st",
                                spv("sp_rcm")[:, :, col0:col0 + N], r1.v3())
                r1.guard = st_r
                xn2.guard = t_mer(st_x, (("pe", prog.cnt["pe"]),))
                xn1.guard = (("pe", prog.cnt["pe"]),)

            for i in range(C.NTT):
                front(i)
            for i in range(C.NTT):
                back(i)

        pass_b()
        prog.barrier()
        alloc_f.reset()
        alloc_b.reset()

        def pass_ckey():
            _wt.append(load_w("WkeyT", 0, D, F))
            wready[0] = wdone()
            xn2 = [alloc_b.take(DT, N), alloc_b.take(DT, N)]
            _kc = alloc_b.take(FT, N)
            kcm2 = [_kc, _kc]
            tbs = alloc_f.take(1, 2 * N)

            def front(i):
                b, j = divmod(i, C.TPB)
                col0 = b * C.SLOC + j * C.TCA
                xn = xn2[i % 2]
                prog.need("sync", xn.guard)
                lv = prog.dma("sync", "ld", xn.v3(),
                              spv("sp_xn2")[:, :, col0:col0 + N])
                xn.ready = lv

            def back(i):
                b, j = divmod(i, C.TPB)
                col0 = b * C.SLOC + j * C.TCA
                xn = xn2[i % 2]
                kc = kcm2[i % 2]

                def epi_key(ep, bk, pev, lo, w, kc=kc):
                    prog.need("scalar", t_mer((("pe", pev),), tbs.guard,
                                              kc.guard))
                    prog.emit("scalar", lambda eng, bk=bk, o=tbs.rg(0, w),
                              w=w: eng.activation(out=o, in_=pb[bk][:, 0:w],
                                                  func=AF.Relu))
                    sv = prog.inc("scalar", lambda eng, kc=kc, o=kc.rg(lo, w),
                                  i_=tbs.rg(0, w): eng.activation(
                                      out=o, in_=i_, func=AF.Square),
                                  "s", 1)
                    tbs.guard = (("s", sv),)
                    return (("s", sv),)
                proj(0, xn, N, FT, DT, epi_key, banks=(0, 1, 2, 3, 6, 7),
                     we_stride=F)
                kc.ready = (("s", prog.cnt["s"]),)
                xn.guard = (("pe", prog.cnt["pe"]),)
                prog.need("gpsimd", kc.ready)
                stv = prog.dma("gpsimd", "st",
                               spv("sp_kcm")[:, :, col0:col0 + N], kc.v3())
                kc.guard = stv

            for i in range(C.NTT + 1):
                if i < C.NTT:
                    front(i)
                if i >= 1:
                    back(i - 1)

        pass_ckey()
        prog.barrier()
        alloc_f.reset()
        alloc_b.reset()

        def pass_cval():
            _wt.append(load_w("WvalT", 0, F, D))
            wready[0] = wdone()
            kcm = [None, None]
            kcm[0] = alloc_b.take(FT, N)
            kcm[1] = kcm[0]
            x1bf = alloc_b.take(DT, N)
            sqb = alloc_b.take(DT, N)
            x0f = [alloc_f.take(DT, N), alloc_f.take(DT, N)]
            yf = [alloc_f.take(DT, N), alloc_f.take(DT, N)]
            rcm = alloc_f.take(DT, N)
            th = alloc_f.take(DT, N)
            x1 = alloc_f.take(DT, N)
            hf1 = alloc_f.take(DT, N)
            mu_sb = alloc_f.take(1, N)
            rs_sb = alloc_f.take(1, N)
            b1pub = 0

            def front(i):
                b, j = divmod(i, C.TPB)
                col0 = b * C.SLOC + j * C.TCA
                prog.need("sync", x0f[i % 2].guard)
                lv = prog.dma("sync", "ld", x0f[i % 2].v3(),
                              spv("x0")[:, :, col0:col0 + N])
                x0f[i % 2].ready = lv
                prog.need("sync", yf[i % 2].guard)
                lv = prog.dma("sync", "ld", yf[i % 2].v3(),
                              spv("sp_y")[:, :, col0:col0 + N])
                yf[i % 2].ready = lv

            def back(i):
                nonlocal b1pub
                b, j = divmod(i, C.TPB)
                col0 = b * C.SLOC + j * C.TCA
                kc = kcm[i % 2]
                prog.need("sync", kc.guard)
                lvk = prog.dma("sync", "ld", kc.v3(),
                               spv("sp_kcm")[:, :, col0:col0 + N])
                kc.ready = lvk
                prog.need("sync", rcm.guard)
                lvr = prog.dma("sync", "ld", rcm.v3(),
                               spv("sp_rcm")[:, :, col0:col0 + N])
                rcm.ready = lvr

                def epi_val(ep, bk, pev, lo, w):
                    prog.need("vector", t_mer((("pe", pev),), rcm.ready,
                                              th.guard))
                    vv = prog.inc("vector", lambda eng, bk=bk, o=th.rg(lo, w),
                                  i_=rcm.rg(lo, w), w=w:
                                  eng.tensor_tensor(out=o,
                                                    in0=pb[bk][:, 0:w],
                                                    in1=i_, op=ALU.mult),
                                  "v", 1)
                    return (("v", vv),)
                proj(0, kc, N, DT, FT, epi_val, banks=(0, 1, 2, 3, 6, 7),
                     we_stride=D)
                th.ready = (("v", prog.cnt["v"]),)
                kc.guard = (("pe", prog.cnt["pe"]),)
                rcm.guard = th.ready
                prog.need("vector", t_mer(th.ready, yf[i % 2].ready,
                                          x0f[i % 2].ready, x1.guard))
                prog.emit("vector", lambda e, o=x1.all(), a=th.all(),
                          bb_=yf[i % 2].all(): e.tensor_add(out=o, in0=a,
                                                            in1=bb_))
                gv = prog.inc("vector", lambda e, o=x1.all(),
                              a=x0f[i % 2].all(): e.tensor_add(
                                  out=o, in0=o, in1=a), "v", 1)
                x1.ready = (("v", gv),)
                th.guard = yf[i % 2].guard = x0f[i % 2].guard = x1.ready
                prog.need("gpsimd", x1.ready)
                st_x1 = prog.dma("gpsimd", "st",
                                 spv("sp_x1")[:, :, col0:col0 + N], x1.v3())
                prog.need("vector", t_mer(x1.ready, x1bf.guard))
                gv = prog.inc("vector", lambda e, o=x1bf.all(), i_=x1.all():
                              e.tensor_copy(out=o, in_=i_), "v", 1)
                x1bf.ready = (("v", gv),)
                ln_g(x1, x1bf, sqb, hf1, N, mu_sb, rs_sb)
                x1.guard = t_mer(hf1.ready, st_x1)
                if j == C.TPB - 1:
                    prog.need("vector", hf1.ready)
                    b3 = M("bnd1", 0, 32).rearrange(
                        "p (d b) -> p d b", d=DT)[:, :, b:b + 1]
                    b1pub = prog.inc("vector", lambda e, o=b3: e.tensor_copy(
                        out=o, in_=hf1.v3()[:, :, N - 1:N]), "v", 1)
                prog.need("gpsimd", hf1.ready)
                stv = prog.dma("gpsimd", "st",
                               spv("sp_hf1")[:, :, col0:col0 + N], hf1.v3())
                hf1.guard = stv

            for i in range(C.NTT + 1):
                if i < C.NTT:
                    front(i)
                if i >= 1:
                    back(i - 1)
            return b1pub

        b1pub = pass_cval()
        def boundary_collective(bnd_name, loc, all_, gname, pub_edge):
            prog.need("sync", (("v", pub_edge),))
            stv = prog.dma("sync", "hst", ap[loc], M(bnd_name, 0, 32))
            prog.need("gpsimd", stv)
            ccv = prog.inc("gpsimd", lambda e: e.collective_compute(
                "AllGather", ALU.bypass, replica_groups=[list(range(NCORES))],
                ins=[ap[loc]], outs=[ap[all_]]), "cc", 1)
            prog.wait("sync", "cc", ccv)
            lv = prog.dma("sync", "ld", M(gname, 0, 8 * 32),
                          ap[all_].rearrange("(j p) c -> p j c", p=128))
            prog.need("vector", t_mer(pre_ld, lv))
            last = None
            for jj in range(8):
                src = M(gname, jj * 32, 32)
                if jj == 0:
                    bld = lambda e, src=src, bn=bnd_name: e.tensor_scalar(
                        out=M(bn, 0, 32), in0=src, scalar1=M("bmask", 0, 1),
                        scalar2=None, op0=ALU.mult)
                else:
                    bld = (lambda e, src=src, jj=jj, bn=bnd_name:
                           e.scalar_tensor_tensor(
                               out=M(bn, 0, 32), in0=src,
                               scalar=M("bmask", jj, 1), in1=M(bn, 0, 32),
                               op0=ALU.mult, op1=ALU.add))
                prog.emit("vector", bld)
            return prog.cnt["v"]

        bnd1_edge = boundary_collective("bnd1", "B1_loc", "B1_all", "B1g",
                                        b1pub)
        prog.barrier()
        alloc_f.reset()
        alloc_b.reset()

        def ffn_pass(src_h, src_x, dst_x, dst_h, bnd_name, bnd_edge, is_final):
            NP = N + 1
            hf = [alloc_f.take(DT, NP), alloc_f.take(DT, NP)]
            tsub = alloc_f.take(DT, N)
            xf = alloc_f.take(DT, N)
            rf = alloc_f.take(DT, N)
            of = alloc_f.take(DT, N)
            x2 = alloc_f.take(DT, N)
            hb = None if is_final else alloc_f.take(DT, N)
            fsc = alloc_f.take(DT, N)
            mu_sb = alloc_f.take(1, N)
            rs_sb = alloc_f.take(1, N)
            xr = [alloc_b.take(DT, N), alloc_b.take(DT, N)]
            xk = [alloc_b.take(DT, N), alloc_b.take(DT, N)]
            kf = alloc_b.take(DT, N)
            hpb = None if is_final else alloc_b.take(DT, N)
            tbs2 = alloc_f.take(1, 2 * N)
            murb = M("mur", 0, DT).rearrange("p t -> p t ()").broadcast_to(
                [128, DT, N])
            mukb = M("muk", 0, DT).rearrange("p t -> p t ()").broadcast_to(
                [128, DT, N])

            def front(i):
                b, j = divmod(i, C.TPB)
                col0 = b * C.SLOC + j * C.TCA
                hfi = hf[i % 2]
                prog.need("sync", hfi.guard)
                if j > 0:
                    lvh = prog.dma("sync", "ld", hfi.v3(),
                                   spv(src_h)[:, :, col0 - 1:col0 + N])
                    hfi.ready = lvh
                else:
                    lvh = prog.dma("sync", "ld",
                                   hfi.v3()[:, :, 1:NP],
                                   spv(src_h)[:, :, col0:col0 + N])
                    prog.need("vector", t_mer(lvh, (("v", bnd_edge),),
                                              hfi.guard))
                    bsl = M(bnd_name, 0, 32).rearrange(
                        "p (d b) -> p d b", d=DT)[:, :, b:b + 1]
                    vvb = prog.inc("vector", lambda e, o=hfi.v3()[:, :, 0:1],
                                   i_=bsl: e.tensor_copy(out=o, in_=i_),
                                   "v", 1)
                    hfi.ready = t_mer((("v", vvb),), lvh)
                def hsl3(off):
                    return hfi.all().rearrange(
                        "p (t n) -> p t n", t=DT)[:, :, off:off + N]
                prog.need("vector", t_mer(hfi.ready, tsub.guard, pre_ld,
                                          xr[i % 2].guard, xk[i % 2].guard))
                prog.emit("vector", lambda e, o=tsub.v3(), a=hsl3(1),
                          bb_=hsl3(0): e.tensor_tensor(out=o, in0=a, in1=bb_,
                                                       op=ALU.subtract))
                prog.emit("vector", lambda e, o=fsc.v3(), a=tsub.v3(), m=murb:
                          e.tensor_tensor(out=o, in0=a, in1=m, op=ALU.mult))
                prog.emit("vector", lambda e, o=xr[i % 2].v3(), a=fsc.v3(),
                          bb_=hsl3(0): e.tensor_tensor(out=o, in0=a, in1=bb_,
                                                       op=ALU.add))
                prog.emit("vector", lambda e, o=fsc.v3(), a=tsub.v3(), m=mukb:
                          e.tensor_tensor(out=o, in0=a, in1=m, op=ALU.mult))
                vv = prog.inc("vector", lambda e, o=xk[i % 2].v3(),
                              a=fsc.v3(), bb_=hsl3(0): e.tensor_tensor(
                                  out=o, in0=a, in1=bb_, op=ALU.add), "v", 1)
                xr[i % 2].ready = xk[i % 2].ready = (("v", vv),)
                tsub.guard = fsc.guard = (("v", vv),)
                hfi.guard = (("v", vv),)

            def back(i):
                b, j = divmod(i, C.TPB)
                col0 = b * C.SLOC + j * C.TCA
                prog.need("sync", xf.guard)
                lvx = prog.dma("sync", "ld", xf.v3(),
                               spv(src_x)[:, :, col0:col0 + N])
                xf.ready = lvx

                def epi_rf(ep, bk, pev, lo, w):
                    prog.need("scalar", t_mer((("pe", pev),), rf.guard))
                    sv = prog.inc("scalar",
                                  lambda eng, bk=bk, o=rf.rg(lo, w), w=w:
                                  eng.activation(
                                      out=o, in_=pb[bk][:, 0:w],
                                      func=AF.Sigmoid), "s", 1)
                    return (("s", sv),)
                proj(0 * D * DT, xr[i % 2], N, DT, DT, epi_rf,
                     banks=(0, 1, 2, 3, 6, 7))
                rf.ready = (("s", prog.cnt["s"]),)
                xr[i % 2].guard = (("pe", prog.cnt["pe"]),)

                def epi_kf(ep, bk, pev, lo, w):
                    prog.need("vector", t_mer((("pe", pev),), tbs2.guard))
                    vv = prog.inc("vector", lambda eng, bk=bk, o=tbs2.rg(0, w),
                                  w=w: eng.tensor_scalar_max(
                                      out=o, in0=pb[bk][:, 0:w],
                                      scalar1=0.0), "v", 1)
                    prog.need("scalar", t_mer((("v", vv),), kf.guard))
                    sv = prog.inc("scalar", lambda eng, o=kf.rg(lo, w),
                                  i_=tbs2.rg(0, w): eng.activation(
                                      out=o, in_=i_, func=AF.Square), "s", 1)
                    tbs2.guard = (("s", sv),)
                    return (("s", sv),)
                proj(1 * D * DT, xk[i % 2], N, DT, DT, epi_kf,
                     banks=(0, 1, 2, 3, 6, 7))
                kf.ready = (("s", prog.cnt["s"]),)
                xk[i % 2].guard = (("pe", prog.cnt["pe"]),)

                def epi_vf(ep, bk, pev, lo, w):
                    prog.need("vector", t_mer((("pe", pev),), rf.ready,
                                              of.guard))
                    vv = prog.inc("vector",
                                  lambda eng, bk=bk, o=of.rg(lo, w),
                                  i_=rf.rg(lo, w), w=w: eng.tensor_tensor(
                                      out=o, in0=pb[bk][:, 0:w],
                                      in1=i_, op=ALU.mult), "v", 1)
                    return (("v", vv),)
                proj(2 * D * DT, kf, N, DT, DT, epi_vf, banks=(0, 1, 2, 3, 6, 7))
                of.ready = (("v", prog.cnt["v"]),)
                rf.guard = of.ready
                kf.guard = (("pe", prog.cnt["pe"]),)

                prog.need("vector", t_mer(of.ready, xf.ready, x2.guard))
                gv = prog.inc("vector", lambda e, o=x2.all(), a=of.all(),
                              bb_=xf.all(): e.tensor_add(out=o, in0=a,
                                                         in1=bb_), "v", 1)
                x2.ready = (("v", gv),)
                of.guard = xf.guard = x2.ready
                prog.need("gpsimd", x2.ready)
                stv = prog.dma("gpsimd", "st", spv(dst_x)[:, :, col0:col0 + N],
                               x2.v3())
                if is_final:
                    x2.guard = stv
                    return

                x2bf = xr[i % 2]
                sqb2 = xk[i % 2]
                prog.need("vector", t_mer(x2.ready, x2bf.guard))
                gv = prog.inc("vector", lambda e, o=x2bf.all(), i_=x2.all():
                              e.tensor_copy(out=o, in_=i_), "v", 1)
                x2bf.ready = (("v", gv),)
                ln_g(x2, x2bf, sqb2, hb, N, mu_sb, rs_sb)
                x2.guard = t_mer(hb.ready, stv)
                prog.need("gpsimd", t_mer(hb.ready, hpb.guard))
                peb = M("pe", 0, DT * B).rearrange(
                    "p (t b) -> p t b", t=DT)[:, :, b:b + 1].broadcast_to(
                    [128, DT, N])
                gv = prog.inc("gpsimd", lambda e, o=hpb.v3(), a=hb.v3(),
                              p_=peb: e.tensor_tensor(out=o, in0=a, in1=p_,
                                                      op=ALU.add), "g", 1)
                hpb.ready = (("g", gv),)
                hb.guard = hpb.ready
                prog.need("gpsimd", hpb.ready)
                stv = prog.dma("gpsimd", "st", spv(dst_h)[:, :, col0:col0 + N],
                               hpb.v3())
                hpb.guard = stv

            for i in range(C.NTT + 1):
                if i < C.NTT:
                    front(i)
                if i >= 1:
                    back(i - 1)

        _wt.append(load_w("fWrT", 0 * D * DT, D, D))
        _wt.append(load_w("fWkT", 1 * D * DT, D, D))
        _wt.append(load_w("fWvT", 2 * D * DT, D, D))
        wready[0] = wdone()
        ffn_pass("sp_hf1", "sp_x1", "sp_x2", "sp_hp", "bnd1", bnd1_edge, False)
        prog.barrier()
        alloc_f.reset()
        alloc_b.reset()

        def pass_e():
            _wt.append(load_w("bWqT", 0 * D * DT, D, D))
            _wt.append(load_w("bWkT", 1 * D * DT, D, D))
            _wt.append(load_w("bWvT", 2 * D * DT, D, D))
            _wt.append(load_w("bWoT", 3 * D * DT, D, D))
            wready[0] = wdone()

            SE = C.SE
            NE = 4 * SE
            N1 = C.TCA
            NT1 = C.TOK // N1
            qb1 = alloc_f.take(DT, N1)
            esc = alloc_f.take(DT, N1)
            hp1 = [alloc_b.take(DT, N1), alloc_b.take(DT, N1)]
            oub = [alloc_b.take(DT, N1), alloc_b.take(DT, N1)]
            e1_sp = []

            for t in range(NT1):
                col0 = t * N1
                hpb = hp1[t % 2]
                prog.need("sync", hpb.guard)
                hpb.ready = prog.dma("sync", "ld", hpb.v3(),
                                     spv("sp_hp")[:, :, col0:col0 + N1])
                for wi, wof, bias1, dsp in ((0, 0 * D * DT, "bq1", "sp_pq"),
                                            (1, 1 * D * DT, "bk1", "sp_pk"),
                                            (2, 2 * D * DT, None, "sp_pv")):
                    ob = oub[(t * 3 + wi) % 2]
                    if bias1 is None:
                        def epi(ep, bk, pev, lo, w, ob=ob):
                            prog.need("vector", t_mer((("pe", pev),), ob.guard,
                                                      pre_ld))
                            vv = prog.inc("vector", lambda eng, bk=bk, ob=ob,
                                          w=w, lo=lo, s1=M("bv", ep, 1):
                                          eng.tensor_scalar(
                                              out=ob.rg(lo, w),
                                              in0=pb[bk][:, 0:w], scalar1=s1,
                                              scalar2=None, op0=ALU.add),
                                          "v", 1)
                            return (("v", vv),)
                        proj(wof, hpb, N1, DT, DT, epi, banks=(0, 1, 2, 3),
                             single=True)
                        ob.ready = (("v", prog.cnt["v"]),)
                    else:
                        def epi(ep, bk, pev, lo, w, b1=bias1):
                            prog.need("vector", t_mer((("pe", pev),),
                                                      qb1.guard, pre_v))
                            vv = prog.inc("vector", lambda eng, bk=bk, w=w,
                                          lo=lo, s1=M(b1, ep, 1):
                                          eng.tensor_scalar(
                                              out=qb1.rg(lo, w),
                                              in0=pb[bk][:, 0:w], scalar1=s1,
                                              scalar2=None, op0=ALU.add),
                                          "v", 1)
                            return (("v", vv),)
                        proj(wof, hpb, N1, DT, DT, epi, banks=(0, 1, 2, 3),
                             single=True)
                        prog.need("vector", t_mer((("v", prog.cnt["v"]),),
                                                  esc.guard))
                        vv = prog.inc("vector", lambda e: e.tensor_scalar(
                            out=esc.all(), in0=qb1.all(), scalar1=1.0,
                            scalar2=None, op0=ALU.min), "v", 1)
                        prog.wait("scalar", "v", vv)
                        sv = prog.inc("scalar", lambda e: e.activation(
                            out=esc.all(), in_=esc.all(), func=AF.Exp,
                            bias=-1.0), "s", 1)
                        prog.need("vector", t_mer((("s", sv),), ob.guard))
                        vv = prog.inc("vector", lambda e, ob=ob:
                                      e.tensor_tensor(out=ob.all(),
                                                      in0=qb1.all(),
                                                      in1=esc.all(),
                                                      op=ALU.max), "v", 1)
                        ob.ready = (("v", vv),)
                        qb1.guard = ob.ready
                        esc.guard = ob.ready
                    prog.need("gpsimd", ob.ready)
                    stv = prog.dma("gpsimd", "st",
                                   spv(dsp)[:, :, col0:col0 + N1], ob.v3())
                    ob.guard = stv
                    e1_sp.append(stv)
                hpb.guard = (("pe", prog.cnt["pe"]),)

            e1done = t_mer((("pe", prog.cnt["pe"]),),
                           (("v", prog.cnt["v"]),),
                           (("s", prog.cnt["s"]),), *e1_sp)

            alloc_f.reset()
            alloc_b.reset()
            pq2 = [Buf(WP, 0 * DT * NE, DT, NE), Buf(WP, 1 * DT * NE, DT, NE)]
            pk2 = [Buf(WP, 2 * DT * NE, DT, NE), Buf(WP, 3 * DT * NE, DT, NE)]
            pv2 = [Buf(WP, 4 * DT * NE, DT, NE), Buf(WP, 5 * DT * NE, DT, NE)]
            numa2 = [alloc_f.take(DT, NE), alloc_f.take(DT, NE)]
            tmpE = alloc_f.take(DT, NE)
            tm = alloc_f.take(DT, NE)
            x2f = alloc_f.take(DT, NE)
            d32 = alloc_f.take(1, 2 * SE)
            attb = alloc_b.take(DT, NE)
            zb = alloc_b.take(DT, SE)
            dbig = alloc_b.take(DT, B * SE)
            x3bf = alloc_b.take(DT, NE)
            for bf in (*pq2, *pk2, *pv2, *numa2, tmpE, tm, x2f, d32, attb, zb,
                       dbig, x3bf):
                bf.guard = e1done
            rowsg = [()]
            b3cell = [0]
            d32g = [e1done, e1done]
            d32r = [(), ()]

            def dbsl(h, bp):
                base = dbig.off + (h * B + bp) * SE
                return dbig.t[:, base:base + SE]

            def front_head(i):
                s0 = i * SE
                par = i % 2
                pq, pk, vf = pq2[par], pk2[par], pv2[par]
                for bf, nmsp in ((pq, "sp_pq"), (pk, "sp_pk"), (vf, "sp_pv")):
                    prog.need("sync", bf.guard)
                    srcv = spv(nmsp).rearrange("p t (b n) -> p t b n", b=B)
                    _t = []
                    for bb in range(B):
                        _t.append(prog.dma("sync", "ld",
                                           bf.v4(B)[:, :, bb, :],
                                           srcv[:, :, bb, s0:s0 + SE]))
                    bf.ready = t_mer(*_t)

                def pkb(bb):
                    return pk.v4(B)[:, :, bb, :]
                prog.need("vector", t_mer(pk.ready, zb.guard))
                prog.emit("vector", lambda e, o=zb.v3(), a=pkb(0), b_=pkb(1):
                          e.tensor_add(out=o, in0=a, in1=b_))
                prog.emit("vector", lambda e, o=zb.v3(), a=pkb(2):
                          e.tensor_add(out=o, in0=o, in1=a))
                vv = prog.inc("vector", lambda e, o=zb.v3(), a=pkb(3):
                              e.tensor_add(out=o, in0=o, in1=a), "v", 1)
                zb.ready = (("v", vv),)

                zbc = zb.all().rearrange("p (t o n) -> p t o n", t=DT,
                                         o=1).broadcast_to([128, DT, B, SE])
                prog.need("vector", t_mer(pq.ready, zb.ready, dbig.guard))
                vv = prog.inc("vector", lambda e,
                              o=dbig.all().rearrange("p (t b n) -> p t b n",
                                                     t=DT, b=B),
                              a=pq.v4(B), z=zbc:
                              e.tensor_tensor(out=o, in0=a, in1=z,
                                              op=ALU.mult), "v", 1)
                dbig.ready = (("v", vv),)
                prog.need("tensor", t_mer(dbig.ready, bankfree.get(7, ())))
                pev_den = 0
                for bb in range(B):
                    for h in range(DT):
                        row = h * 4 + bb
                        last_mm = (bb == B - 1 and h == DT - 1)
                        bld = lambda e, row=row, r_=dbsl(h, bb), lm=last_mm, \
                            fd=(bb == 0 and h == 0): e.matmul(
                                pb[7][0:32, 0:SE], OH2[:, 128 - row:160 - row],
                                r_, start=fd, stop=lm)
                        if last_mm:
                            pev_den = prog.inc("tensor", bld, "pe", 1)
                        else:
                            prog.emit("tensor", bld)
                dbig.guard = (("pe", pev_den),)
                prog.wait("vector", "pe", pev_den)
                prog.need("vector", d32g[par])
                prog.emit("vector", lambda e, o=pb[7][0:32, SE:2 * SE]:
                          e.tensor_scalar(out=o, in0=pb[7][0:32, 0:SE],
                                          scalar1=ATT_EPS, scalar2=None,
                                          op0=ALU.add))
                vv = prog.inc("vector", lambda e,
                              o=d32.t[0:32, d32.off + par * SE:
                                      d32.off + (par + 1) * SE]:
                              e.reciprocal(out=o, in_=pb[7][0:32, SE:2 * SE]),
                              "v", 1)
                d32r[par] = (("v", vv),)
                bankfree[7] = (("v", vv),)

            def front_tail(i):
                par = i % 2
                pq, pk, vf, nm = pq2[par], pk2[par], pv2[par], numa2[par]

                def pqb(bb):
                    return pq.v4(B)[:, :, bb, :]
                def vfb4(bb, hlo):
                    return vf.all().rearrange(
                        "p (t b n) -> p t b n", t=DT, b=B)[:, hlo:hlo + 4, bb, :]
                def nm4(bb, hlo):
                    return nm.all().rearrange(
                        "p (t b n) -> p t b n", t=DT, b=B)[:, hlo:hlo + 4, bb, :]
                def tm4(bb, hlo):
                    return tm.all().rearrange(
                        "p (t b n) -> p t b n", t=DT, b=B)[:, hlo:hlo + 4, bb, :]

                for bb in range(B):
                    pqc = pq.v4(B)[:, :, bb:bb + 1, :].broadcast_to(
                        [128, DT, B, SE])
                    prog.need("vector", t_mer(pq.ready, pk.ready, dbig.guard))
                    vv = prog.inc("vector", lambda e,
                                  o=dbig.all().rearrange(
                                      "p (t b n) -> p t b n", t=DT, b=B),
                                  a=pqc, b_=pk.v4(B):
                                  e.tensor_tensor(out=o, in0=a, in1=b_,
                                                  op=ALU.mult), "v", 1)
                    dbig.ready = (("v", vv),)
                    pevA = 0
                    for bp in range(B):
                        bkA = (2, 3) if bp % 2 == 0 else (4, 5)
                        prog.need("tensor", t_mer(dbig.ready,
                                                  bankfree.get(bkA[0], ()),
                                                  bankfree.get(bkA[1], ())))
                        for h in range(DT):
                            bnk = bkA[0] if h < 4 else bkA[1]
                            cof = (h % 4) * SE
                            bld = lambda e, bnk=bnk, cof=cof, r_=dbsl(h, bp): \
                                e.matmul(pb[bnk][:, cof:cof + SE], allones,
                                         r_, start=True, stop=True)
                            if h == DT - 1:
                                pevA = prog.inc("tensor", bld, "pe", 1)
                            else:
                                prog.emit("tensor", bld)
                        prog.need("vector", t_mer((("pe", pevA),), vf.ready,
                                                  nm.guard if bp == 0 else (),
                                                  tm.guard if bp > 0 else ()))
                        for half in range(2):
                            bnk = bkA[half]
                            ps4 = pb[bnk][:, 0:4 * SE].rearrange(
                                "p (h n) -> p h n", h=4)
                            if bp == 0:
                                bld = (lambda e, o=nm4(bb, half * 4),
                                       i=vfb4(0, half * 4), ps4=ps4:
                                       e.tensor_mul(out=o, in0=i, in1=ps4))
                            else:
                                bld = (lambda e, o=tm4(bb, half * 4),
                                       i=vfb4(bp, half * 4), ps4=ps4:
                                       e.tensor_mul(out=o, in0=i, in1=ps4))
                            if half == 1:
                                vv = prog.inc("vector", bld, "v", 1)
                            else:
                                prog.emit("vector", bld)
                        bankfree[bkA[0]] = (("v", vv),)
                        bankfree[bkA[1]] = (("v", vv),)
                        if bp > 0:
                            prog.wait("gpsimd", "v", vv)
                            gv = prog.inc("gpsimd", lambda e, o=nm4(bb, 0),
                                          i=tm4(bb, 0): e.tensor_add(
                                              out=o, in0=o, in1=i), "g", 1)
                            gv = prog.inc("gpsimd", lambda e, o=nm4(bb, 4),
                                          i=tm4(bb, 4): e.tensor_add(
                                              out=o, in0=o, in1=i), "g", 1)
                            tm.guard = (("g", gv),)
                    dbig.guard = (("pe", pevA),)
                nm.ready = (("g", prog.cnt["g"]), ("v", prog.cnt["v"]))
                pq.guard = (("v", prog.cnt["v"]),)
                pk.guard = (("v", prog.cnt["v"]),)
                vf.guard = (("v", prog.cnt["v"]),)

            def back(i):
                s0 = i * SE
                par = i % 2
                nm = numa2[par]
                def nm4(bb, hlo):
                    return nm.all().rearrange(
                        "p (t b n) -> p t b n", t=DT, b=B)[:, hlo:hlo + 4, bb, :]
                prog.need("sync", x2f.guard)
                srcv = spv("sp_x2").rearrange("p t (b n) -> p t b n", b=B)
                _t = []
                for bb in range(B):
                    _t.append(prog.dma("sync", "ld", x2f.v4(B)[:, :, bb, :],
                                       srcv[:, :, bb, s0:s0 + SE]))
                x2f.ready = t_mer(*_t)
                prog.need("gpsimd", t_mer(d32r[par], rowsg[0]))
                rden_edge = prog.dma(
                    "gpsimd", "gld",
                    ROWS[0:1, rws["rden"]:rws["rden"] + 32 * SE]
                    .rearrange("o (r n) -> o r n", r=32),
                    d32.t[0:32, d32.off + par * SE:d32.off + (par + 1) * SE])
                d32g[par] = rden_edge

                RD = ROWS[0:1, rws["rden"]:rws["rden"] + 32 * SE].rearrange(
                    "o (h x n) -> o h x n", h=8, x=4)
                pevR = 0
                for bb in range(B):
                    bkR = (6, 7) if bb % 2 == 0 else (0, 1)
                    prog.need("tensor", t_mer(rden_edge,
                                              bankfree.get(bkR[0], ()),
                                              bankfree.get(bkR[1], ())))
                    for half in range(2):
                        bnk = bkR[half]
                        rhs = RD[:, half * 4:half * 4 + 4, bb, :]
                        bld = lambda e, bnk=bnk, rhs=rhs: e.matmul(
                            pb[bnk][:, 0:4 * SE], R("onesf", 0, 128), rhs,
                            start=True, stop=True)
                        if half == 1:
                            pevR = prog.inc("tensor", bld, "pe", 1)
                        else:
                            prog.emit("tensor", bld)
                    prog.need("vector", t_mer((("pe", pevR),), nm.ready,
                                              attb.guard))
                    for half in range(2):
                        bnk = bkR[half]
                        ps4 = pb[bnk][:, 0:4 * SE].rearrange("p (h n) -> p h n",
                                                             h=4)
                        ab4 = attb.all().rearrange(
                            "p (t b n) -> p t b n",
                            t=DT, b=B)[:, half * 4:half * 4 + 4, bb, :]
                        bld = lambda e, o=ab4, i=nm4(bb, half * 4), ps4=ps4: \
                            e.tensor_mul(out=o, in0=i, in1=ps4)
                        if half == 1:
                            vv = prog.inc("vector", bld, "v", 1)
                        else:
                            prog.emit("vector", bld)
                    bankfree[bkR[0]] = (("v", vv),)
                    bankfree[bkR[1]] = (("v", vv),)
                attb.ready = (("v", prog.cnt["v"]),)
                rowsg[0] = (("pe", pevR),)

                def epi_ob(ep, bk, pev, lo, w):
                    prog.need("vector", t_mer((("pe", pev),), tmpE.guard,
                                              pre_ld))
                    vv = prog.inc("vector", lambda eng, bk=bk, w=w, lo=lo,
                                  s1=M("bo", ep, 1): eng.tensor_scalar(
                                      out=tmpE.rg(lo, w), in0=pb[bk][:, 0:w],
                                      scalar1=s1, scalar2=None, op0=ALU.add),
                                  "v", 1)
                    return (("v", vv),)
                proj(3 * D * DT, attb, NE, DT, DT, epi_ob, banks=(0, 1),
                     single=True)
                tmpE.ready = (("v", prog.cnt["v"]),)
                attb.guard = (("pe", prog.cnt["pe"]),)
                prog.need("vector", t_mer(tmpE.ready, x2f.ready, attb.ready))
                vv = prog.inc("vector", lambda e, o=nm.all(), a=tmpE.all(),
                              b_=x2f.all(): e.tensor_add(out=o, in0=a, in1=b_),
                              "v", 1)
                nm.ready = (("v", vv),)
                x2f.guard = nm.ready
                prog.need("sync", nm.ready)
                dstv = spv("sp_x3").rearrange("p t (b n) -> p t b n", b=B)
                _t = []
                for bb in range(B):
                    _t.append(prog.dma("sync", "hst",
                                       dstv[:, :, bb, s0:s0 + SE],
                                       nm.v4(B)[:, :, bb, :]))
                stv = t_mer(*_t)

                prog.need("vector", t_mer(nm.ready, x3bf.guard))
                vv = prog.inc("vector", lambda e, o=x3bf.all(), i=nm.all():
                              e.tensor_copy(out=o, in_=i), "v", 1)
                x3bf.ready = (("v", vv),)
                pev = ln_stats(x3bf, attb, NE, x3bf.ready)
                x3bf.guard = (("pe", pev),)
                ln_apply(nm, tmpE, pev, NE)
                nm.guard = t_mer(stv, tmpE.ready)
                if i == C.NTE - 1:
                    prog.need("vector", tmpE.ready)
                    for d in range(DT):
                        for bb in range(B):
                            bld = lambda e, d=d, bb=bb: e.tensor_copy(
                                out=M("bnd3", d * B + bb, 1),
                                in_=tmpE.sl(d, bb * SE + SE - 1, 1))
                            if d == DT - 1 and bb == B - 1:
                                b3cell[0] = prog.inc("vector", bld, "v", 1)
                            else:
                                prog.emit("vector", bld)
                prog.need("sync", tmpE.ready)
                dstv = spv("sp_hf2").rearrange("p t (b n) -> p t b n", b=B)
                _t = []
                for bb in range(B):
                    _t.append(prog.dma("sync", "hst",
                                       dstv[:, :, bb, s0:s0 + SE],
                                       tmpE.v4(B)[:, :, bb, :]))
                tmpE.guard = t_mer(*_t, (("v", b3cell[0]),)
                                   if i == C.NTE - 1 else ())

            for i in range(C.NTE + 1):
                if i < C.NTE:
                    front_head(i)
                if i >= 1:
                    back(i - 1)
                if i < C.NTE:
                    front_tail(i)

            return b3cell[0]
        b3pub = pass_e()
        bnd3_edge = boundary_collective("bnd3", "B3_loc", "B3_all", "B3g",
                                        b3pub)
        prog.barrier()
        alloc_f.reset()
        alloc_b.reset()

        _wt.append(load_w("fWrT", 0 * D * DT, D, D))
        _wt.append(load_w("fWkT", 1 * D * DT, D, D))
        _wt.append(load_w("fWvT", 2 * D * DT, D, D))
        wready[0] = wdone()
        ffn_pass("sp_hf2", "sp_x3", "y_out", None, "bnd3", bnd3_edge, True)

        if taps:
            prog.barrier()
            for nm, tap in taps.items():
                prog.dma("sync", "hst", tap, ap[nm])
        for i in range(4):
            prog.wait("sync", f"st{i}", prog.cnt[f"st{i}"])
        for i in range(2):
            prog.wait("sync", f"hst{i}", prog.cnt[f"hst{i}"])
        for i in range(4):
            prog.wait("sync", f"ld{i}", prog.cnt[f"ld{i}"])
        for i in range(2):
            prog.wait("sync", f"gld{i}", prog.cnt[f"gld{i}"])

        with nc.Block() as block:
            @block.sync
            def _(eng):
                for f in prog.items["sync"]:
                    f(eng)

            @block.scalar
            def _(eng):
                for f in prog.items["scalar"]:
                    f(eng)

            @block.vector
            def _(eng):
                for f in prog.items["vector"]:
                    f(eng)

            @block.tensor
            def _(eng):
                for f in prog.items["tensor"]:
                    f(eng)

            @block.gpsimd
            def _(eng):
                for f in prog.items["gpsimd"]:
                    f(eng)

    return nc



def make_inputs(C, core, **inp):
    s_lo = core * C.SLOC
    s_hi = s_lo + C.SLOC
    x = inp["L0_attn_a_x"]
    xs = np.ascontiguousarray(
        x[:, s_lo:s_hi, :].transpose(2, 0, 1)).reshape(D, C.TOK)

    def wt(a):
        return np.ascontiguousarray(np.asarray(a).T)

    def pm(v):
        return np.ascontiguousarray(np.asarray(v).reshape(DT, 128).T)

    pos = np.arange(B, dtype=np.float32)[:, None]
    div = np.exp(np.arange(0, D, 2, dtype=np.float32)
                 * (-np.log(10000.0) / D))
    pe = np.zeros((B, D), dtype=np.float32)
    pe[:, 0::2] = np.sin(pos * div)
    pe[:, 1::2] = np.cos(pos * div)
    pe_pm = np.zeros((128, DT * B), np.float32)
    for t in range(DT):
        for b in range(B):
            pe_pm[:, t * B + b] = pe[b, t * 128:(t + 1) * 128]

    hmask = np.zeros((128, 8), np.float32)
    hexpo = np.zeros((128, 8), np.float32)
    bmask = np.zeros((128, 8), np.float32)
    for j in range(NCORES):
        if j < core:
            hmask[:, j] = 1.0
            hexpo[:, j] = 1024.0 * 0 + float(C.SLOC * (core - 1 - j))
        if j == core - 1:
            bmask[:, j] = 1.0

    m = dict(
        x0=xs,
        WrT=wt(inp["L0_attn_a_Wr"]), WwT=wt(inp["L0_attn_a_Ww"]),
        WkT=wt(inp["L0_attn_a_Wk"]), WvT=wt(inp["L0_attn_a_Wv"]),
        WoT=wt(inp["L0_attn_a_Wo"]), WrecT=wt(inp["L0_attn_a_Wrec"]),
        WkeyT=wt(inp["L0_attn_a_Wkey"]), WvalT=wt(inp["L0_attn_a_Wval"]),
        bWqT=wt(inp["L0_attn_b_Wq"]), bWkT=wt(inp["L0_attn_b_Wk"]),
        bWvT=wt(inp["L0_attn_b_Wv"]), bWoT=wt(inp["L0_attn_b_Wo"]),
        fWrT=wt(inp["L0_ffn_W_r"]), fWkT=wt(inp["L0_ffn_W_k"]),
        fWvT=wt(inp["L0_ffn_W_v"]),
        g1=pm(inp["L0_attn_a_ln1_g"]), b1=pm(inp["L0_attn_a_ln1_b"]),
        g2=pm(inp["L0_attn_a_ln2_g"]), b2=pm(inp["L0_attn_a_ln2_b"]),
        td=pm(inp["L0_attn_a_time_decay"]),
        mur=pm(inp["L0_ffn_mu_r"]), muk=pm(inp["L0_ffn_mu_k"]),
        bq=pm(inp["L0_attn_b_bq"]), bk=pm(inp["L0_attn_b_bk"]),
        bv=pm(inp["L0_attn_b_bv"]), bo=pm(inp["L0_attn_b_bo"]),
        pe=pe_pm, hmask=hmask, hexpo=hexpo, bmask=bmask,
    )
    return {k: np.ascontiguousarray(np.asarray(v, dtype=m[k].dtype
                                               if hasattr(v, "dtype")
                                               else np.float32))
            for k, v in m.items()}


def kernel(**inputs):
    C = CFG(sloc=1024, tca=256, se=128)
    nc = build_program(C)
    in_maps = [make_inputs(C, core, **inputs) for core in range(NCORES)]
    res = run_bass_kernel_spmd(nc, in_maps, core_ids=list(range(NCORES)))
    outs = []
    for core in range(NCORES):
        y = res.results[core]["y_out"]
        outs.append(y.reshape(D, B, C.SLOC).transpose(1, 2, 0))
    return np.concatenate(outs, axis=1).astype(np.float32)


# version 19
# speedup vs baseline: 1.0078x; 1.0078x over previous
import contextlib
import numpy as np
import concourse.bass as bass
import concourse.mybir as mybir
from concourse.bass_utils import run_bass_kernel_spmd

F32 = mybir.dt.float32
BF16 = mybir.dt.bfloat16
AF = mybir.ActivationFunctionType
ALU = mybir.AluOpType

ENGINES = ("sync", "scalar", "vector", "tensor", "gpsimd")
NCORES = 8
D = 1024
DT = 8
F = 4096
FT = 32
B = 4
LN_EPS = 1e-5
WKV_EPS = 1e-8
ATT_EPS = 1e-6


class CFG:
    def __init__(self, sloc=1024, tca=256, se=128, taps=()):
        self.SLOC = sloc
        self.TOK = B * sloc
        self.TCA = tca
        self.SE = se
        self.TPB = sloc // tca
        self.NTT = B * self.TPB
        self.NTE = sloc // se
        self.taps = tuple(taps)
        assert sloc % tca == 0 and sloc % se == 0


class Prog:
    def __init__(self, nc, stack, semnames):
        self.nc = nc
        self.items = {e: [] for e in ENGINES}
        self.sems = {n: stack.enter_context(nc.semaphore(n)) for n in semnames}
        self.cnt = {n: 0 for n in semnames}
        self.waited = {}
        self.nbar = 0

    CHAIN = {"vector": "v", "scalar": "s", "gpsimd": "g"}

    def emit(self, eng, build):
        ce = self.CHAIN.get(eng)
        if ce is not None:
            self.inc(eng, build, ce, 1)
        else:
            self.items[eng].append(build)

    def inc(self, eng, build, sem, k=1, chain=True):
        ce = self.CHAIN.get(eng)
        if ce is not None and chain:
            lo = getattr(self, "last_op", {}).get(eng)
            if lo:
                self.wait(eng, lo[0], lo[1])
        self.cnt[sem] += k
        v = self.cnt[sem]
        s = self.sems[sem]
        def f(e, b=build, s=s, k=k):
            b(e).then_inc(s, k)
        self.items[eng].append(f)
        if ce is not None and chain:
            self.last_op = getattr(self, "last_op", {})
            self.last_op[eng] = (sem, v)
        return v

    def wait(self, eng, sem, v):
        if v <= 0:
            return
        key = (eng, sem)
        if self.waited.get(key, 0) >= v:
            return
        self.waited[key] = v
        s = self.sems[sem]
        self.items[eng].append(lambda e, s=s, v=v: e.wait_ge(s, v))

    def need(self, eng, toks):
        for sem, v in toks:
            self.wait(eng, sem, v)

    def barrier(self):
        for i in range(4):
            self.wait("sync", f"ld{i}", self.cnt[f"ld{i}"])
            self.wait("sync", f"st{i}", self.cnt[f"st{i}"])
        for i in range(2):
            self.wait("gpsimd", f"gld{i}", self.cnt[f"gld{i}"])
            self.wait("sync", f"hst{i}", self.cnt[f"hst{i}"])
            self.wait("scalar", f"sld{i}", self.cnt[f"sld{i}"])
        self.nbar += 1
        target = 5 * self.nbar
        s = self.sems["bar"]
        for eng in ENGINES:
            def f(e, s=s):
                e.drain()
                e.sem_inc(s, 1)
            self.items[eng].append(f)
        self.cnt["bar"] = target
        for eng in ENGINES:
            self.waited[(eng, "bar")] = target
            self.items[eng].append(
                lambda e, s=s, t=target: e.wait_ge(s, t))

    GSIZE = {"ld": 4, "st": 4, "gld": 2, "hst": 2, "sld": 2}

    def dma(self, q, group, out, in_):
        self.rot = getattr(self, "rot", {})
        i = self.rot[group] = (self.rot.get(group, -1) + 1) % self.GSIZE[group]
        sem = f"{group}{i}"
        self.wait(q, sem, self.cnt[sem])
        v = self.inc(q, lambda e, o=out, i_=in_: e.dma_start(out=o, in_=i_),
                     sem, 16, chain=False)
        return ((sem, v),)


class Buf:
    def __init__(self, t, off, dts, n):
        self.t = t
        self.off = off
        self.dts = dts
        self.n = n
        self.ready = ()
        self.guard = ()

    def sl(self, d, c0=0, n=None):
        n = self.n if n is None else n
        base = self.off + d * self.n + c0
        return self.t[:, base:base + n]

    def all(self):
        return self.t[:, self.off:self.off + self.dts * self.n]

    def rg(self, lo, w):
        return self.t[:, self.off + lo:self.off + lo + w]

    def v3(self):
        return self.all().rearrange("p (t n) -> p t n", t=self.dts)

    def v4(self, b):
        return self.all().rearrange("p (t b n) -> p t b n", t=self.dts, b=b)


class Alloc:
    def __init__(self, tensor, total):
        self.tensor = tensor
        self.total = total
        self.off = 0

    def take(self, dts, n):
        need = dts * n
        assert self.off + need <= self.total, \
            f"SBUF alloc overflow: {self.off}+{need}>{self.total}"
        b = Buf(self.tensor, self.off, dts, n)
        self.off += need
        return b

    def reset(self):
        self.off = 0


def t_mer(*toks_seq):
    out = {}
    for toks in toks_seq:
        for sem, v in toks:
            if out.get(sem, 0) < v:
                out[sem] = v
    return tuple(out.items())


def build_program(C: CFG):
    nc = bass.Bass("TRN2", target_bir_lowering=False)
    TOK = C.TOK
    N = C.TCA

    def din(name, shape, dt=F32):
        return nc.dram_tensor(name, list(shape), dt, kind="ExternalInput").ap()

    ap = {}
    ap["x0"] = din("x0", [D, TOK])
    for nm in ("WrT", "WwT", "WkT", "WvT", "WoT", "WrecT",
               "bWqT", "bWkT", "bWvT", "bWoT", "fWrT", "fWkT", "fWvT"):
        ap[nm] = din(nm, [D, D])
    ap["WkeyT"] = din("WkeyT", [D, F])
    ap["WvalT"] = din("WvalT", [F, D])
    for nm in ("g1", "b1", "g2", "b2", "td", "mur", "muk", "bq", "bk", "bv",
               "bo"):
        ap[nm] = din(nm, [128, DT])
    ap["pe"] = din("pe", [128, DT * B])
    ap["hmask"] = din("hmask", [128, 8])
    ap["hexpo"] = din("hexpo", [128, 8])
    ap["bmask"] = din("bmask", [128, 8])

    ap["y_out"] = nc.dram_tensor("y_out", [D, TOK], F32,
                                 kind="ExternalOutput").ap()

    def spill(name, rows, dt):
        return nc.dram_tensor(name, [rows, TOK], dt, kind="Internal").ap()

    ap["sp_h0"] = spill("sp_h0", D, F32)
    ap["sp_xn1"] = spill("sp_xn1", D, BF16)
    ap["sp_num"] = spill("sp_num", D, F32)
    ap["sp_den"] = spill("sp_den", D, F32)
    ap["sp_y"] = spill("sp_y", D, F32)
    ap["sp_xn2"] = spill("sp_xn2", D, BF16)
    ap["sp_rcm"] = spill("sp_rcm", D, F32)
    ap["sp_kcm"] = spill("sp_kcm", F, BF16)
    ap["sp_x1"] = spill("sp_x1", D, F32)
    ap["sp_hf1"] = spill("sp_hf1", D, F32)
    ap["sp_x2"] = spill("sp_x2", D, F32)
    ap["sp_hp"] = spill("sp_hp", D, BF16)
    ap["sp_x3"] = spill("sp_x3", D, F32)
    ap["sp_hf2"] = spill("sp_hf2", D, F32)
    ap["sp_pq"] = spill("sp_pq", D, BF16)
    ap["sp_pk"] = spill("sp_pk", D, BF16)
    ap["sp_pv"] = spill("sp_pv", D, BF16)

    ap["SC_loc"] = nc.dram_tensor("SC_loc", [128, 64], F32, kind="Internal").ap()
    ap["SC_all"] = nc.dram_tensor("SC_all", [1024, 64], F32, kind="Internal").ap()
    ap["B1_loc"] = nc.dram_tensor("B1_loc", [128, 32], F32, kind="Internal").ap()
    ap["B1_all"] = nc.dram_tensor("B1_all", [1024, 32], F32, kind="Internal").ap()
    ap["B3_loc"] = nc.dram_tensor("B3_loc", [128, 32], F32, kind="Internal").ap()
    ap["B3_all"] = nc.dram_tensor("B3_all", [1024, 32], F32, kind="Internal").ap()

    taps = {}
    for nm in C.taps:
        rows = F if nm == "sp_kcm" else D
        dt = BF16 if nm in ("sp_xn1", "sp_xn2", "sp_kcm", "sp_hp") else F32
        taps[nm] = nc.dram_tensor("tap_" + nm, [rows, TOK], dt,
                                  kind="ExternalOutput").ap()

    def spv(name):
        return ap[name].rearrange("(t p) n -> p t n", p=128)

    with contextlib.ExitStack() as stack:
        E = stack.enter_context
        GENF = E(nc.sbuf_tensor("GENF", [128, 20736], F32))
        GENB = E(nc.sbuf_tensor("GENB", [128, 15360], BF16))
        WP = E(nc.sbuf_tensor("WP", [128, DT * F], BF16))
        MIS = E(nc.sbuf_tensor("MIS", [128, 2496], F32))
        ROWS = E(nc.sbuf_tensor("ROWS", [1, 5760], F32))
        ONES_B = E(nc.sbuf_tensor("ONES_B", [128, 129], BF16))
        OH2 = E(nc.sbuf_tensor("OH2", [128, 160], BF16))
        pb = [E(nc.psum_tensor(f"pb{i}", [128, 512], F32)) for i in range(8)]

        prog = Prog(nc, stack,
                    ["ld0", "ld1", "ld2", "ld3", "st0", "st1", "st2", "st3",
                     "gld0", "gld1", "hst0", "hst1", "sld0", "sld1",
                     "pe", "v", "s", "g", "cc", "bar"])

        mis = {}
        _mo = [0]
        def mtake(name, n):
            mis[name] = _mo[0]
            _mo[0] += n
            assert _mo[0] <= 2496
        for nm in ("g1", "b1", "g2", "b2", "td", "mur", "muk", "bq", "bk",
                   "bv", "bo", "bq1", "bk1", "etd"):
            mtake(nm, DT)
        mtake("pe", DT * B)
        mtake("hmask", 8); mtake("hexpo", 8); mtake("bmask", 8)
        mtake("carN", DT * B); mtake("carD", DT * B)
        mtake("gcN", DT * B); mtake("gcD", DT * B)
        mtake("coef", 8 * DT)
        mtake("Sg", 8 * 64)
        mtake("B1g", 8 * 32); mtake("bnd1", 32)
        mtake("B3g", 8 * 32); mtake("bnd3", 32)
        mtake("iota", C.SLOC)
        mtake("cst", 4)

        def M(name, c0=0, n=1):
            return MIS[:, mis[name] + c0: mis[name] + c0 + n]

        rws = {"mu": 0, "ex2": 512, "rstd": 1024, "onesf": 1536,
               "rden": 1664}
        def R(name, c0=0, n=1):
            return ROWS[0:1, rws[name] + c0: rws[name] + c0 + n]

        onesb_col = ONES_B[:, 128:129]
        allones = ONES_B[:, 0:128]

        alloc_f = Alloc(GENF, 20736)
        alloc_b = Alloc(GENB, 15360)

        bankfree = {}
        rowguard = [()]
        wready = [()]
        carry_edge = [0]

        pre_toks = []
        for nm in ("g1", "b1", "g2", "b2", "td", "mur", "muk", "bq", "bk",
                   "bv", "bo"):
            pre_toks.append(prog.dma("sync", "ld", M(nm, 0, DT), ap[nm]))
        pre_toks.append(prog.dma("sync", "ld", M("pe", 0, DT * B), ap["pe"]))
        pre_toks.append(prog.dma("sync", "ld", M("hmask", 0, 8), ap["hmask"]))
        pre_toks.append(prog.dma("sync", "ld", M("hexpo", 0, 8), ap["hexpo"]))
        pre_toks.append(prog.dma("sync", "ld", M("bmask", 0, 8), ap["bmask"]))
        pre_ld = t_mer(*pre_toks)

        gv = prog.inc("gpsimd", lambda e: e.iota(
            M("iota", 0, C.SLOC), pattern=[[1, C.SLOC]], base=1,
            channel_multiplier=0, allow_small_or_imprecise_dtypes=True), "g", 1)
        gv = prog.inc("gpsimd", lambda e: e.memset(ONES_B[:, :], 1.0), "g", 1)
        gv = prog.inc("gpsimd", lambda e: e.memset(OH2[:, :], 0.0), "g", 1)
        prog.wait("gpsimd", "g", gv)
        gv = prog.inc("gpsimd", lambda e: e.memset(OH2[:, 128:129], 1.0), "g", 1)
        gv = prog.inc("gpsimd", lambda e: e.memset(R("onesf", 0, 128), 1.0),
                      "g", 1)
        gv = prog.inc("gpsimd", lambda e: e.memset(M("cst", 0, 1), LN_EPS), "g", 1)
        gv = prog.inc("gpsimd", lambda e: e.memset(M("cst", 1, 1), -0.5), "g", 1)
        gv = prog.inc("gpsimd", lambda e: e.memset(M("cst", 2, 1), -1.0), "g", 1)
        nc.const_aps.register(F32, LN_EPS, M("cst", 0, 1))
        nc.const_aps.register(F32, -0.5, M("cst", 1, 1))
        nc.const_aps.register(F32, -1.0, M("cst", 2, 1))
        pre_g = (("g", gv),)
        prog.need("scalar", pre_g)
        prog.need("vector", pre_g)

        prog.need("vector", t_mer(pre_ld))
        vv = prog.inc("vector", lambda e: e.tensor_scalar(
            out=M("bq1", 0, DT), in0=M("bq", 0, DT), scalar1=1.0, scalar2=None,
            op0=ALU.add), "v", 1)
        vv = prog.inc("vector", lambda e: e.tensor_scalar(
            out=M("bk1", 0, DT), in0=M("bk", 0, DT), scalar1=1.0, scalar2=None,
            op0=ALU.add), "v", 1)
        pre_v = (("v", vv),)

        _wt = []

        def load_w(name, base, rows, cols):
            return prog.dma("gpsimd", "gld",
                            WP[:, base:base + (rows // 128) * cols].rearrange(
                                "p (t e) -> p t e", t=rows // 128),
                            ap[name].rearrange("(t p) e -> p t e", p=128))

        def wdone():
            t = t_mer(*_wt)
            _wt.clear()
            return t

        prog.need("scalar", pre_ld)
        pre_s_etd = prog.inc("scalar", lambda e: e.activation(
            out=M("etd", 0, DT), in_=M("td", 0, DT), func=AF.Exp), "s", 1)

        def ln_stats(x_bf, sq, n, src_toks):
            prog.need("vector", t_mer(src_toks, sq.guard, rowguard[0]))
            vv = prog.inc("vector", lambda e: e.tensor_mul(
                out=sq.all(), in0=x_bf.all(), in1=x_bf.all()), "v", 1)
            sq.ready = (("v", vv),)
            prog.need("tensor", t_mer(src_toks, sq.ready, pre_g,
                                      bankfree.get(4, ()), bankfree.get(7, ())))
            for d in range(DT):
                prog.emit("tensor", lambda e, d=d: e.matmul(
                    pb[4][0:1, 0:n], onesb_col, x_bf.sl(d, 0, n),
                    start=(d == 0), stop=(d == DT - 1)))
            for d in range(DT):
                bld = lambda e, d=d: e.matmul(
                    pb[7][0:1, 0:n], onesb_col, sq.sl(d, 0, n),
                    start=(d == 0), stop=(d == DT - 1))
                if d == DT - 1:
                    pev = prog.inc("tensor", bld, "pe", 1)
                else:
                    prog.emit("tensor", bld)
            sq.guard = (("pe", pev),)
            prog.wait("vector", "pe", pev)
            prog.emit("vector", lambda e: e.tensor_scalar(
                out=R("mu", 0, n), in0=pb[4][0:1, 0:n], scalar1=1.0 / D,
                scalar2=None, op0=ALU.mult))
            prog.emit("vector", lambda e: e.tensor_scalar(
                out=R("ex2", 0, n), in0=pb[7][0:1, 0:n], scalar1=1.0 / D,
                scalar2=None, op0=ALU.mult))
            prog.emit("vector", lambda e: e.tensor_mul(
                out=R("rstd", 0, n), in0=R("mu", 0, n), in1=R("mu", 0, n)))
            vv = prog.inc("vector", lambda e: e.tensor_tensor(
                out=R("ex2", 0, n), in0=R("ex2", 0, n), in1=R("rstd", 0, n),
                op=ALU.subtract), "v", 1)
            bankfree[4] = (("v", vv),)
            bankfree[7] = (("v", vv),)
            prog.need("scalar", t_mer((("v", vv),), rowguard[0]))
            prog.emit("scalar", lambda e: e.activation(
                out=R("rstd", 0, n), in_=R("ex2", 0, n), func=AF.Ln,
                bias=LN_EPS))
            svv = prog.inc("scalar", lambda e: e.activation(
                out=R("rstd", 0, n), in_=R("rstd", 0, n), func=AF.Exp,
                scale=-0.5), "s", 1)
            prog.need("tensor", t_mer((("v", vv),), (("s", svv),), pre_g,
                                      bankfree.get(5, ()), bankfree.get(6, ())))
            prog.emit("tensor", lambda e: e.matmul(
                pb[5][:, 0:n], R("onesf", 0, 128), R("mu", 0, n), start=True,
                stop=True))
            pev = prog.inc("tensor", lambda e: e.matmul(
                pb[6][:, 0:n], R("onesf", 0, 128), R("rstd", 0, n), start=True,
                stop=True), "pe", 1)
            rowguard[0] = (("pe", pev),)
            return pev

        def ln_apply(x_f32, out_buf, pev, n, extra=()):
            prog.need("vector", t_mer((("pe", pev),), x_f32.ready,
                                      out_buf.guard, extra))
            for d in range(DT):
                prog.emit("vector", lambda e, d=d: e.tensor_tensor(
                    out=out_buf.sl(d, 0, n), in0=x_f32.sl(d, 0, n),
                    in1=pb[5][:, 0:n], op=ALU.subtract))
            for d in range(DT):
                bld = lambda e, d=d: e.tensor_tensor(
                    out=out_buf.sl(d, 0, n), in0=out_buf.sl(d, 0, n),
                    in1=pb[6][:, 0:n], op=ALU.mult)
                if d == DT - 1:
                    vv = prog.inc("vector", bld, "v", 1)
                else:
                    prog.emit("vector", bld)
            out_buf.ready = (("v", vv),)
            bankfree[5] = (("v", vv),)
            bankfree[6] = (("v", vv),)
            return vv

        def ln_g(xf, xbf, sq, out, n, mu_sb, rs_sb, xf_toks=None):
            assert 2 * n <= 512
            prog.need("vector", t_mer(xbf.ready, sq.guard))
            gv = prog.inc("vector", lambda e, o=sq.all(), i=xbf.all():
                          e.tensor_mul(out=o, in0=i, in1=i), "v", 1)
            sq.ready = (("v", gv),)
            prog.need("tensor", t_mer(xbf.ready, sq.ready, pre_g,
                                      bankfree.get(4, ())))
            for d in range(DT):
                prog.emit("tensor", lambda e, d=d: e.matmul(
                    pb[4][0:1, 0:n], onesb_col, xbf.sl(d, 0, n),
                    start=(d == 0), stop=(d == DT - 1)))
            for d in range(DT):
                bld = lambda e, d=d: e.matmul(
                    pb[4][0:1, n:2 * n], onesb_col, sq.sl(d, 0, n),
                    start=(d == 0), stop=(d == DT - 1))
                if d == DT - 1:
                    pev = prog.inc("tensor", bld, "pe", 1)
                else:
                    prog.emit("tensor", bld)
            sq.guard = (("pe", pev),)
            xbf.guard = (("pe", pev),)
            prog.need("vector", t_mer((("pe", pev),), rowguard[0]))
            prog.emit("vector", lambda e: e.tensor_scalar(
                out=R("mu", 0, n), in0=pb[4][0:1, 0:n], scalar1=1.0 / D,
                scalar2=None, op0=ALU.mult))
            prog.emit("vector", lambda e: e.tensor_scalar(
                out=R("ex2", 0, n), in0=pb[4][0:1, n:2 * n], scalar1=1.0 / D,
                scalar2=None, op0=ALU.mult))
            prog.emit("vector", lambda e: e.tensor_mul(
                out=R("rstd", 0, n), in0=R("mu", 0, n), in1=R("mu", 0, n)))
            vv = prog.inc("vector", lambda e: e.tensor_tensor(
                out=R("ex2", 0, n), in0=R("ex2", 0, n), in1=R("rstd", 0, n),
                op=ALU.subtract), "v", 1)
            bankfree[4] = (("v", vv),)
            prog.need("scalar", t_mer((("v", vv),), rowguard[0]))
            prog.emit("scalar", lambda e: e.activation(
                out=R("rstd", 0, n), in_=R("ex2", 0, n), func=AF.Ln,
                bias=LN_EPS))
            svv = prog.inc("scalar", lambda e: e.activation(
                out=R("rstd", 0, n), in_=R("rstd", 0, n), func=AF.Exp,
                scale=-0.5), "s", 1)
            prog.need("tensor", t_mer((("v", vv),), (("s", svv),), pre_g,
                                      bankfree.get(5, ())))
            prog.emit("tensor", lambda e: e.matmul(
                pb[5][:, 0:n], R("onesf", 0, 128), R("mu", 0, n), start=True,
                stop=True))
            pev = prog.inc("tensor", lambda e: e.matmul(
                pb[5][:, n:2 * n], R("onesf", 0, 128), R("rstd", 0, n),
                start=True, stop=True), "pe", 1)
            rowguard[0] = (("pe", pev),)
            prog.need("vector", t_mer((("pe", pev),), out.guard,
                                      xf_toks if xf_toks else xf.ready))
            mb = pb[5][:, 0:n].rearrange("p n -> p () n").broadcast_to(
                [128, DT, n])
            rb = pb[5][:, n:2 * n].rearrange("p n -> p () n").broadcast_to(
                [128, DT, n])
            o3 = out.all().rearrange("p (t n) -> p t n", t=DT)
            i3 = xf.all().rearrange("p (t n) -> p t n", t=DT)
            prog.emit("vector", lambda e, o=o3, i=i3, m=mb: e.tensor_tensor(
                out=o, in0=i, in1=m, op=ALU.subtract))
            vv = prog.inc("vector", lambda e, o=o3, r=rb: e.tensor_tensor(
                out=o, in0=o, in1=r, op=ALU.mult), "v", 1)
            out.ready = (("v", vv),)
            bankfree[5] = out.ready
            return out.ready

        def proj(w_base, x_buf, n, n_e, n_k, epilogue, banks, we_stride=None,
                 single=False):
            stride = we_stride if we_stride else n_e * 128
            pair = 2 if (2 * n <= 512 and n_e % 2 == 0 and not single) else 1
            for ep in range(n_e // pair):
                bk = banks[ep % len(banks)]
                prog.need("tensor", t_mer(bankfree.get(bk, ()), x_buf.ready,
                                          wready[0]))
                for half in range(pair):
                    e = ep * pair + half
                    for k in range(n_k):
                        base = w_base + k * stride + e * 128
                        bld = lambda e_, k=k, base=base, bk=bk, half=half: \
                            e_.matmul(pb[bk][:, half * n:(half + 1) * n],
                                      WP[:, base:base + 128],
                                      x_buf.sl(k, 0, n),
                                      start=(k == 0), stop=(k == n_k - 1))
                        if k == n_k - 1 and half == pair - 1:
                            pev = prog.inc("tensor", bld, "pe", 1)
                        else:
                            prog.emit("tensor", bld)
                bankfree[bk] = epilogue(ep, bk, pev, ep * pair * n, pair * n)

        def pass_a():
            _wt.append(load_w("WwT", 0 * D * DT, D, D))
            _wt.append(load_w("WkT", 1 * D * DT, D, D))
            _wt.append(load_w("WvT", 2 * D * DT, D, D))
            wready[0] = wdone()

            x0f = [alloc_f.take(DT, N), alloc_f.take(DT, N)]
            h0 = alloc_f.take(DT, N)
            th = alloc_f.take(DT, N)
            ewk = alloc_f.take(DT, N)
            tb = alloc_f.take(DT, N)
            u = alloc_f.take(DT, N)
            numb = alloc_f.take(DT, N)
            denb = alloc_f.take(DT, N)
            x0b = alloc_b.take(DT, N)
            sqb = alloc_b.take(DT, N)
            h0b = alloc_b.take(DT, N)
            xn1 = [alloc_b.take(DT, N) for _ in range(3)]
            g1b = M("g1", 0, DT).rearrange("p t -> p t ()").broadcast_to(
                [128, DT, N])
            b1b = M("b1", 0, DT).rearrange("p t -> p t ()").broadcast_to(
                [128, DT, N])

            def stats_mm(xbf, sq):
                prog.need("tensor", t_mer(xbf.ready, sq.ready, pre_g,
                                          bankfree.get(4, ())))
                for d in range(DT):
                    prog.emit("tensor", lambda e, d=d, x=xbf: e.matmul(
                        pb[4][0:1, 0:N], onesb_col, x.sl(d), start=(d == 0),
                        stop=(d == DT - 1)))
                for d in range(DT):
                    bld = lambda e, d=d, q=sq: e.matmul(
                        pb[4][0:1, N:2 * N], onesb_col, q.sl(d),
                        start=(d == 0), stop=(d == DT - 1))
                    if d == DT - 1:
                        pev = prog.inc("tensor", bld, "pe", 1)
                    else:
                        prog.emit("tensor", bld)
                sq.guard = (("pe", pev),)
                xbf.guard = (("pe", pev),)
                return pev

            def rows_rstd(pev):
                prog.need("vector", t_mer((("pe", pev),), rowguard[0]))
                prog.emit("vector", lambda e: e.tensor_scalar(
                    out=R("mu", 0, N), in0=pb[4][0:1, 0:N], scalar1=1.0 / D,
                    scalar2=None, op0=ALU.mult))
                prog.emit("vector", lambda e: e.tensor_scalar(
                    out=R("ex2", 0, N), in0=pb[4][0:1, N:2 * N],
                    scalar1=1.0 / D, scalar2=None, op0=ALU.mult))
                prog.emit("vector", lambda e: e.tensor_mul(
                    out=R("rstd", 0, N), in0=R("mu", 0, N),
                    in1=R("mu", 0, N)))
                vv = prog.inc("vector", lambda e: e.tensor_tensor(
                    out=R("ex2", 0, N), in0=R("ex2", 0, N),
                    in1=R("rstd", 0, N), op=ALU.subtract), "v", 1)
                bankfree[4] = (("v", vv),)
                prog.need("scalar", t_mer((("v", vv),), rowguard[0]))
                prog.emit("scalar", lambda e: e.activation(
                    out=R("rstd", 0, N), in_=R("ex2", 0, N), func=AF.Ln,
                    bias=LN_EPS))
                svv = prog.inc("scalar", lambda e: e.activation(
                    out=R("rstd", 0, N), in_=R("rstd", 0, N), func=AF.Exp,
                    scale=-0.5), "s", 1)
                return vv, svv

            def bcast(vv, svv):
                prog.need("tensor", t_mer((("v", vv),), (("s", svv),), pre_g,
                                          bankfree.get(5, ())))
                prog.emit("tensor", lambda e: e.matmul(
                    pb[5][:, 0:N], R("onesf", 0, 128), R("mu", 0, N),
                    start=True, stop=True))
                pev = prog.inc("tensor", lambda e: e.matmul(
                    pb[5][:, N:2 * N], R("onesf", 0, 128), R("rstd", 0, N),
                    start=True, stop=True), "pe", 1)
                rowguard[0] = (("pe", pev),)
                return pev

            def apply_v(xf, out, pev, xf_toks=None):
                prog.need("vector", t_mer((("pe", pev),), out.guard,
                                          xf_toks if xf_toks else xf.ready))
                mb = pb[5][:, 0:N].rearrange("p n -> p () n").broadcast_to(
                    [128, DT, N])
                rb = pb[5][:, N:2 * N].rearrange("p n -> p () n").broadcast_to(
                    [128, DT, N])
                o3 = out.v3()
                i3 = xf.v3()
                prog.emit("vector", lambda e, o=o3, i=i3, m=mb:
                          e.tensor_tensor(out=o, in0=i, in1=m,
                                          op=ALU.subtract))
                vv = prog.inc("vector", lambda e, o=o3, r=rb: e.tensor_tensor(
                    out=o, in0=o, in1=r, op=ALU.mult), "v", 1)
                out.ready = (("v", vv),)
                bankfree[5] = out.ready
                return vv

            st = {}

            def front_p1(i):
                b, j = divmod(i, C.TPB)
                col0 = b * C.SLOC + j * C.TCA
                xf = x0f[i % 2]
                prog.need("sync", xf.guard)
                lv = prog.dma("sync", "ld", xf.v3(),
                              spv("x0")[:, :, col0:col0 + N])
                xf.ready = lv
                prog.need("vector", t_mer(xf.ready, x0b.guard))
                gv = prog.inc("vector", lambda e, o=x0b.all(), i_=xf.all():
                              e.tensor_copy(out=o, in_=i_), "v", 1)
                x0b.ready = (("v", gv),)
                prog.need("vector", t_mer(x0b.ready, sqb.guard))
                gv = prog.inc("vector", lambda e, o=sqb.all(), i_=x0b.all():
                              e.tensor_mul(out=o, in0=i_, in1=i_), "v", 1)
                sqb.ready = (("v", gv),)
                st[i] = dict(col0=col0, b=b, j=j, xf=xf)
                st[i]["pev0"] = stats_mm(x0b, sqb)

            def front_p2(i):
                t = st[i]
                vv, svv = rows_rstd(t["pev0"])
                pev = bcast(vv, svv)
                apply_v(t["xf"], h0, pev)
                t["xf"].guard = h0.ready
                prog.need("gpsimd", h0.ready)
                t["st_h0"] = prog.dma(
                    "gpsimd", "st",
                    spv("sp_h0")[:, :, t["col0"]:t["col0"] + N], h0.v3())
                prog.need("vector", t_mer(h0.ready, h0b.guard))
                gv = prog.inc("vector", lambda e, o=h0b.all(), i_=h0.all():
                              e.tensor_copy(out=o, in_=i_), "v", 1)
                h0b.ready = (("v", gv),)
                prog.need("vector", t_mer(h0b.ready, sqb.guard))
                gv = prog.inc("vector", lambda e, o=sqb.all(), i_=h0b.all():
                              e.tensor_mul(out=o, in0=i_, in1=i_), "v", 1)
                sqb.ready = (("v", gv),)
                t["pev1"] = stats_mm(h0b, sqb)

            def front_p3(i):
                t = st[i]
                xn = xn1[i % 3]
                vv, svv = rows_rstd(t["pev1"])
                pev = bcast(vv, svv)
                apply_v(h0, th, pev)
                h0.guard = t_mer(th.ready, t["st_h0"])
                prog.need("vector", t_mer(th.ready, xn.guard, pre_ld))
                th3 = th.v3()
                prog.emit("vector", lambda e, o=th3, g=g1b: e.tensor_tensor(
                    out=o, in0=o, in1=g, op=ALU.mult))
                vv = prog.inc("vector", lambda e, o=xn.v3(), i_=th3, bb_=b1b:
                              e.tensor_tensor(out=o, in0=i_, in1=bb_,
                                              op=ALU.add), "v", 1)
                xn.ready = (("v", vv),)
                th.guard = xn.ready
                prog.need("gpsimd", xn.ready)
                st_xn = prog.dma(
                    "gpsimd", "st",
                    spv("sp_xn1")[:, :, t["col0"]:t["col0"] + N], xn.v3())
                xn.guard = st_xn

            def back_pw(i):
                xn = xn1[i % 3]
                def epi_w(ep, bk, pev, lo, w):
                    prog.need("scalar", t_mer((("pe", pev),), ewk.guard))
                    sv = prog.inc("scalar", lambda eng, bk=bk, o=ewk.rg(lo, w), w=w:
                                  eng.activation(out=o, in_=pb[bk][:, 0:w],
                                                 func=AF.Exp), "s", 1)
                    return (("s", sv),)
                proj(0 * D * DT, xn, N, DT, DT, epi_w, banks=(0, 1, 2, 3, 6, 7))
                st[i]["sv_w"] = prog.cnt["s"]

            def back_pk(i):
                xn = xn1[i % 3]
                sv_w = st[i]["sv_w"]
                def epi_k(ep, bk, pev, lo, w):
                    prog.need("vector", t_mer((("pe", pev),), (("s", sv_w),),
                                              tb.guard))
                    vv = prog.inc("vector", lambda eng, bk=bk, o=tb.rg(lo, w),
                                  i_=ewk.rg(lo, w), w=w:
                                  eng.tensor_tensor(out=o,
                                                    in0=pb[bk][:, 0:w],
                                                    in1=i_,
                                                    op=ALU.subtract), "v", 1)
                    return (("v", vv),)
                proj(1 * D * DT, xn, N, DT, DT, epi_k, banks=(0, 1, 2, 3, 6, 7))
                vv_k = prog.cnt["v"]
                prog.need("scalar", (("v", vv_k),))
                sv = prog.inc("scalar", lambda e: e.activation(
                    out=ewk.all(), in_=tb.all(), func=AF.Exp), "s", 1)
                ewk.ready = (("s", sv),)
                tb.guard = ewk.ready

            def back_pv(i):
                xn = xn1[i % 3]
                def epi_v(ep, bk, pev, lo, w):
                    prog.need("vector", t_mer((("pe", pev),), ewk.ready,
                                              u.guard))
                    vv = prog.inc("vector", lambda eng, bk=bk, o=u.rg(lo, w),
                                  i_=ewk.rg(lo, w), w=w:
                                  eng.tensor_tensor(out=o, in0=i_,
                                                    in1=pb[bk][:, 0:w],
                                                    op=ALU.mult), "v", 1)
                    return (("v", vv),)
                proj(2 * D * DT, xn, N, DT, DT, epi_v, banks=(0, 1, 2, 3, 6, 7))
                u.ready = (("v", prog.cnt["v"]),)
                xn.guard = t_mer(xn.guard, (("pe", prog.cnt["pe"]),))

            def back_scan(i):
                t = st.pop(i)
                b, j, col0 = t["b"], t["j"], t["col0"]
                prog.need("vector", t_mer(u.ready, ewk.ready, pre_ld,
                                          numb.guard, denb.guard,
                                          (("s", pre_s_etd),)))
                if j > 0:
                    prog.wait("vector", "v", carry_edge[0])
                for d in range(DT):
                    dec_d = M("etd", d, 1).broadcast_to([128, N])
                    init_n = 0.0 if j == 0 else M("carN", d * B + b, 1)
                    init_d = 0.0 if j == 0 else M("carD", d * B + b, 1)
                    prog.emit("vector",
                              lambda e, d=d, dd=dec_d, init=init_n:
                              e.tensor_tensor_scan(
                                  out=numb.sl(d), data0=dd, data1=u.sl(d),
                                  initial=init, op0=ALU.mult, op1=ALU.add))
                    prog.emit("vector",
                              lambda e, d=d, dd=dec_d, init=init_d:
                              e.tensor_tensor_scan(
                                  out=denb.sl(d), data0=dd, data1=ewk.sl(d),
                                  initial=init, op0=ALU.mult, op1=ALU.add))
                carN3 = M("carN", 0, DT * B).rearrange(
                    "p (d b) -> p d b", d=DT)[:, :, b:b + 1]
                carD3 = M("carD", 0, DT * B).rearrange(
                    "p (d b) -> p d b", d=DT)[:, :, b:b + 1]
                prog.emit("vector", lambda e, o=carN3: e.tensor_copy(
                    out=o, in_=numb.v3()[:, :, N - 1:N]))
                vv = prog.inc("vector", lambda e, o=carD3: e.tensor_copy(
                    out=o, in_=denb.v3()[:, :, N - 1:N]), "v", 1)
                carry_edge[0] = vv
                numb.ready = denb.ready = (("v", vv),)
                u.guard = (("v", vv),)
                ewk.guard = (("v", vv),)
                prog.need("gpsimd", numb.ready)
                stv = prog.dma("gpsimd", "st",
                               spv("sp_num")[:, :, col0:col0 + N], numb.v3())
                numb.guard = stv
                stv = prog.dma("gpsimd", "st",
                               spv("sp_den")[:, :, col0:col0 + N], denb.v3())
                denb.guard = stv

            for i in range(C.NTT + 2):
                if i < C.NTT:
                    front_p1(i)
                if i >= 2:
                    back_pw(i - 2)
                if i < C.NTT:
                    front_p2(i)
                if i >= 2:
                    back_pk(i - 2)
                if i < C.NTT:
                    front_p3(i)
                if i >= 2:
                    back_pv(i - 2)
                    back_scan(i - 2)

        pass_a()
        def coll_sc():
            prog.need("sync", (("v", carry_edge[0]),))
            spv_sc = ap["SC_loc"].rearrange("p (t c b) -> p t c b", t=DT, c=2)
            stv1 = prog.dma("sync", "hst", spv_sc[:, :, 0, :],
                            M("carN", 0, DT * B).rearrange("p (t b) -> p t b", t=DT))
            stv2 = prog.dma("sync", "hst", spv_sc[:, :, 1, :],
                            M("carD", 0, DT * B).rearrange("p (t b) -> p t b", t=DT))
            prog.need("gpsimd", t_mer(stv1, stv2))
            ccv = prog.inc("gpsimd", lambda e: e.collective_compute(
                "AllGather", ALU.bypass, replica_groups=[list(range(NCORES))],
                ins=[ap["SC_loc"]], outs=[ap["SC_all"]]), "cc", 1)
            prog.wait("sync", "cc", ccv)
            lv_sg = prog.dma("sync", "ld", M("Sg", 0, 8 * 64),
                             ap["SC_all"].rearrange("(j p) c -> p j c", p=128))
            prog.need("vector", t_mer(pre_ld))
            for jj in range(8):
                bld = lambda e, jj=jj: e.tensor_scalar(
                    out=M("coef", jj * DT, DT), in0=M("td", 0, DT),
                    scalar1=M("hexpo", jj, 1), scalar2=None, op0=ALU.mult)
                if jj == 7:
                    vv = prog.inc("vector", bld, "v", 1)
                else:
                    prog.emit("vector", bld)
            prog.wait("scalar", "v", vv)
            sv = prog.inc("scalar", lambda e: e.activation(
                out=M("coef", 0, 8 * DT), in_=M("coef", 0, 8 * DT), func=AF.Exp),
                "s", 1)
            prog.wait("vector", "s", sv)
            for jj in range(8):
                bld = lambda e, jj=jj: e.tensor_scalar(
                    out=M("coef", jj * DT, DT), in0=M("coef", jj * DT, DT),
                    scalar1=M("hmask", jj, 1), scalar2=None, op0=ALU.mult)
                if jj == 7:
                    vv = prog.inc("vector", bld, "v", 1)
                else:
                    prog.emit("vector", bld)
            prog.wait("vector", "v", vv)
            prog.need("vector", lv_sg)
            last = None
            for jj in range(8):
                for t in range(DT):
                    for cn, tens in (("gcN", 0), ("gcD", 1)):
                        src = M("Sg", jj * 64 + t * 8 + tens * 4, B)
                        dst = M(cn, t * B, B)
                        if jj == 0:
                            bld = lambda e, src=src, dst=dst, t=t: e.tensor_scalar(
                                out=dst, in0=src, scalar1=M("coef", t, 1),
                                scalar2=None, op0=ALU.mult)
                        else:
                            bld = (lambda e, src=src, dst=dst, t=t, jj=jj:
                                   e.scalar_tensor_tensor(
                                       out=dst, in0=src,
                                       scalar=M("coef", jj * DT + t, 1),
                                       in1=dst, op0=ALU.mult, op1=ALU.add))
                        prog.emit("vector", bld)
            gcar_edge = prog.cnt["v"]
            return gcar_edge
        gcar_edge = coll_sc()

        prog.barrier()
        alloc_f.reset()
        alloc_b.reset()

        def pass_b():
            _wt.append(load_w("WrT", 0 * D * DT, D, D))
            _wt.append(load_w("WoT", 1 * D * DT, D, D))
            _wt.append(load_w("WrecT", 2 * D * DT, D, D))
            wready[0] = wdone()

            numb = [alloc_f.take(DT, N), alloc_f.take(DT, N)]
            denb = alloc_f.take(DT, N)
            scl = alloc_f.take(DT, N)
            rden = alloc_f.take(DT, N)
            rscr = scl
            h0 = alloc_f.take(DT, N)
            r1 = alloc_f.take(DT, N)
            yb = alloc_f.take(DT, N)
            th = alloc_f.take(DT, N)
            mu_sb = alloc_f.take(1, N)
            rs_sb = alloc_f.take(1, N)
            xn1 = alloc_b.take(DT, N)
            rw = alloc_b.take(DT, N)
            ybf = alloc_b.take(DT, N)
            sqb = alloc_b.take(DT, N)
            xn2 = alloc_b.take(DT, N)
            g2b = M("g2", 0, DT).rearrange("p t -> p t ()").broadcast_to(
                [128, DT, N])
            b2b = M("b2", 0, DT).rearrange("p t -> p t ()").broadcast_to(
                [128, DT, N])

            def front(i):
                b, j = divmod(i, C.TPB)
                col0 = b * C.SLOC + j * C.TCA
                s0 = j * C.TCA
                nb = numb[i % 2]
                prog.need("sync", nb.guard)
                lvn = prog.dma("sync", "ld", nb.v3(),
                               spv("sp_num")[:, :, col0:col0 + N])
                nb.ready = lvn
                prog.need("sync", denb.guard)
                lvd = prog.dma("sync", "ld", denb.v3(),
                               spv("sp_den")[:, :, col0:col0 + N])
                denb.ready = lvd
                prog.need("scalar", t_mer(pre_g, pre_ld, scl.guard))
                for d in range(DT):
                    bld = lambda e, d=d, ia=M("iota", s0, N): e.activation(
                        out=scl.sl(d), in_=ia, func=AF.Exp,
                        scale=M("td", d, 1))
                    if d == DT - 1:
                        sv = prog.inc("scalar", bld, "s", 1)
                    else:
                        prog.emit("scalar", bld)
                scl.ready = (("s", sv),)
                gcnb = M("gcN", 0, DT * B).rearrange(
                    "p (d b) -> p d b", d=DT)[:, :, b:b + 1].broadcast_to(
                    [128, DT, N])
                gcdb = M("gcD", 0, DT * B).rearrange(
                    "p (d b) -> p d b", d=DT)[:, :, b:b + 1].broadcast_to(
                    [128, DT, N])
                prog.need("vector", t_mer(nb.ready, denb.ready, scl.ready,
                                          (("v", gcar_edge),), rden.guard,
                                          rscr.guard))
                prog.emit("vector", lambda e, o=rden.v3(), i=scl.v3(), g=gcnb:
                          e.tensor_tensor(out=o, in0=i, in1=g, op=ALU.mult))
                prog.emit("vector", lambda e, o=nb.v3(), i=rden.v3():
                          e.tensor_add(out=o, in0=o, in1=i))
                prog.emit("vector", lambda e, o=rscr.v3(), i=scl.v3(), g=gcdb:
                          e.tensor_tensor(out=o, in0=i, in1=g, op=ALU.mult))
                prog.emit("vector", lambda e, o=denb.v3(), i=rscr.v3():
                          e.tensor_add(out=o, in0=o, in1=i))
                vv = prog.inc("vector", lambda e, o=denb.all():
                              e.tensor_scalar(out=o, in0=o, scalar1=WKV_EPS,
                                              scalar2=None, op0=ALU.add),
                              "v", 1)
                scl.guard = (("v", vv),)
                prog.wait("scalar", "v", vv)
                prog.need("scalar", rscr.guard)
                prog.emit("scalar", lambda e, o=rscr.all(), i=denb.all():
                          e.activation(out=o, in_=i, func=AF.Ln))
                sv = prog.inc("scalar", lambda e, o=rden.all(), i=rscr.all():
                              e.activation(out=o, in_=i, func=AF.Exp,
                                           scale=-1.0), "s", 1)
                rden.ready = (("s", sv),)
                prog.need("vector", t_mer(rden.ready,))
                gv = prog.inc("vector", lambda e, o=nb.all(), r=rden.all():
                              e.tensor_mul(out=o, in0=o, in1=r), "v", 1)
                nb.ready = (("v", gv),)
                denb.guard = nb.ready
                rden.guard = nb.ready
                rscr.guard = nb.ready

            def back(i):
                b, j = divmod(i, C.TPB)
                col0 = b * C.SLOC + j * C.TCA
                nb = numb[i % 2]
                prog.need("sync", xn1.guard)
                lvx = prog.dma("sync", "ld", xn1.v3(),
                               spv("sp_xn1")[:, :, col0:col0 + N])
                xn1.ready = lvx
                prog.need("sync", h0.guard)
                lvh = prog.dma("sync", "ld", h0.v3(),
                               spv("sp_h0")[:, :, col0:col0 + N])
                h0.ready = lvh

                def epi_r(ep, bk, pev, lo, w):
                    prog.need("scalar", t_mer((("pe", pev),), r1.guard))
                    sv = prog.inc("scalar", lambda eng, bk=bk, o=r1.rg(lo, w),
                                  w=w: eng.activation(out=o,
                                                      in_=pb[bk][:, 0:w],
                                                      func=AF.Sigmoid),
                                  "s", 1)
                    return (("s", sv),)
                proj(0 * D * DT, xn1, N, DT, DT, epi_r, banks=(0, 1, 2, 3, 6, 7))
                r1.ready = (("s", prog.cnt["s"]),)

                prog.need("gpsimd", t_mer(r1.ready, nb.ready, rw.guard))
                gv = prog.inc("gpsimd", lambda e, o=rw.all(), i=nb.all(),
                              r=r1.all(): e.tensor_tensor(
                                  out=o, in0=i, in1=r, op=ALU.mult), "g", 1)
                rw.ready = (("g", gv),)
                nb.guard = rw.ready
                r1.guard = rw.ready

                def epi_o(ep, bk, pev, lo, w):
                    prog.need("vector", t_mer((("pe", pev),), h0.ready,
                                              yb.guard))
                    vv = prog.inc("vector", lambda eng, bk=bk, o=yb.rg(lo, w),
                                  i_=h0.rg(lo, w), w=w:
                                  eng.tensor_tensor(out=o,
                                                    in0=pb[bk][:, 0:w],
                                                    in1=i_, op=ALU.add),
                                  "v", 1)
                    return (("v", vv),)
                proj(1 * D * DT, rw, N, DT, DT, epi_o, banks=(0, 1, 2, 3, 6, 7))
                yb.ready = (("v", prog.cnt["v"]),)
                h0.guard = yb.ready
                rw.guard = (("pe", prog.cnt["pe"]),)
                prog.need("gpsimd", yb.ready)
                st_y = prog.dma("gpsimd", "st",
                                spv("sp_y")[:, :, col0:col0 + N], yb.v3())

                prog.need("vector", t_mer(yb.ready, ybf.guard))
                gv = prog.inc("vector", lambda e, o=ybf.all(), i=yb.all():
                              e.tensor_copy(out=o, in_=i), "v", 1)
                ybf.ready = (("v", gv),)
                ln_g(yb, ybf, sqb, th, N, mu_sb, rs_sb)
                yb.guard = t_mer(th.ready, st_y)
                prog.need("vector", t_mer(th.ready, xn2.guard, pre_ld))
                th3 = th.v3()
                prog.emit("vector", lambda e, o=th3, g=g2b: e.tensor_tensor(
                    out=o, in0=o, in1=g, op=ALU.mult))
                vv = prog.inc("vector", lambda e, o=xn2.v3(), i=th3, bb_=b2b:
                              e.tensor_tensor(out=o, in0=i, in1=bb_,
                                              op=ALU.add), "v", 1)
                xn2.ready = (("v", vv),)
                th.guard = xn2.ready
                prog.need("gpsimd", xn2.ready)
                st_x = prog.dma("gpsimd", "st",
                                spv("sp_xn2")[:, :, col0:col0 + N], xn2.v3())

                def epi_rc(ep, bk, pev, lo, w):
                    prog.need("scalar", t_mer((("pe", pev),), r1.guard))
                    sv = prog.inc("scalar", lambda eng, bk=bk, o=r1.rg(lo, w),
                                  w=w: eng.activation(out=o,
                                                      in_=pb[bk][:, 0:w],
                                                      func=AF.Sigmoid),
                                  "s", 1)
                    return (("s", sv),)
                proj(2 * D * DT, xn2, N, DT, DT, epi_rc, banks=(0, 1, 2, 3, 6, 7))
                r1.ready = (("s", prog.cnt["s"]),)
                prog.need("gpsimd", r1.ready)
                st_r = prog.dma("gpsimd", "st",
                                spv("sp_rcm")[:, :, col0:col0 + N], r1.v3())
                r1.guard = st_r
                xn2.guard = t_mer(st_x, (("pe", prog.cnt["pe"]),))
                xn1.guard = (("pe", prog.cnt["pe"]),)

            for i in range(C.NTT + 1):
                if i < C.NTT:
                    front(i)
                if i >= 1:
                    back(i - 1)

        pass_b()
        prog.barrier()
        alloc_f.reset()
        alloc_b.reset()

        def pass_ckey():
            _wt.append(load_w("WkeyT", 0, D, F))
            wready[0] = wdone()
            xn2 = [alloc_b.take(DT, N), alloc_b.take(DT, N)]
            _kc = alloc_b.take(FT, N)
            kcm2 = [_kc, _kc]
            tbs = alloc_f.take(1, 2 * N)

            def front(i):
                b, j = divmod(i, C.TPB)
                col0 = b * C.SLOC + j * C.TCA
                xn = xn2[i % 2]
                prog.need("sync", xn.guard)
                lv = prog.dma("sync", "ld", xn.v3(),
                              spv("sp_xn2")[:, :, col0:col0 + N])
                xn.ready = lv

            def back(i):
                b, j = divmod(i, C.TPB)
                col0 = b * C.SLOC + j * C.TCA
                xn = xn2[i % 2]
                kc = kcm2[i % 2]

                def epi_key(ep, bk, pev, lo, w, kc=kc):
                    prog.need("scalar", t_mer((("pe", pev),), tbs.guard,
                                              kc.guard))
                    prog.emit("scalar", lambda eng, bk=bk, o=tbs.rg(0, w),
                              w=w: eng.activation(out=o, in_=pb[bk][:, 0:w],
                                                  func=AF.Relu))
                    sv = prog.inc("scalar", lambda eng, kc=kc, o=kc.rg(lo, w),
                                  i_=tbs.rg(0, w): eng.activation(
                                      out=o, in_=i_, func=AF.Square),
                                  "s", 1)
                    tbs.guard = (("s", sv),)
                    return (("s", sv),)
                proj(0, xn, N, FT, DT, epi_key, banks=(0, 1, 2, 3, 6, 7),
                     we_stride=F)
                kc.ready = (("s", prog.cnt["s"]),)
                xn.guard = (("pe", prog.cnt["pe"]),)
                prog.need("gpsimd", kc.ready)
                stv = prog.dma("gpsimd", "st",
                               spv("sp_kcm")[:, :, col0:col0 + N], kc.v3())
                kc.guard = stv

            for i in range(C.NTT + 1):
                if i < C.NTT:
                    front(i)
                if i >= 1:
                    back(i - 1)

        pass_ckey()
        prog.barrier()
        alloc_f.reset()
        alloc_b.reset()

        def pass_cval():
            _wt.append(load_w("WvalT", 0, F, D))
            wready[0] = wdone()
            kcm = [None, None]
            kcm[0] = alloc_b.take(FT, N)
            kcm[1] = kcm[0]
            x1bf = alloc_b.take(DT, N)
            sqb = alloc_b.take(DT, N)
            x0f = [alloc_f.take(DT, N), alloc_f.take(DT, N)]
            yf = [alloc_f.take(DT, N), alloc_f.take(DT, N)]
            rcm = alloc_f.take(DT, N)
            th = alloc_f.take(DT, N)
            x1 = alloc_f.take(DT, N)
            hf1 = alloc_f.take(DT, N)
            mu_sb = alloc_f.take(1, N)
            rs_sb = alloc_f.take(1, N)
            b1pub = 0

            def front(i):
                b, j = divmod(i, C.TPB)
                col0 = b * C.SLOC + j * C.TCA
                prog.need("sync", x0f[i % 2].guard)
                lv = prog.dma("sync", "ld", x0f[i % 2].v3(),
                              spv("x0")[:, :, col0:col0 + N])
                x0f[i % 2].ready = lv
                prog.need("sync", yf[i % 2].guard)
                lv = prog.dma("sync", "ld", yf[i % 2].v3(),
                              spv("sp_y")[:, :, col0:col0 + N])
                yf[i % 2].ready = lv

            def back(i):
                nonlocal b1pub
                b, j = divmod(i, C.TPB)
                col0 = b * C.SLOC + j * C.TCA
                kc = kcm[i % 2]
                q_eng = "sync" if i % 2 == 0 else "scalar"
                q_grp = "ld" if i % 2 == 0 else "sld"
                prog.need(q_eng, kc.guard)
                lvk = prog.dma(q_eng, q_grp, kc.v3(),
                               spv("sp_kcm")[:, :, col0:col0 + N])
                kc.ready = lvk
                prog.need("sync", rcm.guard)
                lvr = prog.dma("sync", "ld", rcm.v3(),
                               spv("sp_rcm")[:, :, col0:col0 + N])
                rcm.ready = lvr

                def epi_val(ep, bk, pev, lo, w):
                    prog.need("vector", t_mer((("pe", pev),), rcm.ready,
                                              th.guard))
                    vv = prog.inc("vector", lambda eng, bk=bk, o=th.rg(lo, w),
                                  i_=rcm.rg(lo, w), w=w:
                                  eng.tensor_tensor(out=o,
                                                    in0=pb[bk][:, 0:w],
                                                    in1=i_, op=ALU.mult),
                                  "v", 1)
                    return (("v", vv),)
                proj(0, kc, N, DT, FT, epi_val, banks=(0, 1, 2, 3, 6, 7),
                     we_stride=D)
                th.ready = (("v", prog.cnt["v"]),)
                kc.guard = (("pe", prog.cnt["pe"]),)
                rcm.guard = th.ready
                prog.need("vector", t_mer(th.ready, yf[i % 2].ready,
                                          x0f[i % 2].ready, x1.guard))
                prog.emit("vector", lambda e, o=x1.all(), a=th.all(),
                          bb_=yf[i % 2].all(): e.tensor_add(out=o, in0=a,
                                                            in1=bb_))
                gv = prog.inc("vector", lambda e, o=x1.all(),
                              a=x0f[i % 2].all(): e.tensor_add(
                                  out=o, in0=o, in1=a), "v", 1)
                x1.ready = (("v", gv),)
                th.guard = yf[i % 2].guard = x0f[i % 2].guard = x1.ready
                prog.need("gpsimd", x1.ready)
                st_x1 = prog.dma("gpsimd", "st",
                                 spv("sp_x1")[:, :, col0:col0 + N], x1.v3())
                prog.need("vector", t_mer(x1.ready, x1bf.guard))
                gv = prog.inc("vector", lambda e, o=x1bf.all(), i_=x1.all():
                              e.tensor_copy(out=o, in_=i_), "v", 1)
                x1bf.ready = (("v", gv),)
                ln_g(x1, x1bf, sqb, hf1, N, mu_sb, rs_sb)
                x1.guard = t_mer(hf1.ready, st_x1)
                if j == C.TPB - 1:
                    prog.need("vector", hf1.ready)
                    b3 = M("bnd1", 0, 32).rearrange(
                        "p (d b) -> p d b", d=DT)[:, :, b:b + 1]
                    b1pub = prog.inc("vector", lambda e, o=b3: e.tensor_copy(
                        out=o, in_=hf1.v3()[:, :, N - 1:N]), "v", 1)
                prog.need("gpsimd", hf1.ready)
                stv = prog.dma("gpsimd", "st",
                               spv("sp_hf1")[:, :, col0:col0 + N], hf1.v3())
                hf1.guard = stv

            for i in range(C.NTT + 1):
                if i < C.NTT:
                    front(i)
                if i >= 1:
                    back(i - 1)
            return b1pub

        b1pub = pass_cval()
        def boundary_collective(bnd_name, loc, all_, gname, pub_edge):
            prog.need("sync", (("v", pub_edge),))
            stv = prog.dma("sync", "hst", ap[loc], M(bnd_name, 0, 32))
            prog.need("gpsimd", stv)
            ccv = prog.inc("gpsimd", lambda e: e.collective_compute(
                "AllGather", ALU.bypass, replica_groups=[list(range(NCORES))],
                ins=[ap[loc]], outs=[ap[all_]]), "cc", 1)
            prog.wait("sync", "cc", ccv)
            lv = prog.dma("sync", "ld", M(gname, 0, 8 * 32),
                          ap[all_].rearrange("(j p) c -> p j c", p=128))
            prog.need("vector", t_mer(pre_ld, lv))
            last = None
            for jj in range(8):
                src = M(gname, jj * 32, 32)
                if jj == 0:
                    bld = lambda e, src=src, bn=bnd_name: e.tensor_scalar(
                        out=M(bn, 0, 32), in0=src, scalar1=M("bmask", 0, 1),
                        scalar2=None, op0=ALU.mult)
                else:
                    bld = (lambda e, src=src, jj=jj, bn=bnd_name:
                           e.scalar_tensor_tensor(
                               out=M(bn, 0, 32), in0=src,
                               scalar=M("bmask", jj, 1), in1=M(bn, 0, 32),
                               op0=ALU.mult, op1=ALU.add))
                prog.emit("vector", bld)
            return prog.cnt["v"]

        bnd1_edge = boundary_collective("bnd1", "B1_loc", "B1_all", "B1g",
                                        b1pub)
        prog.barrier()
        alloc_f.reset()
        alloc_b.reset()

        def ffn_pass(src_h, src_x, dst_x, dst_h, bnd_name, bnd_edge, is_final):
            NP = N + 1
            hf = [alloc_f.take(DT, NP), alloc_f.take(DT, NP)]
            tsub = alloc_f.take(DT, N)
            xf = alloc_f.take(DT, N)
            rf = alloc_f.take(DT, N)
            of = alloc_f.take(DT, N)
            x2 = alloc_f.take(DT, N)
            hb = None if is_final else alloc_f.take(DT, N)
            fsc = alloc_f.take(DT, N)
            mu_sb = alloc_f.take(1, N)
            rs_sb = alloc_f.take(1, N)
            xr = [alloc_b.take(DT, N), alloc_b.take(DT, N)]
            xk = [alloc_b.take(DT, N), alloc_b.take(DT, N)]
            kf = alloc_b.take(DT, N)
            hpb = None if is_final else alloc_b.take(DT, N)
            tbs2 = alloc_f.take(1, 2 * N)
            murb = M("mur", 0, DT).rearrange("p t -> p t ()").broadcast_to(
                [128, DT, N])
            mukb = M("muk", 0, DT).rearrange("p t -> p t ()").broadcast_to(
                [128, DT, N])

            def front(i):
                b, j = divmod(i, C.TPB)
                col0 = b * C.SLOC + j * C.TCA
                hfi = hf[i % 2]
                prog.need("sync", hfi.guard)
                if j > 0:
                    lvh = prog.dma("sync", "ld", hfi.v3(),
                                   spv(src_h)[:, :, col0 - 1:col0 + N])
                    hfi.ready = lvh
                else:
                    lvh = prog.dma("sync", "ld",
                                   hfi.v3()[:, :, 1:NP],
                                   spv(src_h)[:, :, col0:col0 + N])
                    prog.need("vector", t_mer(lvh, (("v", bnd_edge),),
                                              hfi.guard))
                    bsl = M(bnd_name, 0, 32).rearrange(
                        "p (d b) -> p d b", d=DT)[:, :, b:b + 1]
                    vvb = prog.inc("vector", lambda e, o=hfi.v3()[:, :, 0:1],
                                   i_=bsl: e.tensor_copy(out=o, in_=i_),
                                   "v", 1)
                    hfi.ready = t_mer((("v", vvb),), lvh)
                def hsl3(off):
                    return hfi.all().rearrange(
                        "p (t n) -> p t n", t=DT)[:, :, off:off + N]
                prog.need("vector", t_mer(hfi.ready, tsub.guard, pre_ld,
                                          xr[i % 2].guard, xk[i % 2].guard))
                prog.emit("vector", lambda e, o=tsub.v3(), a=hsl3(1),
                          bb_=hsl3(0): e.tensor_tensor(out=o, in0=a, in1=bb_,
                                                       op=ALU.subtract))
                prog.emit("vector", lambda e, o=fsc.v3(), a=tsub.v3(), m=murb:
                          e.tensor_tensor(out=o, in0=a, in1=m, op=ALU.mult))
                prog.emit("vector", lambda e, o=xr[i % 2].v3(), a=fsc.v3(),
                          bb_=hsl3(0): e.tensor_tensor(out=o, in0=a, in1=bb_,
                                                       op=ALU.add))
                prog.emit("vector", lambda e, o=fsc.v3(), a=tsub.v3(), m=mukb:
                          e.tensor_tensor(out=o, in0=a, in1=m, op=ALU.mult))
                vv = prog.inc("vector", lambda e, o=xk[i % 2].v3(),
                              a=fsc.v3(), bb_=hsl3(0): e.tensor_tensor(
                                  out=o, in0=a, in1=bb_, op=ALU.add), "v", 1)
                xr[i % 2].ready = xk[i % 2].ready = (("v", vv),)
                tsub.guard = fsc.guard = (("v", vv),)
                hfi.guard = (("v", vv),)

            def back(i):
                b, j = divmod(i, C.TPB)
                col0 = b * C.SLOC + j * C.TCA
                prog.need("sync", xf.guard)
                lvx = prog.dma("sync", "ld", xf.v3(),
                               spv(src_x)[:, :, col0:col0 + N])
                xf.ready = lvx

                def epi_rf(ep, bk, pev, lo, w):
                    prog.need("scalar", t_mer((("pe", pev),), rf.guard))
                    sv = prog.inc("scalar",
                                  lambda eng, bk=bk, o=rf.rg(lo, w), w=w:
                                  eng.activation(
                                      out=o, in_=pb[bk][:, 0:w],
                                      func=AF.Sigmoid), "s", 1)
                    return (("s", sv),)
                proj(0 * D * DT, xr[i % 2], N, DT, DT, epi_rf,
                     banks=(0, 1, 2, 3, 6, 7))
                rf.ready = (("s", prog.cnt["s"]),)
                xr[i % 2].guard = (("pe", prog.cnt["pe"]),)

                def epi_kf(ep, bk, pev, lo, w):
                    prog.need("vector", t_mer((("pe", pev),), tbs2.guard))
                    vv = prog.inc("vector", lambda eng, bk=bk, o=tbs2.rg(0, w),
                                  w=w: eng.tensor_scalar_max(
                                      out=o, in0=pb[bk][:, 0:w],
                                      scalar1=0.0), "v", 1)
                    prog.need("scalar", t_mer((("v", vv),), kf.guard))
                    sv = prog.inc("scalar", lambda eng, o=kf.rg(lo, w),
                                  i_=tbs2.rg(0, w): eng.activation(
                                      out=o, in_=i_, func=AF.Square), "s", 1)
                    tbs2.guard = (("s", sv),)
                    return (("s", sv),)
                proj(1 * D * DT, xk[i % 2], N, DT, DT, epi_kf,
                     banks=(0, 1, 2, 3, 6, 7))
                kf.ready = (("s", prog.cnt["s"]),)
                xk[i % 2].guard = (("pe", prog.cnt["pe"]),)

                def epi_vf(ep, bk, pev, lo, w):
                    prog.need("vector", t_mer((("pe", pev),), rf.ready,
                                              of.guard))
                    vv = prog.inc("vector",
                                  lambda eng, bk=bk, o=of.rg(lo, w),
                                  i_=rf.rg(lo, w), w=w: eng.tensor_tensor(
                                      out=o, in0=pb[bk][:, 0:w],
                                      in1=i_, op=ALU.mult), "v", 1)
                    return (("v", vv),)
                proj(2 * D * DT, kf, N, DT, DT, epi_vf, banks=(0, 1, 2, 3, 6, 7))
                of.ready = (("v", prog.cnt["v"]),)
                rf.guard = of.ready
                kf.guard = (("pe", prog.cnt["pe"]),)

                prog.need("vector", t_mer(of.ready, xf.ready, x2.guard))
                gv = prog.inc("vector", lambda e, o=x2.all(), a=of.all(),
                              bb_=xf.all(): e.tensor_add(out=o, in0=a,
                                                         in1=bb_), "v", 1)
                x2.ready = (("v", gv),)
                of.guard = xf.guard = x2.ready
                prog.need("gpsimd", x2.ready)
                stv = prog.dma("gpsimd", "st", spv(dst_x)[:, :, col0:col0 + N],
                               x2.v3())
                if is_final:
                    x2.guard = stv
                    return

                x2bf = xr[i % 2]
                sqb2 = xk[i % 2]
                prog.need("vector", t_mer(x2.ready, x2bf.guard))
                gv = prog.inc("vector", lambda e, o=x2bf.all(), i_=x2.all():
                              e.tensor_copy(out=o, in_=i_), "v", 1)
                x2bf.ready = (("v", gv),)
                ln_g(x2, x2bf, sqb2, hb, N, mu_sb, rs_sb)
                x2.guard = t_mer(hb.ready, stv)
                prog.need("gpsimd", t_mer(hb.ready, hpb.guard))
                peb = M("pe", 0, DT * B).rearrange(
                    "p (t b) -> p t b", t=DT)[:, :, b:b + 1].broadcast_to(
                    [128, DT, N])
                gv = prog.inc("gpsimd", lambda e, o=hpb.v3(), a=hb.v3(),
                              p_=peb: e.tensor_tensor(out=o, in0=a, in1=p_,
                                                      op=ALU.add), "g", 1)
                hpb.ready = (("g", gv),)
                hb.guard = hpb.ready
                prog.need("gpsimd", hpb.ready)
                stv = prog.dma("gpsimd", "st", spv(dst_h)[:, :, col0:col0 + N],
                               hpb.v3())
                hpb.guard = stv

            for i in range(C.NTT + 1):
                if i < C.NTT:
                    front(i)
                if i >= 1:
                    back(i - 1)

        _wt.append(load_w("fWrT", 0 * D * DT, D, D))
        _wt.append(load_w("fWkT", 1 * D * DT, D, D))
        _wt.append(load_w("fWvT", 2 * D * DT, D, D))
        wready[0] = wdone()
        ffn_pass("sp_hf1", "sp_x1", "sp_x2", "sp_hp", "bnd1", bnd1_edge, False)
        prog.barrier()
        alloc_f.reset()
        alloc_b.reset()

        def pass_e():
            _wt.append(load_w("bWqT", 0 * D * DT, D, D))
            _wt.append(load_w("bWkT", 1 * D * DT, D, D))
            _wt.append(load_w("bWvT", 2 * D * DT, D, D))
            _wt.append(load_w("bWoT", 3 * D * DT, D, D))
            wready[0] = wdone()

            SE = C.SE
            NE = 4 * SE
            N1 = C.TCA
            NT1 = C.TOK // N1
            qb1 = alloc_f.take(DT, N1)
            esc = alloc_f.take(DT, N1)
            hp1 = [alloc_b.take(DT, N1), alloc_b.take(DT, N1)]
            oub = [alloc_b.take(DT, N1), alloc_b.take(DT, N1)]
            e1_sp = []

            for t in range(NT1):
                col0 = t * N1
                hpb = hp1[t % 2]
                prog.need("sync", hpb.guard)
                hpb.ready = prog.dma("sync", "ld", hpb.v3(),
                                     spv("sp_hp")[:, :, col0:col0 + N1])
                for wi, wof, bias1, dsp in ((0, 0 * D * DT, "bq1", "sp_pq"),
                                            (1, 1 * D * DT, "bk1", "sp_pk"),
                                            (2, 2 * D * DT, None, "sp_pv")):
                    ob = oub[(t * 3 + wi) % 2]
                    if bias1 is None:
                        def epi(ep, bk, pev, lo, w, ob=ob):
                            prog.need("vector", t_mer((("pe", pev),), ob.guard,
                                                      pre_ld))
                            vv = prog.inc("vector", lambda eng, bk=bk, ob=ob,
                                          w=w, lo=lo, s1=M("bv", ep, 1):
                                          eng.tensor_scalar(
                                              out=ob.rg(lo, w),
                                              in0=pb[bk][:, 0:w], scalar1=s1,
                                              scalar2=None, op0=ALU.add),
                                          "v", 1)
                            return (("v", vv),)
                        proj(wof, hpb, N1, DT, DT, epi, banks=(0, 1, 2, 3),
                             single=True)
                        ob.ready = (("v", prog.cnt["v"]),)
                    else:
                        def epi(ep, bk, pev, lo, w, b1=bias1):
                            prog.need("vector", t_mer((("pe", pev),),
                                                      qb1.guard, pre_v))
                            vv = prog.inc("vector", lambda eng, bk=bk, w=w,
                                          lo=lo, s1=M(b1, ep, 1):
                                          eng.tensor_scalar(
                                              out=qb1.rg(lo, w),
                                              in0=pb[bk][:, 0:w], scalar1=s1,
                                              scalar2=None, op0=ALU.add),
                                          "v", 1)
                            return (("v", vv),)
                        proj(wof, hpb, N1, DT, DT, epi, banks=(0, 1, 2, 3),
                             single=True)
                        prog.need("vector", t_mer((("v", prog.cnt["v"]),),
                                                  esc.guard))
                        vv = prog.inc("vector", lambda e: e.tensor_scalar(
                            out=esc.all(), in0=qb1.all(), scalar1=1.0,
                            scalar2=None, op0=ALU.min), "v", 1)
                        prog.wait("scalar", "v", vv)
                        sv = prog.inc("scalar", lambda e: e.activation(
                            out=esc.all(), in_=esc.all(), func=AF.Exp,
                            bias=-1.0), "s", 1)
                        prog.need("vector", t_mer((("s", sv),), ob.guard))
                        vv = prog.inc("vector", lambda e, ob=ob:
                                      e.tensor_tensor(out=ob.all(),
                                                      in0=qb1.all(),
                                                      in1=esc.all(),
                                                      op=ALU.max), "v", 1)
                        ob.ready = (("v", vv),)
                        qb1.guard = ob.ready
                        esc.guard = ob.ready
                    prog.need("gpsimd", ob.ready)
                    stv = prog.dma("gpsimd", "st",
                                   spv(dsp)[:, :, col0:col0 + N1], ob.v3())
                    ob.guard = stv
                    e1_sp.append(stv)
                hpb.guard = (("pe", prog.cnt["pe"]),)

            e1done = t_mer((("pe", prog.cnt["pe"]),),
                           (("v", prog.cnt["v"]),),
                           (("s", prog.cnt["s"]),), *e1_sp)

            alloc_f.reset()
            alloc_b.reset()
            pq2 = [Buf(WP, 0 * DT * NE, DT, NE), Buf(WP, 1 * DT * NE, DT, NE)]
            pk2 = [Buf(WP, 2 * DT * NE, DT, NE), Buf(WP, 3 * DT * NE, DT, NE)]
            pv2 = [Buf(WP, 4 * DT * NE, DT, NE), Buf(WP, 5 * DT * NE, DT, NE)]
            numa2 = [alloc_f.take(DT, NE), alloc_f.take(DT, NE)]
            tmpE = alloc_f.take(DT, NE)
            tm = alloc_f.take(DT, NE)
            x2f = alloc_f.take(DT, NE)
            d32 = alloc_f.take(1, 2 * SE)
            attb = alloc_b.take(DT, NE)
            zb = alloc_b.take(DT, SE)
            dbig = alloc_b.take(DT, B * SE)
            x3bf = alloc_b.take(DT, NE)
            for bf in (*pq2, *pk2, *pv2, *numa2, tmpE, tm, x2f, d32, attb, zb,
                       dbig, x3bf):
                bf.guard = e1done
            rowsg = [()]
            b3cell = [0]
            d32g = [e1done, e1done]
            d32r = [(), ()]

            def dbsl(h, bp):
                base = dbig.off + (h * B + bp) * SE
                return dbig.t[:, base:base + SE]

            def front_head(i):
                s0 = i * SE
                par = i % 2
                pq, pk, vf = pq2[par], pk2[par], pv2[par]
                for bf, nmsp in ((pq, "sp_pq"), (pk, "sp_pk"), (vf, "sp_pv")):
                    prog.need("sync", bf.guard)
                    srcv = spv(nmsp).rearrange("p t (b n) -> p t b n", b=B)
                    _t = []
                    for bb in range(B):
                        _t.append(prog.dma("sync", "ld",
                                           bf.v4(B)[:, :, bb, :],
                                           srcv[:, :, bb, s0:s0 + SE]))
                    bf.ready = t_mer(*_t)

                def pkb(bb):
                    return pk.v4(B)[:, :, bb, :]
                prog.need("vector", t_mer(pk.ready, zb.guard))
                prog.emit("vector", lambda e, o=zb.v3(), a=pkb(0), b_=pkb(1):
                          e.tensor_add(out=o, in0=a, in1=b_))
                prog.emit("vector", lambda e, o=zb.v3(), a=pkb(2):
                          e.tensor_add(out=o, in0=o, in1=a))
                vv = prog.inc("vector", lambda e, o=zb.v3(), a=pkb(3):
                              e.tensor_add(out=o, in0=o, in1=a), "v", 1)
                zb.ready = (("v", vv),)

                zbc = zb.all().rearrange("p (t o n) -> p t o n", t=DT,
                                         o=1).broadcast_to([128, DT, B, SE])
                prog.need("vector", t_mer(pq.ready, zb.ready, dbig.guard))
                vv = prog.inc("vector", lambda e,
                              o=dbig.all().rearrange("p (t b n) -> p t b n",
                                                     t=DT, b=B),
                              a=pq.v4(B), z=zbc:
                              e.tensor_tensor(out=o, in0=a, in1=z,
                                              op=ALU.mult), "v", 1)
                dbig.ready = (("v", vv),)
                prog.need("tensor", t_mer(dbig.ready, bankfree.get(7, ())))
                pev_den = 0
                for bb in range(B):
                    for h in range(DT):
                        row = h * 4 + bb
                        last_mm = (bb == B - 1 and h == DT - 1)
                        bld = lambda e, row=row, r_=dbsl(h, bb), lm=last_mm, \
                            fd=(bb == 0 and h == 0): e.matmul(
                                pb[7][0:32, 0:SE], OH2[:, 128 - row:160 - row],
                                r_, start=fd, stop=lm)
                        if last_mm:
                            pev_den = prog.inc("tensor", bld, "pe", 1)
                        else:
                            prog.emit("tensor", bld)
                dbig.guard = (("pe", pev_den),)
                prog.wait("vector", "pe", pev_den)
                prog.need("vector", d32g[par])
                prog.emit("vector", lambda e, o=pb[7][0:32, SE:2 * SE]:
                          e.tensor_scalar(out=o, in0=pb[7][0:32, 0:SE],
                                          scalar1=ATT_EPS, scalar2=None,
                                          op0=ALU.add))
                vv = prog.inc("vector", lambda e,
                              o=d32.t[0:32, d32.off + par * SE:
                                      d32.off + (par + 1) * SE]:
                              e.reciprocal(out=o, in_=pb[7][0:32, SE:2 * SE]),
                              "v", 1)
                d32r[par] = (("v", vv),)
                bankfree[7] = (("v", vv),)

            def front_tail(i):
                par = i % 2
                pq, pk, vf, nm = pq2[par], pk2[par], pv2[par], numa2[par]

                def pqb(bb):
                    return pq.v4(B)[:, :, bb, :]
                def vfb4(bb, hlo):
                    return vf.all().rearrange(
                        "p (t b n) -> p t b n", t=DT, b=B)[:, hlo:hlo + 4, bb, :]
                def nm4(bb, hlo):
                    return nm.all().rearrange(
                        "p (t b n) -> p t b n", t=DT, b=B)[:, hlo:hlo + 4, bb, :]
                def tm4(bb, hlo):
                    return tm.all().rearrange(
                        "p (t b n) -> p t b n", t=DT, b=B)[:, hlo:hlo + 4, bb, :]

                for bb in range(B):
                    pqc = pq.v4(B)[:, :, bb:bb + 1, :].broadcast_to(
                        [128, DT, B, SE])
                    prog.need("vector", t_mer(pq.ready, pk.ready, dbig.guard))
                    vv = prog.inc("vector", lambda e,
                                  o=dbig.all().rearrange(
                                      "p (t b n) -> p t b n", t=DT, b=B),
                                  a=pqc, b_=pk.v4(B):
                                  e.tensor_tensor(out=o, in0=a, in1=b_,
                                                  op=ALU.mult), "v", 1)
                    dbig.ready = (("v", vv),)
                    pevA = 0
                    for bp in range(B):
                        bkA = (2, 3) if bp % 2 == 0 else (4, 5)
                        prog.need("tensor", t_mer(dbig.ready,
                                                  bankfree.get(bkA[0], ()),
                                                  bankfree.get(bkA[1], ())))
                        for h in range(DT):
                            bnk = bkA[0] if h < 4 else bkA[1]
                            cof = (h % 4) * SE
                            bld = lambda e, bnk=bnk, cof=cof, r_=dbsl(h, bp): \
                                e.matmul(pb[bnk][:, cof:cof + SE], allones,
                                         r_, start=True, stop=True)
                            if h == DT - 1:
                                pevA = prog.inc("tensor", bld, "pe", 1)
                            else:
                                prog.emit("tensor", bld)
                        prog.need("vector", t_mer((("pe", pevA),), vf.ready,
                                                  nm.guard if bp == 0 else (),
                                                  tm.guard if bp > 0 else ()))
                        for half in range(2):
                            bnk = bkA[half]
                            ps4 = pb[bnk][:, 0:4 * SE].rearrange(
                                "p (h n) -> p h n", h=4)
                            if bp == 0:
                                bld = (lambda e, o=nm4(bb, half * 4),
                                       i=vfb4(0, half * 4), ps4=ps4:
                                       e.tensor_mul(out=o, in0=i, in1=ps4))
                            else:
                                bld = (lambda e, o=tm4(bb, half * 4),
                                       i=vfb4(bp, half * 4), ps4=ps4:
                                       e.tensor_mul(out=o, in0=i, in1=ps4))
                            if half == 1:
                                vv = prog.inc("vector", bld, "v", 1)
                            else:
                                prog.emit("vector", bld)
                        bankfree[bkA[0]] = (("v", vv),)
                        bankfree[bkA[1]] = (("v", vv),)
                        if bp > 0:
                            prog.wait("gpsimd", "v", vv)
                            gv = prog.inc("gpsimd", lambda e, o=nm4(bb, 0),
                                          i=tm4(bb, 0): e.tensor_add(
                                              out=o, in0=o, in1=i), "g", 1)
                            gv = prog.inc("gpsimd", lambda e, o=nm4(bb, 4),
                                          i=tm4(bb, 4): e.tensor_add(
                                              out=o, in0=o, in1=i), "g", 1)
                            tm.guard = (("g", gv),)
                    dbig.guard = (("pe", pevA),)
                nm.ready = (("g", prog.cnt["g"]), ("v", prog.cnt["v"]))
                pq.guard = (("v", prog.cnt["v"]),)
                pk.guard = (("v", prog.cnt["v"]),)
                vf.guard = (("v", prog.cnt["v"]),)

            def back(i):
                s0 = i * SE
                par = i % 2
                nm = numa2[par]
                def nm4(bb, hlo):
                    return nm.all().rearrange(
                        "p (t b n) -> p t b n", t=DT, b=B)[:, hlo:hlo + 4, bb, :]
                prog.need("sync", x2f.guard)
                srcv = spv("sp_x2").rearrange("p t (b n) -> p t b n", b=B)
                _t = []
                for bb in range(B):
                    _t.append(prog.dma("sync", "ld", x2f.v4(B)[:, :, bb, :],
                                       srcv[:, :, bb, s0:s0 + SE]))
                x2f.ready = t_mer(*_t)
                prog.need("gpsimd", t_mer(d32r[par], rowsg[0]))
                rden_edge = prog.dma(
                    "gpsimd", "gld",
                    ROWS[0:1, rws["rden"]:rws["rden"] + 32 * SE]
                    .rearrange("o (r n) -> o r n", r=32),
                    d32.t[0:32, d32.off + par * SE:d32.off + (par + 1) * SE])
                d32g[par] = rden_edge

                RD = ROWS[0:1, rws["rden"]:rws["rden"] + 32 * SE].rearrange(
                    "o (h x n) -> o h x n", h=8, x=4)
                pevR = 0
                for bb in range(B):
                    bkR = (6, 7) if bb % 2 == 0 else (0, 1)
                    prog.need("tensor", t_mer(rden_edge,
                                              bankfree.get(bkR[0], ()),
                                              bankfree.get(bkR[1], ())))
                    for half in range(2):
                        bnk = bkR[half]
                        rhs = RD[:, half * 4:half * 4 + 4, bb, :]
                        bld = lambda e, bnk=bnk, rhs=rhs: e.matmul(
                            pb[bnk][:, 0:4 * SE], R("onesf", 0, 128), rhs,
                            start=True, stop=True)
                        if half == 1:
                            pevR = prog.inc("tensor", bld, "pe", 1)
                        else:
                            prog.emit("tensor", bld)
                    prog.need("vector", t_mer((("pe", pevR),), nm.ready,
                                              attb.guard))
                    for half in range(2):
                        bnk = bkR[half]
                        ps4 = pb[bnk][:, 0:4 * SE].rearrange("p (h n) -> p h n",
                                                             h=4)
                        ab4 = attb.all().rearrange(
                            "p (t b n) -> p t b n",
                            t=DT, b=B)[:, half * 4:half * 4 + 4, bb, :]
                        bld = lambda e, o=ab4, i=nm4(bb, half * 4), ps4=ps4: \
                            e.tensor_mul(out=o, in0=i, in1=ps4)
                        if half == 1:
                            vv = prog.inc("vector", bld, "v", 1)
                        else:
                            prog.emit("vector", bld)
                    bankfree[bkR[0]] = (("v", vv),)
                    bankfree[bkR[1]] = (("v", vv),)
                attb.ready = (("v", prog.cnt["v"]),)
                rowsg[0] = (("pe", pevR),)

                def epi_ob(ep, bk, pev, lo, w):
                    prog.need("vector", t_mer((("pe", pev),), tmpE.guard,
                                              pre_ld))
                    vv = prog.inc("vector", lambda eng, bk=bk, w=w, lo=lo,
                                  s1=M("bo", ep, 1): eng.tensor_scalar(
                                      out=tmpE.rg(lo, w), in0=pb[bk][:, 0:w],
                                      scalar1=s1, scalar2=None, op0=ALU.add),
                                  "v", 1)
                    return (("v", vv),)
                proj(3 * D * DT, attb, NE, DT, DT, epi_ob, banks=(0, 1),
                     single=True)
                tmpE.ready = (("v", prog.cnt["v"]),)
                attb.guard = (("pe", prog.cnt["pe"]),)
                prog.need("vector", t_mer(tmpE.ready, x2f.ready, attb.ready))
                vv = prog.inc("vector", lambda e, o=nm.all(), a=tmpE.all(),
                              b_=x2f.all(): e.tensor_add(out=o, in0=a, in1=b_),
                              "v", 1)
                nm.ready = (("v", vv),)
                x2f.guard = nm.ready
                prog.need("sync", nm.ready)
                dstv = spv("sp_x3").rearrange("p t (b n) -> p t b n", b=B)
                _t = []
                for bb in range(B):
                    _t.append(prog.dma("sync", "hst",
                                       dstv[:, :, bb, s0:s0 + SE],
                                       nm.v4(B)[:, :, bb, :]))
                stv = t_mer(*_t)

                prog.need("vector", t_mer(nm.ready, x3bf.guard))
                vv = prog.inc("vector", lambda e, o=x3bf.all(), i=nm.all():
                              e.tensor_copy(out=o, in_=i), "v", 1)
                x3bf.ready = (("v", vv),)
                pev = ln_stats(x3bf, attb, NE, x3bf.ready)
                x3bf.guard = (("pe", pev),)
                ln_apply(nm, tmpE, pev, NE)
                nm.guard = t_mer(stv, tmpE.ready)
                if i == C.NTE - 1:
                    prog.need("vector", tmpE.ready)
                    for d in range(DT):
                        for bb in range(B):
                            bld = lambda e, d=d, bb=bb: e.tensor_copy(
                                out=M("bnd3", d * B + bb, 1),
                                in_=tmpE.sl(d, bb * SE + SE - 1, 1))
                            if d == DT - 1 and bb == B - 1:
                                b3cell[0] = prog.inc("vector", bld, "v", 1)
                            else:
                                prog.emit("vector", bld)
                prog.need("sync", tmpE.ready)
                dstv = spv("sp_hf2").rearrange("p t (b n) -> p t b n", b=B)
                _t = []
                for bb in range(B):
                    _t.append(prog.dma("sync", "hst",
                                       dstv[:, :, bb, s0:s0 + SE],
                                       tmpE.v4(B)[:, :, bb, :]))
                tmpE.guard = t_mer(*_t, (("v", b3cell[0]),)
                                   if i == C.NTE - 1 else ())

            for i in range(C.NTE + 1):
                if i < C.NTE:
                    front_head(i)
                if i >= 1:
                    back(i - 1)
                if i < C.NTE:
                    front_tail(i)

            return b3cell[0]
        b3pub = pass_e()
        bnd3_edge = boundary_collective("bnd3", "B3_loc", "B3_all", "B3g",
                                        b3pub)
        prog.barrier()
        alloc_f.reset()
        alloc_b.reset()

        _wt.append(load_w("fWrT", 0 * D * DT, D, D))
        _wt.append(load_w("fWkT", 1 * D * DT, D, D))
        _wt.append(load_w("fWvT", 2 * D * DT, D, D))
        wready[0] = wdone()
        ffn_pass("sp_hf2", "sp_x3", "y_out", None, "bnd3", bnd3_edge, True)

        if taps:
            prog.barrier()
            for nm, tap in taps.items():
                prog.dma("sync", "hst", tap, ap[nm])
        for i in range(4):
            prog.wait("sync", f"st{i}", prog.cnt[f"st{i}"])
        for i in range(2):
            prog.wait("sync", f"hst{i}", prog.cnt[f"hst{i}"])
        for i in range(4):
            prog.wait("sync", f"ld{i}", prog.cnt[f"ld{i}"])
        for i in range(2):
            prog.wait("sync", f"gld{i}", prog.cnt[f"gld{i}"])
            prog.wait("sync", f"sld{i}", prog.cnt[f"sld{i}"])

        with nc.Block() as block:
            @block.sync
            def _(eng):
                for f in prog.items["sync"]:
                    f(eng)

            @block.scalar
            def _(eng):
                for f in prog.items["scalar"]:
                    f(eng)

            @block.vector
            def _(eng):
                for f in prog.items["vector"]:
                    f(eng)

            @block.tensor
            def _(eng):
                for f in prog.items["tensor"]:
                    f(eng)

            @block.gpsimd
            def _(eng):
                for f in prog.items["gpsimd"]:
                    f(eng)

    return nc



def make_inputs(C, core, **inp):
    s_lo = core * C.SLOC
    s_hi = s_lo + C.SLOC
    x = inp["L0_attn_a_x"]
    xs = np.ascontiguousarray(
        x[:, s_lo:s_hi, :].transpose(2, 0, 1)).reshape(D, C.TOK)

    def wt(a):
        return np.ascontiguousarray(np.asarray(a).T)

    def pm(v):
        return np.ascontiguousarray(np.asarray(v).reshape(DT, 128).T)

    pos = np.arange(B, dtype=np.float32)[:, None]
    div = np.exp(np.arange(0, D, 2, dtype=np.float32)
                 * (-np.log(10000.0) / D))
    pe = np.zeros((B, D), dtype=np.float32)
    pe[:, 0::2] = np.sin(pos * div)
    pe[:, 1::2] = np.cos(pos * div)
    pe_pm = np.zeros((128, DT * B), np.float32)
    for t in range(DT):
        for b in range(B):
            pe_pm[:, t * B + b] = pe[b, t * 128:(t + 1) * 128]

    hmask = np.zeros((128, 8), np.float32)
    hexpo = np.zeros((128, 8), np.float32)
    bmask = np.zeros((128, 8), np.float32)
    for j in range(NCORES):
        if j < core:
            hmask[:, j] = 1.0
            hexpo[:, j] = 1024.0 * 0 + float(C.SLOC * (core - 1 - j))
        if j == core - 1:
            bmask[:, j] = 1.0

    m = dict(
        x0=xs,
        WrT=wt(inp["L0_attn_a_Wr"]), WwT=wt(inp["L0_attn_a_Ww"]),
        WkT=wt(inp["L0_attn_a_Wk"]), WvT=wt(inp["L0_attn_a_Wv"]),
        WoT=wt(inp["L0_attn_a_Wo"]), WrecT=wt(inp["L0_attn_a_Wrec"]),
        WkeyT=wt(inp["L0_attn_a_Wkey"]), WvalT=wt(inp["L0_attn_a_Wval"]),
        bWqT=wt(inp["L0_attn_b_Wq"]), bWkT=wt(inp["L0_attn_b_Wk"]),
        bWvT=wt(inp["L0_attn_b_Wv"]), bWoT=wt(inp["L0_attn_b_Wo"]),
        fWrT=wt(inp["L0_ffn_W_r"]), fWkT=wt(inp["L0_ffn_W_k"]),
        fWvT=wt(inp["L0_ffn_W_v"]),
        g1=pm(inp["L0_attn_a_ln1_g"]), b1=pm(inp["L0_attn_a_ln1_b"]),
        g2=pm(inp["L0_attn_a_ln2_g"]), b2=pm(inp["L0_attn_a_ln2_b"]),
        td=pm(inp["L0_attn_a_time_decay"]),
        mur=pm(inp["L0_ffn_mu_r"]), muk=pm(inp["L0_ffn_mu_k"]),
        bq=pm(inp["L0_attn_b_bq"]), bk=pm(inp["L0_attn_b_bk"]),
        bv=pm(inp["L0_attn_b_bv"]), bo=pm(inp["L0_attn_b_bo"]),
        pe=pe_pm, hmask=hmask, hexpo=hexpo, bmask=bmask,
    )
    return {k: np.ascontiguousarray(np.asarray(v, dtype=m[k].dtype
                                               if hasattr(v, "dtype")
                                               else np.float32))
            for k, v in m.items()}


def kernel(**inputs):
    C = CFG(sloc=1024, tca=256, se=128)
    nc = build_program(C)
    in_maps = [make_inputs(C, core, **inputs) for core in range(NCORES)]
    res = run_bass_kernel_spmd(nc, in_maps, core_ids=list(range(NCORES)))
    outs = []
    for core in range(NCORES):
        y = res.results[core]["y_out"]
        outs.append(y.reshape(D, B, C.SLOC).transpose(1, 2, 0))
    return np.concatenate(outs, axis=1).astype(np.float32)


# version 20
# speedup vs baseline: 1.0435x; 1.0355x over previous
import contextlib
import numpy as np
import concourse.bass as bass
import concourse.mybir as mybir
from concourse.bass_utils import run_bass_kernel_spmd

F32 = mybir.dt.float32
BF16 = mybir.dt.bfloat16
AF = mybir.ActivationFunctionType
ALU = mybir.AluOpType

ENGINES = ("sync", "scalar", "vector", "tensor", "gpsimd")
NCORES = 8
D = 1024
DT = 8
F = 4096
FT = 32
B = 4
LN_EPS = 1e-5
WKV_EPS = 1e-8
ATT_EPS = 1e-6


class CFG:
    def __init__(self, sloc=1024, tca=256, se=128, taps=()):
        self.SLOC = sloc
        self.TOK = B * sloc
        self.TCA = tca
        self.SE = se
        self.TPB = sloc // tca
        self.NTT = B * self.TPB
        self.NTE = sloc // se
        self.taps = tuple(taps)
        assert sloc % tca == 0 and sloc % se == 0


class Prog:
    def __init__(self, nc, stack, semnames):
        self.nc = nc
        self.items = {e: [] for e in ENGINES}
        self.sems = {n: stack.enter_context(nc.semaphore(n)) for n in semnames}
        self.cnt = {n: 0 for n in semnames}
        self.waited = {}
        self.nbar = 0

    CHAIN = {"vector": "v", "scalar": "s", "gpsimd": "g"}

    def emit(self, eng, build):
        ce = self.CHAIN.get(eng)
        if ce is not None:
            self.inc(eng, build, ce, 1)
        else:
            self.items[eng].append(build)

    def inc(self, eng, build, sem, k=1, chain=True):
        ce = self.CHAIN.get(eng)
        if ce is not None and chain:
            lo = getattr(self, "last_op", {}).get(eng)
            if lo:
                self.wait(eng, lo[0], lo[1])
        self.cnt[sem] += k
        v = self.cnt[sem]
        s = self.sems[sem]
        def f(e, b=build, s=s, k=k):
            b(e).then_inc(s, k)
        self.items[eng].append(f)
        if ce is not None and chain:
            self.last_op = getattr(self, "last_op", {})
            self.last_op[eng] = (sem, v)
        return v

    def wait(self, eng, sem, v):
        if v <= 0:
            return
        key = (eng, sem)
        if self.waited.get(key, 0) >= v:
            return
        self.waited[key] = v
        s = self.sems[sem]
        self.items[eng].append(lambda e, s=s, v=v: e.wait_ge(s, v))

    def need(self, eng, toks):
        for sem, v in toks:
            self.wait(eng, sem, v)

    def barrier(self):
        for i in range(4):
            self.wait("sync", f"ld{i}", self.cnt[f"ld{i}"])
            self.wait("sync", f"st{i}", self.cnt[f"st{i}"])
        for i in range(2):
            self.wait("gpsimd", f"gld{i}", self.cnt[f"gld{i}"])
            self.wait("sync", f"hst{i}", self.cnt[f"hst{i}"])
        self.nbar += 1
        target = 5 * self.nbar
        s = self.sems["bar"]
        for eng in ENGINES:
            def f(e, s=s):
                e.drain()
                e.sem_inc(s, 1)
            self.items[eng].append(f)
        self.cnt["bar"] = target
        for eng in ENGINES:
            self.waited[(eng, "bar")] = target
            self.items[eng].append(
                lambda e, s=s, t=target: e.wait_ge(s, t))

    GSIZE = {"ld": 4, "st": 4, "gld": 2, "hst": 2}

    def dma(self, q, group, out, in_):
        self.rot = getattr(self, "rot", {})
        i = self.rot[group] = (self.rot.get(group, -1) + 1) % self.GSIZE[group]
        sem = f"{group}{i}"
        self.wait(q, sem, self.cnt[sem])
        v = self.inc(q, lambda e, o=out, i_=in_: e.dma_start(out=o, in_=i_),
                     sem, 16, chain=False)
        return ((sem, v),)


class Buf:
    def __init__(self, t, off, dts, n):
        self.t = t
        self.off = off
        self.dts = dts
        self.n = n
        self.ready = ()
        self.guard = ()

    def sl(self, d, c0=0, n=None):
        n = self.n if n is None else n
        base = self.off + d * self.n + c0
        return self.t[:, base:base + n]

    def all(self):
        return self.t[:, self.off:self.off + self.dts * self.n]

    def rg(self, lo, w):
        return self.t[:, self.off + lo:self.off + lo + w]

    def v3(self):
        return self.all().rearrange("p (t n) -> p t n", t=self.dts)

    def v4(self, b):
        return self.all().rearrange("p (t b n) -> p t b n", t=self.dts, b=b)


class Alloc:
    def __init__(self, tensor, total):
        self.tensor = tensor
        self.total = total
        self.off = 0

    def take(self, dts, n):
        need = dts * n
        assert self.off + need <= self.total, \
            f"SBUF alloc overflow: {self.off}+{need}>{self.total}"
        b = Buf(self.tensor, self.off, dts, n)
        self.off += need
        return b

    def reset(self):
        self.off = 0


def t_mer(*toks_seq):
    out = {}
    for toks in toks_seq:
        for sem, v in toks:
            if out.get(sem, 0) < v:
                out[sem] = v
    return tuple(out.items())


def build_program(C: CFG):
    nc = bass.Bass("TRN2", target_bir_lowering=False)
    TOK = C.TOK
    N = C.TCA

    def din(name, shape, dt=F32):
        return nc.dram_tensor(name, list(shape), dt, kind="ExternalInput").ap()

    ap = {}
    ap["x0"] = din("x0", [D, TOK])
    for nm in ("WrT", "WwT", "WkT", "WvT", "WoT", "WrecT",
               "bWqT", "bWkT", "bWvT", "bWoT", "fWrT", "fWkT", "fWvT"):
        ap[nm] = din(nm, [D, D])
    ap["WkeyT"] = din("WkeyT", [D, F])
    ap["WvalT"] = din("WvalT", [F, D])
    for nm in ("g1", "b1", "g2", "b2", "td", "mur", "muk", "bq", "bk", "bv",
               "bo"):
        ap[nm] = din(nm, [128, DT])
    ap["pe"] = din("pe", [128, DT * B])
    ap["hmask"] = din("hmask", [128, 8])
    ap["hexpo"] = din("hexpo", [128, 8])
    ap["bmask"] = din("bmask", [128, 8])

    ap["y_out"] = nc.dram_tensor("y_out", [D, TOK], F32,
                                 kind="ExternalOutput").ap()

    def spill(name, rows, dt):
        return nc.dram_tensor(name, [rows, TOK], dt, kind="Internal").ap()

    ap["sp_h0"] = spill("sp_h0", D, F32)
    ap["sp_xn1"] = spill("sp_xn1", D, BF16)
    ap["sp_num"] = spill("sp_num", D, F32)
    ap["sp_den"] = spill("sp_den", D, F32)
    ap["sp_y"] = spill("sp_y", D, F32)
    ap["sp_xn2"] = spill("sp_xn2", D, BF16)
    ap["sp_rcm"] = spill("sp_rcm", D, F32)
    ap["sp_kcm"] = spill("sp_kcm", F, BF16)
    ap["sp_x1"] = spill("sp_x1", D, F32)
    ap["sp_hf1"] = spill("sp_hf1", D, F32)
    ap["sp_x2"] = spill("sp_x2", D, F32)
    ap["sp_hp"] = spill("sp_hp", D, BF16)
    ap["sp_x3"] = spill("sp_x3", D, F32)
    ap["sp_hf2"] = spill("sp_hf2", D, F32)
    ap["sp_pq"] = spill("sp_pq", D, BF16)
    ap["sp_pk"] = spill("sp_pk", D, BF16)
    ap["sp_pv"] = spill("sp_pv", D, BF16)

    ap["SC_loc"] = nc.dram_tensor("SC_loc", [128, 64], F32, kind="Internal").ap()
    ap["SC_all"] = nc.dram_tensor("SC_all", [1024, 64], F32, kind="Internal").ap()
    ap["B1_loc"] = nc.dram_tensor("B1_loc", [128, 32], F32, kind="Internal").ap()
    ap["B1_all"] = nc.dram_tensor("B1_all", [1024, 32], F32, kind="Internal").ap()
    ap["B3_loc"] = nc.dram_tensor("B3_loc", [128, 32], F32, kind="Internal").ap()
    ap["B3_all"] = nc.dram_tensor("B3_all", [1024, 32], F32, kind="Internal").ap()

    taps = {}
    for nm in C.taps:
        rows = F if nm == "sp_kcm" else D
        dt = BF16 if nm in ("sp_xn1", "sp_xn2", "sp_kcm", "sp_hp") else F32
        taps[nm] = nc.dram_tensor("tap_" + nm, [rows, TOK], dt,
                                  kind="ExternalOutput").ap()

    def spv(name):
        return ap[name].rearrange("(t p) n -> p t n", p=128)

    with contextlib.ExitStack() as stack:
        E = stack.enter_context
        GENF = E(nc.sbuf_tensor("GENF", [128, 20736], F32))
        GENB = E(nc.sbuf_tensor("GENB", [128, 15360], BF16))
        WP = E(nc.sbuf_tensor("WP", [128, DT * F], BF16))
        MIS = E(nc.sbuf_tensor("MIS", [128, 2496], F32))
        ROWS = E(nc.sbuf_tensor("ROWS", [1, 5760], F32))
        ONES_B = E(nc.sbuf_tensor("ONES_B", [128, 129], BF16))
        OH2 = E(nc.sbuf_tensor("OH2", [128, 160], BF16))
        pb = [E(nc.psum_tensor(f"pb{i}", [128, 512], F32)) for i in range(8)]

        prog = Prog(nc, stack,
                    ["ld0", "ld1", "ld2", "ld3", "st0", "st1", "st2", "st3",
                     "gld0", "gld1", "hst0", "hst1", "pe", "v", "s", "g",
                     "cc", "bar"])

        mis = {}
        _mo = [0]
        def mtake(name, n):
            mis[name] = _mo[0]
            _mo[0] += n
            assert _mo[0] <= 2496
        for nm in ("g1", "b1", "g2", "b2", "td", "mur", "muk", "bq", "bk",
                   "bv", "bo", "bq1", "bk1", "etd"):
            mtake(nm, DT)
        mtake("pe", DT * B)
        mtake("hmask", 8); mtake("hexpo", 8); mtake("bmask", 8)
        mtake("carN", DT * B); mtake("carD", DT * B)
        mtake("gcN", DT * B); mtake("gcD", DT * B)
        mtake("coef", 8 * DT)
        mtake("Sg", 8 * 64)
        mtake("B1g", 8 * 32); mtake("bnd1", 32)
        mtake("B3g", 8 * 32); mtake("bnd3", 32)
        mtake("iota", C.SLOC)
        mtake("cst", 4)

        def M(name, c0=0, n=1):
            return MIS[:, mis[name] + c0: mis[name] + c0 + n]

        rws = {"mu": 0, "ex2": 512, "rstd": 1024, "onesf": 1536,
               "rden": 1664}
        def R(name, c0=0, n=1):
            return ROWS[0:1, rws[name] + c0: rws[name] + c0 + n]

        onesb_col = ONES_B[:, 128:129]
        allones = ONES_B[:, 0:128]

        alloc_f = Alloc(GENF, 20736)
        alloc_b = Alloc(GENB, 15360)

        bankfree = {}
        rowguard = [()]
        wready = [()]
        carry_edge = [0]

        pre_toks = []
        for nm in ("g1", "b1", "g2", "b2", "td", "mur", "muk", "bq", "bk",
                   "bv", "bo"):
            pre_toks.append(prog.dma("sync", "ld", M(nm, 0, DT), ap[nm]))
        pre_toks.append(prog.dma("sync", "ld", M("pe", 0, DT * B), ap["pe"]))
        pre_toks.append(prog.dma("sync", "ld", M("hmask", 0, 8), ap["hmask"]))
        pre_toks.append(prog.dma("sync", "ld", M("hexpo", 0, 8), ap["hexpo"]))
        pre_toks.append(prog.dma("sync", "ld", M("bmask", 0, 8), ap["bmask"]))
        pre_ld = t_mer(*pre_toks)

        gv = prog.inc("gpsimd", lambda e: e.iota(
            M("iota", 0, C.SLOC), pattern=[[1, C.SLOC]], base=1,
            channel_multiplier=0, allow_small_or_imprecise_dtypes=True), "g", 1)
        gv = prog.inc("gpsimd", lambda e: e.memset(ONES_B[:, :], 1.0), "g", 1)
        gv = prog.inc("gpsimd", lambda e: e.memset(OH2[:, :], 0.0), "g", 1)
        prog.wait("gpsimd", "g", gv)
        gv = prog.inc("gpsimd", lambda e: e.memset(OH2[:, 128:129], 1.0), "g", 1)
        gv = prog.inc("gpsimd", lambda e: e.memset(R("onesf", 0, 128), 1.0),
                      "g", 1)
        gv = prog.inc("gpsimd", lambda e: e.memset(M("cst", 0, 1), LN_EPS), "g", 1)
        gv = prog.inc("gpsimd", lambda e: e.memset(M("cst", 1, 1), -0.5), "g", 1)
        gv = prog.inc("gpsimd", lambda e: e.memset(M("cst", 2, 1), -1.0), "g", 1)
        nc.const_aps.register(F32, LN_EPS, M("cst", 0, 1))
        nc.const_aps.register(F32, -0.5, M("cst", 1, 1))
        nc.const_aps.register(F32, -1.0, M("cst", 2, 1))
        pre_g = (("g", gv),)
        prog.need("scalar", pre_g)
        prog.need("vector", pre_g)

        prog.need("vector", t_mer(pre_ld))
        vv = prog.inc("vector", lambda e: e.tensor_scalar(
            out=M("bq1", 0, DT), in0=M("bq", 0, DT), scalar1=1.0, scalar2=None,
            op0=ALU.add), "v", 1)
        vv = prog.inc("vector", lambda e: e.tensor_scalar(
            out=M("bk1", 0, DT), in0=M("bk", 0, DT), scalar1=1.0, scalar2=None,
            op0=ALU.add), "v", 1)
        pre_v = (("v", vv),)

        _wt = []

        def load_w(name, base, rows, cols):
            return prog.dma("gpsimd", "gld",
                            WP[:, base:base + (rows // 128) * cols].rearrange(
                                "p (t e) -> p t e", t=rows // 128),
                            ap[name].rearrange("(t p) e -> p t e", p=128))

        def wdone():
            t = t_mer(*_wt)
            _wt.clear()
            return t

        prog.need("scalar", pre_ld)
        pre_s_etd = prog.inc("scalar", lambda e: e.activation(
            out=M("etd", 0, DT), in_=M("td", 0, DT), func=AF.Exp), "s", 1)

        def ln_stats(x_bf, sq, n, src_toks):
            prog.need("vector", t_mer(src_toks, sq.guard, rowguard[0]))
            vv = prog.inc("vector", lambda e: e.tensor_mul(
                out=sq.all(), in0=x_bf.all(), in1=x_bf.all()), "v", 1)
            sq.ready = (("v", vv),)
            prog.need("tensor", t_mer(src_toks, sq.ready, pre_g,
                                      bankfree.get(4, ()), bankfree.get(7, ())))
            for d in range(DT):
                prog.emit("tensor", lambda e, d=d: e.matmul(
                    pb[4][0:1, 0:n], onesb_col, x_bf.sl(d, 0, n),
                    start=(d == 0), stop=(d == DT - 1)))
            for d in range(DT):
                bld = lambda e, d=d: e.matmul(
                    pb[7][0:1, 0:n], onesb_col, sq.sl(d, 0, n),
                    start=(d == 0), stop=(d == DT - 1))
                if d == DT - 1:
                    pev = prog.inc("tensor", bld, "pe", 1)
                else:
                    prog.emit("tensor", bld)
            sq.guard = (("pe", pev),)
            prog.wait("vector", "pe", pev)
            prog.emit("vector", lambda e: e.tensor_scalar(
                out=R("mu", 0, n), in0=pb[4][0:1, 0:n], scalar1=1.0 / D,
                scalar2=None, op0=ALU.mult))
            prog.emit("vector", lambda e: e.tensor_scalar(
                out=R("ex2", 0, n), in0=pb[7][0:1, 0:n], scalar1=1.0 / D,
                scalar2=None, op0=ALU.mult))
            prog.emit("vector", lambda e: e.tensor_mul(
                out=R("rstd", 0, n), in0=R("mu", 0, n), in1=R("mu", 0, n)))
            vv = prog.inc("vector", lambda e: e.tensor_tensor(
                out=R("ex2", 0, n), in0=R("ex2", 0, n), in1=R("rstd", 0, n),
                op=ALU.subtract), "v", 1)
            bankfree[4] = (("v", vv),)
            bankfree[7] = (("v", vv),)
            prog.need("scalar", t_mer((("v", vv),), rowguard[0]))
            prog.emit("scalar", lambda e: e.activation(
                out=R("rstd", 0, n), in_=R("ex2", 0, n), func=AF.Ln,
                bias=LN_EPS))
            svv = prog.inc("scalar", lambda e: e.activation(
                out=R("rstd", 0, n), in_=R("rstd", 0, n), func=AF.Exp,
                scale=-0.5), "s", 1)
            prog.need("tensor", t_mer((("v", vv),), (("s", svv),), pre_g,
                                      bankfree.get(5, ()), bankfree.get(6, ())))
            prog.emit("tensor", lambda e: e.matmul(
                pb[5][:, 0:n], R("onesf", 0, 128), R("mu", 0, n), start=True,
                stop=True))
            pev = prog.inc("tensor", lambda e: e.matmul(
                pb[6][:, 0:n], R("onesf", 0, 128), R("rstd", 0, n), start=True,
                stop=True), "pe", 1)
            rowguard[0] = (("pe", pev),)
            return pev

        def ln_apply(x_f32, out_buf, pev, n, extra=()):
            prog.need("vector", t_mer((("pe", pev),), x_f32.ready,
                                      out_buf.guard, extra))
            for d in range(DT):
                prog.emit("vector", lambda e, d=d: e.tensor_tensor(
                    out=out_buf.sl(d, 0, n), in0=x_f32.sl(d, 0, n),
                    in1=pb[5][:, 0:n], op=ALU.subtract))
            for d in range(DT):
                bld = lambda e, d=d: e.tensor_tensor(
                    out=out_buf.sl(d, 0, n), in0=out_buf.sl(d, 0, n),
                    in1=pb[6][:, 0:n], op=ALU.mult)
                if d == DT - 1:
                    vv = prog.inc("vector", bld, "v", 1)
                else:
                    prog.emit("vector", bld)
            out_buf.ready = (("v", vv),)
            bankfree[5] = (("v", vv),)
            bankfree[6] = (("v", vv),)
            return vv

        def ln_g(xf, xbf, sq, out, n, mu_sb, rs_sb, xf_toks=None):
            assert 2 * n <= 512
            prog.need("vector", t_mer(xbf.ready, sq.guard))
            gv = prog.inc("vector", lambda e, o=sq.all(), i=xbf.all():
                          e.tensor_mul(out=o, in0=i, in1=i), "v", 1)
            sq.ready = (("v", gv),)
            prog.need("tensor", t_mer(xbf.ready, sq.ready, pre_g,
                                      bankfree.get(4, ())))
            for d in range(DT):
                prog.emit("tensor", lambda e, d=d: e.matmul(
                    pb[4][0:1, 0:n], onesb_col, xbf.sl(d, 0, n),
                    start=(d == 0), stop=(d == DT - 1)))
            for d in range(DT):
                bld = lambda e, d=d: e.matmul(
                    pb[4][0:1, n:2 * n], onesb_col, sq.sl(d, 0, n),
                    start=(d == 0), stop=(d == DT - 1))
                if d == DT - 1:
                    pev = prog.inc("tensor", bld, "pe", 1)
                else:
                    prog.emit("tensor", bld)
            sq.guard = (("pe", pev),)
            xbf.guard = (("pe", pev),)
            prog.need("vector", t_mer((("pe", pev),), rowguard[0]))
            prog.emit("vector", lambda e: e.tensor_scalar(
                out=R("mu", 0, n), in0=pb[4][0:1, 0:n], scalar1=1.0 / D,
                scalar2=None, op0=ALU.mult))
            prog.emit("vector", lambda e: e.tensor_scalar(
                out=R("ex2", 0, n), in0=pb[4][0:1, n:2 * n], scalar1=1.0 / D,
                scalar2=None, op0=ALU.mult))
            prog.emit("vector", lambda e: e.tensor_mul(
                out=R("rstd", 0, n), in0=R("mu", 0, n), in1=R("mu", 0, n)))
            vv = prog.inc("vector", lambda e: e.tensor_tensor(
                out=R("ex2", 0, n), in0=R("ex2", 0, n), in1=R("rstd", 0, n),
                op=ALU.subtract), "v", 1)
            bankfree[4] = (("v", vv),)
            prog.need("scalar", t_mer((("v", vv),), rowguard[0]))
            prog.emit("scalar", lambda e: e.activation(
                out=R("rstd", 0, n), in_=R("ex2", 0, n), func=AF.Ln,
                bias=LN_EPS))
            svv = prog.inc("scalar", lambda e: e.activation(
                out=R("rstd", 0, n), in_=R("rstd", 0, n), func=AF.Exp,
                scale=-0.5), "s", 1)
            prog.need("tensor", t_mer((("v", vv),), (("s", svv),), pre_g,
                                      bankfree.get(5, ())))
            prog.emit("tensor", lambda e: e.matmul(
                pb[5][:, 0:n], R("onesf", 0, 128), R("mu", 0, n), start=True,
                stop=True))
            pev = prog.inc("tensor", lambda e: e.matmul(
                pb[5][:, n:2 * n], R("onesf", 0, 128), R("rstd", 0, n),
                start=True, stop=True), "pe", 1)
            rowguard[0] = (("pe", pev),)
            prog.need("vector", t_mer((("pe", pev),), out.guard,
                                      xf_toks if xf_toks else xf.ready))
            mb = pb[5][:, 0:n].rearrange("p n -> p () n").broadcast_to(
                [128, DT, n])
            rb = pb[5][:, n:2 * n].rearrange("p n -> p () n").broadcast_to(
                [128, DT, n])
            o3 = out.all().rearrange("p (t n) -> p t n", t=DT)
            i3 = xf.all().rearrange("p (t n) -> p t n", t=DT)
            prog.emit("vector", lambda e, o=o3, i=i3, m=mb: e.tensor_tensor(
                out=o, in0=i, in1=m, op=ALU.subtract))
            vv = prog.inc("vector", lambda e, o=o3, r=rb: e.tensor_tensor(
                out=o, in0=o, in1=r, op=ALU.mult), "v", 1)
            out.ready = (("v", vv),)
            bankfree[5] = out.ready
            return out.ready

        def proj(w_base, x_buf, n, n_e, n_k, epilogue, banks, we_stride=None,
                 single=False):
            stride = we_stride if we_stride else n_e * 128
            pair = 2 if (2 * n <= 512 and n_e % 2 == 0 and not single) else 1
            for ep in range(n_e // pair):
                bk = banks[ep % len(banks)]
                prog.need("tensor", t_mer(bankfree.get(bk, ()), x_buf.ready,
                                          wready[0]))
                for half in range(pair):
                    e = ep * pair + half
                    for k in range(n_k):
                        base = w_base + k * stride + e * 128
                        bld = lambda e_, k=k, base=base, bk=bk, half=half: \
                            e_.matmul(pb[bk][:, half * n:(half + 1) * n],
                                      WP[:, base:base + 128],
                                      x_buf.sl(k, 0, n),
                                      start=(k == 0), stop=(k == n_k - 1))
                        if k == n_k - 1 and half == pair - 1:
                            pev = prog.inc("tensor", bld, "pe", 1)
                        else:
                            prog.emit("tensor", bld)
                bankfree[bk] = epilogue(ep, bk, pev, ep * pair * n, pair * n)

        def pass_a():
            _wt.append(load_w("WwT", 0 * D * DT, D, D))
            _wt.append(load_w("WkT", 1 * D * DT, D, D))
            _wt.append(load_w("WvT", 2 * D * DT, D, D))
            wready[0] = wdone()

            x0f = [alloc_f.take(DT, N), alloc_f.take(DT, N)]
            h0 = alloc_f.take(DT, N)
            th = alloc_f.take(DT, N)
            ewk = alloc_f.take(DT, N)
            tb = alloc_f.take(DT, N)
            u = alloc_f.take(DT, N)
            numb = alloc_f.take(DT, N)
            denb = alloc_f.take(DT, N)
            x0b = alloc_b.take(DT, N)
            sqb = alloc_b.take(DT, N)
            h0b = alloc_b.take(DT, N)
            xn1 = [alloc_b.take(DT, N) for _ in range(3)]
            g1b = M("g1", 0, DT).rearrange("p t -> p t ()").broadcast_to(
                [128, DT, N])
            b1b = M("b1", 0, DT).rearrange("p t -> p t ()").broadcast_to(
                [128, DT, N])

            def stats_mm(xbf, sq):
                prog.need("tensor", t_mer(xbf.ready, sq.ready, pre_g,
                                          bankfree.get(4, ())))
                for d in range(DT):
                    prog.emit("tensor", lambda e, d=d, x=xbf: e.matmul(
                        pb[4][0:1, 0:N], onesb_col, x.sl(d), start=(d == 0),
                        stop=(d == DT - 1)))
                for d in range(DT):
                    bld = lambda e, d=d, q=sq: e.matmul(
                        pb[4][0:1, N:2 * N], onesb_col, q.sl(d),
                        start=(d == 0), stop=(d == DT - 1))
                    if d == DT - 1:
                        pev = prog.inc("tensor", bld, "pe", 1)
                    else:
                        prog.emit("tensor", bld)
                sq.guard = (("pe", pev),)
                xbf.guard = (("pe", pev),)
                return pev

            def rows_rstd(pev):
                prog.need("vector", t_mer((("pe", pev),), rowguard[0]))
                prog.emit("vector", lambda e: e.tensor_scalar(
                    out=R("mu", 0, N), in0=pb[4][0:1, 0:N], scalar1=1.0 / D,
                    scalar2=None, op0=ALU.mult))
                prog.emit("vector", lambda e: e.tensor_scalar(
                    out=R("ex2", 0, N), in0=pb[4][0:1, N:2 * N],
                    scalar1=1.0 / D, scalar2=None, op0=ALU.mult))
                prog.emit("vector", lambda e: e.tensor_mul(
                    out=R("rstd", 0, N), in0=R("mu", 0, N),
                    in1=R("mu", 0, N)))
                vv = prog.inc("vector", lambda e: e.tensor_tensor(
                    out=R("ex2", 0, N), in0=R("ex2", 0, N),
                    in1=R("rstd", 0, N), op=ALU.subtract), "v", 1)
                bankfree[4] = (("v", vv),)
                prog.need("scalar", t_mer((("v", vv),), rowguard[0]))
                prog.emit("scalar", lambda e: e.activation(
                    out=R("rstd", 0, N), in_=R("ex2", 0, N), func=AF.Ln,
                    bias=LN_EPS))
                svv = prog.inc("scalar", lambda e: e.activation(
                    out=R("rstd", 0, N), in_=R("rstd", 0, N), func=AF.Exp,
                    scale=-0.5), "s", 1)
                return vv, svv

            def bcast(vv, svv):
                prog.need("tensor", t_mer((("v", vv),), (("s", svv),), pre_g,
                                          bankfree.get(5, ())))
                prog.emit("tensor", lambda e: e.matmul(
                    pb[5][:, 0:N], R("onesf", 0, 128), R("mu", 0, N),
                    start=True, stop=True))
                pev = prog.inc("tensor", lambda e: e.matmul(
                    pb[5][:, N:2 * N], R("onesf", 0, 128), R("rstd", 0, N),
                    start=True, stop=True), "pe", 1)
                rowguard[0] = (("pe", pev),)
                return pev

            def apply_v(xf, out, pev, xf_toks=None):
                prog.need("vector", t_mer((("pe", pev),), out.guard,
                                          xf_toks if xf_toks else xf.ready))
                mb = pb[5][:, 0:N].rearrange("p n -> p () n").broadcast_to(
                    [128, DT, N])
                rb = pb[5][:, N:2 * N].rearrange("p n -> p () n").broadcast_to(
                    [128, DT, N])
                o3 = out.v3()
                i3 = xf.v3()
                prog.emit("vector", lambda e, o=o3, i=i3, m=mb:
                          e.tensor_tensor(out=o, in0=i, in1=m,
                                          op=ALU.subtract))
                vv = prog.inc("vector", lambda e, o=o3, r=rb: e.tensor_tensor(
                    out=o, in0=o, in1=r, op=ALU.mult), "v", 1)
                out.ready = (("v", vv),)
                bankfree[5] = out.ready
                return vv

            st = {}

            def front_p1(i):
                b, j = divmod(i, C.TPB)
                col0 = b * C.SLOC + j * C.TCA
                xf = x0f[i % 2]
                prog.need("sync", xf.guard)
                lv = prog.dma("sync", "ld", xf.v3(),
                              spv("x0")[:, :, col0:col0 + N])
                xf.ready = lv
                prog.need("vector", t_mer(xf.ready, x0b.guard))
                gv = prog.inc("vector", lambda e, o=x0b.all(), i_=xf.all():
                              e.tensor_copy(out=o, in_=i_), "v", 1)
                x0b.ready = (("v", gv),)
                prog.need("vector", t_mer(x0b.ready, sqb.guard))
                gv = prog.inc("vector", lambda e, o=sqb.all(), i_=x0b.all():
                              e.tensor_mul(out=o, in0=i_, in1=i_), "v", 1)
                sqb.ready = (("v", gv),)
                st[i] = dict(col0=col0, b=b, j=j, xf=xf)
                st[i]["pev0"] = stats_mm(x0b, sqb)

            def front_p2(i):
                t = st[i]
                vv, svv = rows_rstd(t["pev0"])
                pev = bcast(vv, svv)
                apply_v(t["xf"], h0, pev)
                t["xf"].guard = h0.ready
                prog.need("gpsimd", h0.ready)
                t["st_h0"] = prog.dma(
                    "gpsimd", "st",
                    spv("sp_h0")[:, :, t["col0"]:t["col0"] + N], h0.v3())
                prog.need("vector", t_mer(h0.ready, h0b.guard))
                gv = prog.inc("vector", lambda e, o=h0b.all(), i_=h0.all():
                              e.tensor_copy(out=o, in_=i_), "v", 1)
                h0b.ready = (("v", gv),)
                prog.need("vector", t_mer(h0b.ready, sqb.guard))
                gv = prog.inc("vector", lambda e, o=sqb.all(), i_=h0b.all():
                              e.tensor_mul(out=o, in0=i_, in1=i_), "v", 1)
                sqb.ready = (("v", gv),)
                t["pev1"] = stats_mm(h0b, sqb)

            def front_p3(i):
                t = st[i]
                xn = xn1[i % 3]
                vv, svv = rows_rstd(t["pev1"])
                pev = bcast(vv, svv)
                apply_v(h0, th, pev)
                h0.guard = t_mer(th.ready, t["st_h0"])
                prog.need("vector", t_mer(th.ready, xn.guard, pre_ld))
                th3 = th.v3()
                prog.emit("vector", lambda e, o=th3, g=g1b: e.tensor_tensor(
                    out=o, in0=o, in1=g, op=ALU.mult))
                vv = prog.inc("vector", lambda e, o=xn.v3(), i_=th3, bb_=b1b:
                              e.tensor_tensor(out=o, in0=i_, in1=bb_,
                                              op=ALU.add), "v", 1)
                xn.ready = (("v", vv),)
                th.guard = xn.ready
                prog.need("gpsimd", xn.ready)
                st_xn = prog.dma(
                    "gpsimd", "st",
                    spv("sp_xn1")[:, :, t["col0"]:t["col0"] + N], xn.v3())
                xn.guard = st_xn

            def back_pw(i):
                xn = xn1[i % 3]
                def epi_w(ep, bk, pev, lo, w):
                    prog.need("scalar", t_mer((("pe", pev),), ewk.guard))
                    sv = prog.inc("scalar", lambda eng, bk=bk, o=ewk.rg(lo, w), w=w:
                                  eng.activation(out=o, in_=pb[bk][:, 0:w],
                                                 func=AF.Exp), "s", 1)
                    return (("s", sv),)
                proj(0 * D * DT, xn, N, DT, DT, epi_w, banks=(0, 1, 2, 3, 6, 7))
                st[i]["sv_w"] = prog.cnt["s"]

            def back_pk(i):
                xn = xn1[i % 3]
                sv_w = st[i]["sv_w"]
                def epi_k(ep, bk, pev, lo, w):
                    prog.need("vector", t_mer((("pe", pev),), (("s", sv_w),),
                                              tb.guard))
                    vv = prog.inc("vector", lambda eng, bk=bk, o=tb.rg(lo, w),
                                  i_=ewk.rg(lo, w), w=w:
                                  eng.tensor_tensor(out=o,
                                                    in0=pb[bk][:, 0:w],
                                                    in1=i_,
                                                    op=ALU.subtract), "v", 1)
                    return (("v", vv),)
                proj(1 * D * DT, xn, N, DT, DT, epi_k, banks=(0, 1, 2, 3, 6, 7))
                vv_k = prog.cnt["v"]
                prog.need("scalar", (("v", vv_k),))
                sv = prog.inc("scalar", lambda e: e.activation(
                    out=ewk.all(), in_=tb.all(), func=AF.Exp), "s", 1)
                ewk.ready = (("s", sv),)
                tb.guard = ewk.ready

            def back_pv(i):
                xn = xn1[i % 3]
                def epi_v(ep, bk, pev, lo, w):
                    prog.need("vector", t_mer((("pe", pev),), ewk.ready,
                                              u.guard))
                    vv = prog.inc("vector", lambda eng, bk=bk, o=u.rg(lo, w),
                                  i_=ewk.rg(lo, w), w=w:
                                  eng.tensor_tensor(out=o, in0=i_,
                                                    in1=pb[bk][:, 0:w],
                                                    op=ALU.mult), "v", 1)
                    return (("v", vv),)
                proj(2 * D * DT, xn, N, DT, DT, epi_v, banks=(0, 1, 2, 3, 6, 7))
                u.ready = (("v", prog.cnt["v"]),)
                xn.guard = t_mer(xn.guard, (("pe", prog.cnt["pe"]),))

            def back_scan(i):
                t = st.pop(i)
                b, j, col0 = t["b"], t["j"], t["col0"]
                prog.need("vector", t_mer(u.ready, ewk.ready, pre_ld,
                                          numb.guard, denb.guard,
                                          (("s", pre_s_etd),)))
                if j > 0:
                    prog.wait("vector", "v", carry_edge[0])
                for d in range(DT):
                    dec_d = M("etd", d, 1).broadcast_to([128, N])
                    init_n = 0.0 if j == 0 else M("carN", d * B + b, 1)
                    init_d = 0.0 if j == 0 else M("carD", d * B + b, 1)
                    prog.emit("vector",
                              lambda e, d=d, dd=dec_d, init=init_n:
                              e.tensor_tensor_scan(
                                  out=numb.sl(d), data0=dd, data1=u.sl(d),
                                  initial=init, op0=ALU.mult, op1=ALU.add))
                    prog.emit("vector",
                              lambda e, d=d, dd=dec_d, init=init_d:
                              e.tensor_tensor_scan(
                                  out=denb.sl(d), data0=dd, data1=ewk.sl(d),
                                  initial=init, op0=ALU.mult, op1=ALU.add))
                carN3 = M("carN", 0, DT * B).rearrange(
                    "p (d b) -> p d b", d=DT)[:, :, b:b + 1]
                carD3 = M("carD", 0, DT * B).rearrange(
                    "p (d b) -> p d b", d=DT)[:, :, b:b + 1]
                prog.emit("vector", lambda e, o=carN3: e.tensor_copy(
                    out=o, in_=numb.v3()[:, :, N - 1:N]))
                vv = prog.inc("vector", lambda e, o=carD3: e.tensor_copy(
                    out=o, in_=denb.v3()[:, :, N - 1:N]), "v", 1)
                carry_edge[0] = vv
                numb.ready = denb.ready = (("v", vv),)
                u.guard = (("v", vv),)
                ewk.guard = (("v", vv),)
                prog.need("gpsimd", numb.ready)
                stv = prog.dma("gpsimd", "st",
                               spv("sp_num")[:, :, col0:col0 + N], numb.v3())
                numb.guard = stv
                stv = prog.dma("gpsimd", "st",
                               spv("sp_den")[:, :, col0:col0 + N], denb.v3())
                denb.guard = stv

            for i in range(C.NTT + 2):
                if i < C.NTT:
                    front_p1(i)
                if i >= 2:
                    back_pw(i - 2)
                if i < C.NTT:
                    front_p2(i)
                if i >= 2:
                    back_pk(i - 2)
                if i < C.NTT:
                    front_p3(i)
                if i >= 2:
                    back_pv(i - 2)
                    back_scan(i - 2)

        pass_a()
        def coll_sc():
            prog.need("sync", (("v", carry_edge[0]),))
            spv_sc = ap["SC_loc"].rearrange("p (t c b) -> p t c b", t=DT, c=2)
            stv1 = prog.dma("sync", "hst", spv_sc[:, :, 0, :],
                            M("carN", 0, DT * B).rearrange("p (t b) -> p t b", t=DT))
            stv2 = prog.dma("sync", "hst", spv_sc[:, :, 1, :],
                            M("carD", 0, DT * B).rearrange("p (t b) -> p t b", t=DT))
            prog.need("gpsimd", t_mer(stv1, stv2))
            ccv = prog.inc("gpsimd", lambda e: e.collective_compute(
                "AllGather", ALU.bypass, replica_groups=[list(range(NCORES))],
                ins=[ap["SC_loc"]], outs=[ap["SC_all"]]), "cc", 1)
            prog.wait("sync", "cc", ccv)
            lv_sg = prog.dma("sync", "ld", M("Sg", 0, 8 * 64),
                             ap["SC_all"].rearrange("(j p) c -> p j c", p=128))
            prog.need("vector", t_mer(pre_ld))
            for jj in range(8):
                bld = lambda e, jj=jj: e.tensor_scalar(
                    out=M("coef", jj * DT, DT), in0=M("td", 0, DT),
                    scalar1=M("hexpo", jj, 1), scalar2=None, op0=ALU.mult)
                if jj == 7:
                    vv = prog.inc("vector", bld, "v", 1)
                else:
                    prog.emit("vector", bld)
            prog.wait("scalar", "v", vv)
            sv = prog.inc("scalar", lambda e: e.activation(
                out=M("coef", 0, 8 * DT), in_=M("coef", 0, 8 * DT), func=AF.Exp),
                "s", 1)
            prog.wait("vector", "s", sv)
            for jj in range(8):
                bld = lambda e, jj=jj: e.tensor_scalar(
                    out=M("coef", jj * DT, DT), in0=M("coef", jj * DT, DT),
                    scalar1=M("hmask", jj, 1), scalar2=None, op0=ALU.mult)
                if jj == 7:
                    vv = prog.inc("vector", bld, "v", 1)
                else:
                    prog.emit("vector", bld)
            prog.wait("vector", "v", vv)
            prog.need("vector", lv_sg)
            last = None
            for jj in range(8):
                for t in range(DT):
                    for cn, tens in (("gcN", 0), ("gcD", 1)):
                        src = M("Sg", jj * 64 + t * 8 + tens * 4, B)
                        dst = M(cn, t * B, B)
                        if jj == 0:
                            bld = lambda e, src=src, dst=dst, t=t: e.tensor_scalar(
                                out=dst, in0=src, scalar1=M("coef", t, 1),
                                scalar2=None, op0=ALU.mult)
                        else:
                            bld = (lambda e, src=src, dst=dst, t=t, jj=jj:
                                   e.scalar_tensor_tensor(
                                       out=dst, in0=src,
                                       scalar=M("coef", jj * DT + t, 1),
                                       in1=dst, op0=ALU.mult, op1=ALU.add))
                        prog.emit("vector", bld)
            gcar_edge = prog.cnt["v"]
            return gcar_edge
        gcar_edge = coll_sc()

        prog.barrier()
        alloc_f.reset()
        alloc_b.reset()

        def pass_b():
            _wt.append(load_w("WrT", 0 * D * DT, D, D))
            _wt.append(load_w("WoT", 1 * D * DT, D, D))
            _wt.append(load_w("WrecT", 2 * D * DT, D, D))
            wready[0] = wdone()

            numb = [alloc_f.take(DT, N), alloc_f.take(DT, N)]
            denb = alloc_f.take(DT, N)
            scl = alloc_f.take(DT, N)
            rden = alloc_f.take(DT, N)
            rscr = scl
            h0 = alloc_f.take(DT, N)
            r1 = alloc_f.take(DT, N)
            yb = alloc_f.take(DT, N)
            th = alloc_f.take(DT, N)
            mu_sb = alloc_f.take(1, N)
            rs_sb = alloc_f.take(1, N)
            xn1 = alloc_b.take(DT, N)
            rw = alloc_b.take(DT, N)
            ybf = alloc_b.take(DT, N)
            sqb = alloc_b.take(DT, N)
            xn2 = alloc_b.take(DT, N)
            g2b = M("g2", 0, DT).rearrange("p t -> p t ()").broadcast_to(
                [128, DT, N])
            b2b = M("b2", 0, DT).rearrange("p t -> p t ()").broadcast_to(
                [128, DT, N])

            def front(i):
                b, j = divmod(i, C.TPB)
                col0 = b * C.SLOC + j * C.TCA
                s0 = j * C.TCA
                nb = numb[i % 2]
                prog.need("sync", nb.guard)
                lvn = prog.dma("sync", "ld", nb.v3(),
                               spv("sp_num")[:, :, col0:col0 + N])
                nb.ready = lvn
                prog.need("sync", denb.guard)
                lvd = prog.dma("sync", "ld", denb.v3(),
                               spv("sp_den")[:, :, col0:col0 + N])
                denb.ready = lvd
                prog.need("scalar", t_mer(pre_g, pre_ld, scl.guard))
                for d in range(DT):
                    bld = lambda e, d=d, ia=M("iota", s0, N): e.activation(
                        out=scl.sl(d), in_=ia, func=AF.Exp,
                        scale=M("td", d, 1))
                    if d == DT - 1:
                        sv = prog.inc("scalar", bld, "s", 1)
                    else:
                        prog.emit("scalar", bld)
                scl.ready = (("s", sv),)
                gcnb = M("gcN", 0, DT * B).rearrange(
                    "p (d b) -> p d b", d=DT)[:, :, b:b + 1].broadcast_to(
                    [128, DT, N])
                gcdb = M("gcD", 0, DT * B).rearrange(
                    "p (d b) -> p d b", d=DT)[:, :, b:b + 1].broadcast_to(
                    [128, DT, N])
                prog.need("vector", t_mer(nb.ready, denb.ready, scl.ready,
                                          (("v", gcar_edge),), rden.guard,
                                          rscr.guard))
                prog.emit("vector", lambda e, o=rden.v3(), i=scl.v3(), g=gcnb:
                          e.tensor_tensor(out=o, in0=i, in1=g, op=ALU.mult))
                prog.emit("vector", lambda e, o=nb.v3(), i=rden.v3():
                          e.tensor_add(out=o, in0=o, in1=i))
                prog.emit("vector", lambda e, o=rscr.v3(), i=scl.v3(), g=gcdb:
                          e.tensor_tensor(out=o, in0=i, in1=g, op=ALU.mult))
                prog.emit("vector", lambda e, o=denb.v3(), i=rscr.v3():
                          e.tensor_add(out=o, in0=o, in1=i))
                vv = prog.inc("vector", lambda e, o=denb.all():
                              e.tensor_scalar(out=o, in0=o, scalar1=WKV_EPS,
                                              scalar2=None, op0=ALU.add),
                              "v", 1)
                scl.guard = (("v", vv),)
                prog.wait("scalar", "v", vv)
                prog.need("scalar", rscr.guard)
                prog.emit("scalar", lambda e, o=rscr.all(), i=denb.all():
                          e.activation(out=o, in_=i, func=AF.Ln))
                sv = prog.inc("scalar", lambda e, o=rden.all(), i=rscr.all():
                              e.activation(out=o, in_=i, func=AF.Exp,
                                           scale=-1.0), "s", 1)
                rden.ready = (("s", sv),)
                prog.need("vector", t_mer(rden.ready,))
                gv = prog.inc("vector", lambda e, o=nb.all(), r=rden.all():
                              e.tensor_mul(out=o, in0=o, in1=r), "v", 1)
                nb.ready = (("v", gv),)
                denb.guard = nb.ready
                rden.guard = nb.ready
                rscr.guard = nb.ready

            def back(i):
                b, j = divmod(i, C.TPB)
                col0 = b * C.SLOC + j * C.TCA
                nb = numb[i % 2]
                prog.need("sync", xn1.guard)
                lvx = prog.dma("sync", "ld", xn1.v3(),
                               spv("sp_xn1")[:, :, col0:col0 + N])
                xn1.ready = lvx
                prog.need("sync", h0.guard)
                lvh = prog.dma("sync", "ld", h0.v3(),
                               spv("sp_h0")[:, :, col0:col0 + N])
                h0.ready = lvh

                def epi_r(ep, bk, pev, lo, w):
                    prog.need("scalar", t_mer((("pe", pev),), r1.guard))
                    sv = prog.inc("scalar", lambda eng, bk=bk, o=r1.rg(lo, w),
                                  w=w: eng.activation(out=o,
                                                      in_=pb[bk][:, 0:w],
                                                      func=AF.Sigmoid),
                                  "s", 1)
                    return (("s", sv),)
                proj(0 * D * DT, xn1, N, DT, DT, epi_r, banks=(0, 1, 2, 3, 6, 7))
                r1.ready = (("s", prog.cnt["s"]),)

                prog.need("gpsimd", t_mer(r1.ready, nb.ready, rw.guard))
                gv = prog.inc("gpsimd", lambda e, o=rw.all(), i=nb.all(),
                              r=r1.all(): e.tensor_tensor(
                                  out=o, in0=i, in1=r, op=ALU.mult), "g", 1)
                rw.ready = (("g", gv),)
                nb.guard = rw.ready
                r1.guard = rw.ready

                def epi_o(ep, bk, pev, lo, w):
                    prog.need("vector", t_mer((("pe", pev),), h0.ready,
                                              yb.guard))
                    vv = prog.inc("vector", lambda eng, bk=bk, o=yb.rg(lo, w),
                                  i_=h0.rg(lo, w), w=w:
                                  eng.tensor_tensor(out=o,
                                                    in0=pb[bk][:, 0:w],
                                                    in1=i_, op=ALU.add),
                                  "v", 1)
                    return (("v", vv),)
                proj(1 * D * DT, rw, N, DT, DT, epi_o, banks=(0, 1, 2, 3, 6, 7))
                yb.ready = (("v", prog.cnt["v"]),)
                h0.guard = yb.ready
                rw.guard = (("pe", prog.cnt["pe"]),)
                prog.need("gpsimd", yb.ready)
                st_y = prog.dma("gpsimd", "st",
                                spv("sp_y")[:, :, col0:col0 + N], yb.v3())

                prog.need("vector", t_mer(yb.ready, ybf.guard))
                gv = prog.inc("vector", lambda e, o=ybf.all(), i=yb.all():
                              e.tensor_copy(out=o, in_=i), "v", 1)
                ybf.ready = (("v", gv),)
                ln_g(yb, ybf, sqb, th, N, mu_sb, rs_sb)
                yb.guard = t_mer(th.ready, st_y)
                prog.need("vector", t_mer(th.ready, xn2.guard, pre_ld))
                th3 = th.v3()
                prog.emit("vector", lambda e, o=th3, g=g2b: e.tensor_tensor(
                    out=o, in0=o, in1=g, op=ALU.mult))
                vv = prog.inc("vector", lambda e, o=xn2.v3(), i=th3, bb_=b2b:
                              e.tensor_tensor(out=o, in0=i, in1=bb_,
                                              op=ALU.add), "v", 1)
                xn2.ready = (("v", vv),)
                th.guard = xn2.ready
                prog.need("gpsimd", xn2.ready)
                st_x = prog.dma("gpsimd", "st",
                                spv("sp_xn2")[:, :, col0:col0 + N], xn2.v3())

                def epi_rc(ep, bk, pev, lo, w):
                    prog.need("scalar", t_mer((("pe", pev),), r1.guard))
                    sv = prog.inc("scalar", lambda eng, bk=bk, o=r1.rg(lo, w),
                                  w=w: eng.activation(out=o,
                                                      in_=pb[bk][:, 0:w],
                                                      func=AF.Sigmoid),
                                  "s", 1)
                    return (("s", sv),)
                proj(2 * D * DT, xn2, N, DT, DT, epi_rc, banks=(0, 1, 2, 3, 6, 7))
                r1.ready = (("s", prog.cnt["s"]),)
                prog.need("gpsimd", r1.ready)
                st_r = prog.dma("gpsimd", "st",
                                spv("sp_rcm")[:, :, col0:col0 + N], r1.v3())
                r1.guard = st_r
                xn2.guard = t_mer(st_x, (("pe", prog.cnt["pe"]),))
                xn1.guard = (("pe", prog.cnt["pe"]),)

            for i in range(C.NTT + 1):
                if i < C.NTT:
                    front(i)
                if i >= 1:
                    back(i - 1)

        pass_b()
        prog.barrier()
        alloc_f.reset()
        alloc_b.reset()

        def pass_ckey():
            _wt.append(load_w("WkeyT", 0, D, F))
            wready[0] = wdone()
            xn2 = [alloc_b.take(DT, N), alloc_b.take(DT, N)]
            _kc = alloc_b.take(FT, N)
            kcm2 = [_kc, _kc]
            tbs = alloc_f.take(1, 2 * N)

            def front(i):
                b, j = divmod(i, C.TPB)
                col0 = b * C.SLOC + j * C.TCA
                xn = xn2[i % 2]
                prog.need("sync", xn.guard)
                lv = prog.dma("sync", "ld", xn.v3(),
                              spv("sp_xn2")[:, :, col0:col0 + N])
                xn.ready = lv

            def back(i):
                b, j = divmod(i, C.TPB)
                col0 = b * C.SLOC + j * C.TCA
                xn = xn2[i % 2]
                kc = kcm2[i % 2]

                def epi_key(ep, bk, pev, lo, w, kc=kc):
                    prog.need("scalar", t_mer((("pe", pev),), tbs.guard,
                                              kc.guard))
                    prog.emit("scalar", lambda eng, bk=bk, o=tbs.rg(0, w),
                              w=w: eng.activation(out=o, in_=pb[bk][:, 0:w],
                                                  func=AF.Relu))
                    sv = prog.inc("scalar", lambda eng, kc=kc, o=kc.rg(lo, w),
                                  i_=tbs.rg(0, w): eng.activation(
                                      out=o, in_=i_, func=AF.Square),
                                  "s", 1)
                    tbs.guard = (("s", sv),)
                    return (("s", sv),)
                proj(0, xn, N, FT, DT, epi_key, banks=(0, 1, 2, 3, 6, 7),
                     we_stride=F)
                kc.ready = (("s", prog.cnt["s"]),)
                xn.guard = (("pe", prog.cnt["pe"]),)
                prog.need("gpsimd", kc.ready)
                stv = prog.dma("gpsimd", "st",
                               spv("sp_kcm")[:, :, col0:col0 + N], kc.v3())
                kc.guard = stv

            for i in range(C.NTT + 1):
                if i < C.NTT:
                    front(i)
                if i >= 1:
                    back(i - 1)

        pass_ckey()
        prog.barrier()
        alloc_f.reset()
        alloc_b.reset()

        def pass_cval():
            _wt.append(load_w("WvalT", 0, F, D))
            wready[0] = wdone()
            kcm = [None, None]
            kcm[0] = alloc_b.take(FT, N)
            kcm[1] = kcm[0]
            x1bf = alloc_b.take(DT, N)
            sqb = alloc_b.take(DT, N)
            x0f = [alloc_f.take(DT, N), alloc_f.take(DT, N)]
            yf = [alloc_f.take(DT, N), alloc_f.take(DT, N)]
            rcm = alloc_f.take(DT, N)
            th = alloc_f.take(DT, N)
            x1 = alloc_f.take(DT, N)
            hf1 = alloc_f.take(DT, N)
            mu_sb = alloc_f.take(1, N)
            rs_sb = alloc_f.take(1, N)
            b1pub = 0

            def front(i):
                b, j = divmod(i, C.TPB)
                col0 = b * C.SLOC + j * C.TCA
                prog.need("sync", x0f[i % 2].guard)
                lv = prog.dma("sync", "ld", x0f[i % 2].v3(),
                              spv("x0")[:, :, col0:col0 + N])
                x0f[i % 2].ready = lv
                prog.need("sync", yf[i % 2].guard)
                lv = prog.dma("sync", "ld", yf[i % 2].v3(),
                              spv("sp_y")[:, :, col0:col0 + N])
                yf[i % 2].ready = lv

            def back(i):
                nonlocal b1pub
                b, j = divmod(i, C.TPB)
                col0 = b * C.SLOC + j * C.TCA
                kc = kcm[i % 2]
                prog.need("sync", kc.guard)
                lvk = prog.dma("sync", "ld", kc.v3(),
                               spv("sp_kcm")[:, :, col0:col0 + N])
                kc.ready = lvk
                prog.need("sync", rcm.guard)
                lvr = prog.dma("sync", "ld", rcm.v3(),
                               spv("sp_rcm")[:, :, col0:col0 + N])
                rcm.ready = lvr

                def epi_val(ep, bk, pev, lo, w):
                    prog.need("vector", t_mer((("pe", pev),), rcm.ready,
                                              th.guard))
                    vv = prog.inc("vector", lambda eng, bk=bk, o=th.rg(lo, w),
                                  i_=rcm.rg(lo, w), w=w:
                                  eng.tensor_tensor(out=o,
                                                    in0=pb[bk][:, 0:w],
                                                    in1=i_, op=ALU.mult),
                                  "v", 1)
                    return (("v", vv),)
                proj(0, kc, N, DT, FT, epi_val, banks=(0, 1, 2, 3, 6, 7),
                     we_stride=D)
                th.ready = (("v", prog.cnt["v"]),)
                kc.guard = (("pe", prog.cnt["pe"]),)
                rcm.guard = th.ready
                prog.need("vector", t_mer(th.ready, yf[i % 2].ready,
                                          x0f[i % 2].ready, x1.guard))
                prog.emit("vector", lambda e, o=x1.all(), a=th.all(),
                          bb_=yf[i % 2].all(): e.tensor_add(out=o, in0=a,
                                                            in1=bb_))
                gv = prog.inc("vector", lambda e, o=x1.all(),
                              a=x0f[i % 2].all(): e.tensor_add(
                                  out=o, in0=o, in1=a), "v", 1)
                x1.ready = (("v", gv),)
                th.guard = yf[i % 2].guard = x0f[i % 2].guard = x1.ready
                prog.need("gpsimd", x1.ready)
                st_x1 = prog.dma("gpsimd", "st",
                                 spv("sp_x1")[:, :, col0:col0 + N], x1.v3())
                prog.need("vector", t_mer(x1.ready, x1bf.guard))
                gv = prog.inc("vector", lambda e, o=x1bf.all(), i_=x1.all():
                              e.tensor_copy(out=o, in_=i_), "v", 1)
                x1bf.ready = (("v", gv),)
                ln_g(x1, x1bf, sqb, hf1, N, mu_sb, rs_sb)
                x1.guard = t_mer(hf1.ready, st_x1)
                if j == C.TPB - 1:
                    prog.need("vector", hf1.ready)
                    b3 = M("bnd1", 0, 32).rearrange(
                        "p (d b) -> p d b", d=DT)[:, :, b:b + 1]
                    b1pub = prog.inc("vector", lambda e, o=b3: e.tensor_copy(
                        out=o, in_=hf1.v3()[:, :, N - 1:N]), "v", 1)
                prog.need("gpsimd", hf1.ready)
                stv = prog.dma("gpsimd", "st",
                               spv("sp_hf1")[:, :, col0:col0 + N], hf1.v3())
                hf1.guard = stv

            for i in range(C.NTT + 1):
                if i < C.NTT:
                    front(i)
                if i >= 1:
                    back(i - 1)
            return b1pub

        b1pub = pass_cval()
        def boundary_collective(bnd_name, loc, all_, gname, pub_edge):
            prog.need("sync", (("v", pub_edge),))
            stv = prog.dma("sync", "hst", ap[loc], M(bnd_name, 0, 32))
            prog.need("gpsimd", stv)
            ccv = prog.inc("gpsimd", lambda e: e.collective_compute(
                "AllGather", ALU.bypass, replica_groups=[list(range(NCORES))],
                ins=[ap[loc]], outs=[ap[all_]]), "cc", 1)
            prog.wait("sync", "cc", ccv)
            lv = prog.dma("sync", "ld", M(gname, 0, 8 * 32),
                          ap[all_].rearrange("(j p) c -> p j c", p=128))
            prog.need("vector", t_mer(pre_ld, lv))
            last = None
            for jj in range(8):
                src = M(gname, jj * 32, 32)
                if jj == 0:
                    bld = lambda e, src=src, bn=bnd_name: e.tensor_scalar(
                        out=M(bn, 0, 32), in0=src, scalar1=M("bmask", 0, 1),
                        scalar2=None, op0=ALU.mult)
                else:
                    bld = (lambda e, src=src, jj=jj, bn=bnd_name:
                           e.scalar_tensor_tensor(
                               out=M(bn, 0, 32), in0=src,
                               scalar=M("bmask", jj, 1), in1=M(bn, 0, 32),
                               op0=ALU.mult, op1=ALU.add))
                prog.emit("vector", bld)
            return prog.cnt["v"]

        bnd1_edge = boundary_collective("bnd1", "B1_loc", "B1_all", "B1g",
                                        b1pub)
        prog.barrier()
        alloc_f.reset()
        alloc_b.reset()

        def ffn_pass(src_h, src_x, dst_x, dst_h, bnd_name, bnd_edge, is_final):
            NP = N + 1
            hf = [alloc_f.take(DT, NP), alloc_f.take(DT, NP)]
            tsub = alloc_f.take(DT, N)
            xf = alloc_f.take(DT, N)
            rf = alloc_f.take(DT, N)
            of = alloc_f.take(DT, N)
            x2 = alloc_f.take(DT, N)
            hb = None if is_final else alloc_f.take(DT, N)
            fsc = alloc_f.take(DT, N)
            mu_sb = alloc_f.take(1, N)
            rs_sb = alloc_f.take(1, N)
            xr = [alloc_b.take(DT, N), alloc_b.take(DT, N)]
            xk = [alloc_b.take(DT, N), alloc_b.take(DT, N)]
            kf = alloc_b.take(DT, N)
            hpb = None if is_final else alloc_b.take(DT, N)
            tbs2 = alloc_f.take(1, 2 * N)
            murb = M("mur", 0, DT).rearrange("p t -> p t ()").broadcast_to(
                [128, DT, N])
            mukb = M("muk", 0, DT).rearrange("p t -> p t ()").broadcast_to(
                [128, DT, N])

            def front(i):
                b, j = divmod(i, C.TPB)
                col0 = b * C.SLOC + j * C.TCA
                hfi = hf[i % 2]
                prog.need("sync", hfi.guard)
                if j > 0:
                    lvh = prog.dma("sync", "ld", hfi.v3(),
                                   spv(src_h)[:, :, col0 - 1:col0 + N])
                    hfi.ready = lvh
                else:
                    lvh = prog.dma("sync", "ld",
                                   hfi.v3()[:, :, 1:NP],
                                   spv(src_h)[:, :, col0:col0 + N])
                    prog.need("vector", t_mer(lvh, (("v", bnd_edge),),
                                              hfi.guard))
                    bsl = M(bnd_name, 0, 32).rearrange(
                        "p (d b) -> p d b", d=DT)[:, :, b:b + 1]
                    vvb = prog.inc("vector", lambda e, o=hfi.v3()[:, :, 0:1],
                                   i_=bsl: e.tensor_copy(out=o, in_=i_),
                                   "v", 1)
                    hfi.ready = t_mer((("v", vvb),), lvh)
                def hsl3(off):
                    return hfi.all().rearrange(
                        "p (t n) -> p t n", t=DT)[:, :, off:off + N]
                prog.need("vector", t_mer(hfi.ready, tsub.guard, pre_ld,
                                          xr[i % 2].guard, xk[i % 2].guard))
                prog.emit("vector", lambda e, o=tsub.v3(), a=hsl3(1),
                          bb_=hsl3(0): e.tensor_tensor(out=o, in0=a, in1=bb_,
                                                       op=ALU.subtract))
                prog.emit("vector", lambda e, o=fsc.v3(), a=tsub.v3(), m=murb:
                          e.tensor_tensor(out=o, in0=a, in1=m, op=ALU.mult))
                prog.emit("vector", lambda e, o=xr[i % 2].v3(), a=fsc.v3(),
                          bb_=hsl3(0): e.tensor_tensor(out=o, in0=a, in1=bb_,
                                                       op=ALU.add))
                prog.emit("vector", lambda e, o=fsc.v3(), a=tsub.v3(), m=mukb:
                          e.tensor_tensor(out=o, in0=a, in1=m, op=ALU.mult))
                vv = prog.inc("vector", lambda e, o=xk[i % 2].v3(),
                              a=fsc.v3(), bb_=hsl3(0): e.tensor_tensor(
                                  out=o, in0=a, in1=bb_, op=ALU.add), "v", 1)
                xr[i % 2].ready = xk[i % 2].ready = (("v", vv),)
                tsub.guard = fsc.guard = (("v", vv),)
                hfi.guard = (("v", vv),)

            def back(i):
                b, j = divmod(i, C.TPB)
                col0 = b * C.SLOC + j * C.TCA
                prog.need("sync", xf.guard)
                lvx = prog.dma("sync", "ld", xf.v3(),
                               spv(src_x)[:, :, col0:col0 + N])
                xf.ready = lvx

                def epi_rf(ep, bk, pev, lo, w):
                    prog.need("scalar", t_mer((("pe", pev),), rf.guard))
                    sv = prog.inc("scalar",
                                  lambda eng, bk=bk, o=rf.rg(lo, w), w=w:
                                  eng.activation(
                                      out=o, in_=pb[bk][:, 0:w],
                                      func=AF.Sigmoid), "s", 1)
                    return (("s", sv),)
                proj(0 * D * DT, xr[i % 2], N, DT, DT, epi_rf,
                     banks=(0, 1, 2, 3, 6, 7))
                rf.ready = (("s", prog.cnt["s"]),)
                xr[i % 2].guard = (("pe", prog.cnt["pe"]),)

                def epi_kf(ep, bk, pev, lo, w):
                    prog.need("vector", t_mer((("pe", pev),), tbs2.guard))
                    vv = prog.inc("vector", lambda eng, bk=bk, o=tbs2.rg(0, w),
                                  w=w: eng.tensor_scalar_max(
                                      out=o, in0=pb[bk][:, 0:w],
                                      scalar1=0.0), "v", 1)
                    prog.need("scalar", t_mer((("v", vv),), kf.guard))
                    sv = prog.inc("scalar", lambda eng, o=kf.rg(lo, w),
                                  i_=tbs2.rg(0, w): eng.activation(
                                      out=o, in_=i_, func=AF.Square), "s", 1)
                    tbs2.guard = (("s", sv),)
                    return (("s", sv),)
                proj(1 * D * DT, xk[i % 2], N, DT, DT, epi_kf,
                     banks=(0, 1, 2, 3, 6, 7))
                kf.ready = (("s", prog.cnt["s"]),)
                xk[i % 2].guard = (("pe", prog.cnt["pe"]),)

                def epi_vf(ep, bk, pev, lo, w):
                    prog.need("vector", t_mer((("pe", pev),), rf.ready,
                                              of.guard))
                    vv = prog.inc("vector",
                                  lambda eng, bk=bk, o=of.rg(lo, w),
                                  i_=rf.rg(lo, w), w=w: eng.tensor_tensor(
                                      out=o, in0=pb[bk][:, 0:w],
                                      in1=i_, op=ALU.mult), "v", 1)
                    return (("v", vv),)
                proj(2 * D * DT, kf, N, DT, DT, epi_vf, banks=(0, 1, 2, 3, 6, 7))
                of.ready = (("v", prog.cnt["v"]),)
                rf.guard = of.ready
                kf.guard = (("pe", prog.cnt["pe"]),)

                prog.need("vector", t_mer(of.ready, xf.ready, x2.guard))
                gv = prog.inc("vector", lambda e, o=x2.all(), a=of.all(),
                              bb_=xf.all(): e.tensor_add(out=o, in0=a,
                                                         in1=bb_), "v", 1)
                x2.ready = (("v", gv),)
                of.guard = xf.guard = x2.ready
                prog.need("gpsimd", x2.ready)
                stv = prog.dma("gpsimd", "st", spv(dst_x)[:, :, col0:col0 + N],
                               x2.v3())
                if is_final:
                    x2.guard = stv
                    return

                x2bf = xr[i % 2]
                sqb2 = xk[i % 2]
                prog.need("vector", t_mer(x2.ready, x2bf.guard))
                gv = prog.inc("vector", lambda e, o=x2bf.all(), i_=x2.all():
                              e.tensor_copy(out=o, in_=i_), "v", 1)
                x2bf.ready = (("v", gv),)
                ln_g(x2, x2bf, sqb2, hb, N, mu_sb, rs_sb)
                x2.guard = t_mer(hb.ready, stv)
                prog.need("gpsimd", t_mer(hb.ready, hpb.guard))
                peb = M("pe", 0, DT * B).rearrange(
                    "p (t b) -> p t b", t=DT)[:, :, b:b + 1].broadcast_to(
                    [128, DT, N])
                gv = prog.inc("gpsimd", lambda e, o=hpb.v3(), a=hb.v3(),
                              p_=peb: e.tensor_tensor(out=o, in0=a, in1=p_,
                                                      op=ALU.add), "g", 1)
                hpb.ready = (("g", gv),)
                hb.guard = hpb.ready
                prog.need("gpsimd", hpb.ready)
                stv = prog.dma("gpsimd", "st", spv(dst_h)[:, :, col0:col0 + N],
                               hpb.v3())
                hpb.guard = stv

            for i in range(C.NTT + 1):
                if i < C.NTT:
                    front(i)
                if i >= 1:
                    back(i - 1)

        _wt.append(load_w("fWrT", 0 * D * DT, D, D))
        _wt.append(load_w("fWkT", 1 * D * DT, D, D))
        _wt.append(load_w("fWvT", 2 * D * DT, D, D))
        wready[0] = wdone()
        ffn_pass("sp_hf1", "sp_x1", "sp_x2", "sp_hp", "bnd1", bnd1_edge, False)
        prog.barrier()
        alloc_f.reset()
        alloc_b.reset()

        def pass_e():
            _wt.append(load_w("bWqT", 0 * D * DT, D, D))
            _wt.append(load_w("bWkT", 1 * D * DT, D, D))
            _wt.append(load_w("bWvT", 2 * D * DT, D, D))
            _wt.append(load_w("bWoT", 3 * D * DT, D, D))
            wready[0] = wdone()

            SE = C.SE
            NE = 4 * SE
            N1 = C.TCA
            NT1 = C.TOK // N1
            qb1 = alloc_f.take(DT, N1)
            esc = alloc_f.take(DT, N1)
            hp1 = [alloc_b.take(DT, N1) for _ in range(3)]
            oub = [alloc_b.take(DT, N1) for _ in range(3)]
            e1_sp = []

            for t in range(NT1):
                col0 = t * N1
                hpb = hp1[t % 3]
                prog.need("sync", hpb.guard)
                hpb.ready = prog.dma("sync", "ld", hpb.v3(),
                                     spv("sp_hp")[:, :, col0:col0 + N1])
                for wi, wof, bias1, dsp in ((0, 0 * D * DT, "bq1", "sp_pq"),
                                            (1, 1 * D * DT, "bk1", "sp_pk"),
                                            (2, 2 * D * DT, None, "sp_pv")):
                    ob = oub[(t * 3 + wi) % 3]
                    if bias1 is None:
                        def epi(ep, bk, pev, lo, w, ob=ob):
                            prog.need("vector", t_mer((("pe", pev),), ob.guard,
                                                      pre_ld))
                            vv = prog.inc("vector", lambda eng, bk=bk, ob=ob,
                                          w=w, lo=lo, s1=M("bv", ep, 1):
                                          eng.tensor_scalar(
                                              out=ob.rg(lo, w),
                                              in0=pb[bk][:, 0:w], scalar1=s1,
                                              scalar2=None, op0=ALU.add),
                                          "v", 1)
                            return (("v", vv),)
                        proj(wof, hpb, N1, DT, DT, epi, banks=(0, 1, 2, 3),
                             single=True)
                        ob.ready = (("v", prog.cnt["v"]),)
                    else:
                        def epi(ep, bk, pev, lo, w, b1=bias1):
                            prog.need("vector", t_mer((("pe", pev),),
                                                      qb1.guard, pre_v))
                            vv = prog.inc("vector", lambda eng, bk=bk, w=w,
                                          lo=lo, s1=M(b1, ep, 1):
                                          eng.tensor_scalar(
                                              out=qb1.rg(lo, w),
                                              in0=pb[bk][:, 0:w], scalar1=s1,
                                              scalar2=None, op0=ALU.add),
                                          "v", 1)
                            return (("v", vv),)
                        proj(wof, hpb, N1, DT, DT, epi, banks=(0, 1, 2, 3),
                             single=True)
                        prog.need("vector", t_mer((("v", prog.cnt["v"]),),
                                                  esc.guard))
                        vv = prog.inc("vector", lambda e: e.tensor_scalar(
                            out=esc.all(), in0=qb1.all(), scalar1=1.0,
                            scalar2=None, op0=ALU.min), "v", 1)
                        prog.wait("scalar", "v", vv)
                        sv = prog.inc("scalar", lambda e: e.activation(
                            out=esc.all(), in_=esc.all(), func=AF.Exp,
                            bias=-1.0), "s", 1)
                        prog.need("vector", t_mer((("s", sv),), ob.guard))
                        vv = prog.inc("vector", lambda e, ob=ob:
                                      e.tensor_tensor(out=ob.all(),
                                                      in0=qb1.all(),
                                                      in1=esc.all(),
                                                      op=ALU.max), "v", 1)
                        ob.ready = (("v", vv),)
                        qb1.guard = ob.ready
                        esc.guard = ob.ready
                    prog.need("gpsimd", ob.ready)
                    stv = prog.dma("gpsimd", "st",
                                   spv(dsp)[:, :, col0:col0 + N1], ob.v3())
                    ob.guard = stv
                    e1_sp.append(stv)
                hpb.guard = (("pe", prog.cnt["pe"]),)

            e1done = t_mer((("pe", prog.cnt["pe"]),),
                           (("v", prog.cnt["v"]),),
                           (("s", prog.cnt["s"]),), *e1_sp)

            alloc_f.reset()
            alloc_b.reset()
            pq2 = [Buf(WP, 0 * DT * NE, DT, NE), Buf(WP, 1 * DT * NE, DT, NE)]
            pk2 = [Buf(WP, 2 * DT * NE, DT, NE), Buf(WP, 3 * DT * NE, DT, NE)]
            pv2 = [Buf(WP, 4 * DT * NE, DT, NE), Buf(WP, 5 * DT * NE, DT, NE)]
            numa2 = [alloc_f.take(DT, NE), alloc_f.take(DT, NE)]
            tmpE = alloc_f.take(DT, NE)
            tm = alloc_f.take(DT, NE)
            x2f = alloc_f.take(DT, NE)
            d32 = alloc_f.take(1, 2 * SE)
            attb = alloc_b.take(DT, NE)
            zb = alloc_b.take(DT, SE)
            dbig = alloc_b.take(DT, B * SE)
            x3bf = alloc_b.take(DT, NE)
            for bf in (*pq2, *pk2, *pv2, *numa2, tmpE, tm, x2f, d32, attb, zb,
                       dbig, x3bf):
                bf.guard = e1done
            rowsg = [()]
            b3cell = [0]
            d32g = [e1done, e1done]
            d32r = [(), ()]

            def dbsl(h, bp):
                base = dbig.off + (h * B + bp) * SE
                return dbig.t[:, base:base + SE]

            def front_head(i):
                s0 = i * SE
                par = i % 2
                pq, pk, vf = pq2[par], pk2[par], pv2[par]
                for bf, nmsp in ((pq, "sp_pq"), (pk, "sp_pk"), (vf, "sp_pv")):
                    prog.need("sync", bf.guard)
                    srcv = spv(nmsp).rearrange("p t (b n) -> p t b n", b=B)
                    _t = []
                    for bb in range(B):
                        _t.append(prog.dma("sync", "ld",
                                           bf.v4(B)[:, :, bb, :],
                                           srcv[:, :, bb, s0:s0 + SE]))
                    bf.ready = t_mer(*_t)

                def pkb(bb):
                    return pk.v4(B)[:, :, bb, :]
                prog.need("vector", t_mer(pk.ready, zb.guard))
                prog.emit("vector", lambda e, o=zb.v3(), a=pkb(0), b_=pkb(1):
                          e.tensor_add(out=o, in0=a, in1=b_))
                prog.emit("vector", lambda e, o=zb.v3(), a=pkb(2):
                          e.tensor_add(out=o, in0=o, in1=a))
                vv = prog.inc("vector", lambda e, o=zb.v3(), a=pkb(3):
                              e.tensor_add(out=o, in0=o, in1=a), "v", 1)
                zb.ready = (("v", vv),)

                zbc = zb.all().rearrange("p (t o n) -> p t o n", t=DT,
                                         o=1).broadcast_to([128, DT, B, SE])
                prog.need("vector", t_mer(pq.ready, zb.ready, dbig.guard))
                vv = prog.inc("vector", lambda e,
                              o=dbig.all().rearrange("p (t b n) -> p t b n",
                                                     t=DT, b=B),
                              a=pq.v4(B), z=zbc:
                              e.tensor_tensor(out=o, in0=a, in1=z,
                                              op=ALU.mult), "v", 1)
                dbig.ready = (("v", vv),)
                prog.need("tensor", t_mer(dbig.ready, bankfree.get(7, ())))
                pev_den = 0
                for bb in range(B):
                    for h in range(DT):
                        row = h * 4 + bb
                        last_mm = (bb == B - 1 and h == DT - 1)
                        bld = lambda e, row=row, r_=dbsl(h, bb), lm=last_mm, \
                            fd=(bb == 0 and h == 0): e.matmul(
                                pb[7][0:32, 0:SE], OH2[:, 128 - row:160 - row],
                                r_, start=fd, stop=lm)
                        if last_mm:
                            pev_den = prog.inc("tensor", bld, "pe", 1)
                        else:
                            prog.emit("tensor", bld)
                dbig.guard = (("pe", pev_den),)
                prog.wait("vector", "pe", pev_den)
                prog.need("vector", d32g[par])
                prog.emit("vector", lambda e, o=pb[7][0:32, SE:2 * SE]:
                          e.tensor_scalar(out=o, in0=pb[7][0:32, 0:SE],
                                          scalar1=ATT_EPS, scalar2=None,
                                          op0=ALU.add))
                vv = prog.inc("vector", lambda e,
                              o=d32.t[0:32, d32.off + par * SE:
                                      d32.off + (par + 1) * SE]:
                              e.reciprocal(out=o, in_=pb[7][0:32, SE:2 * SE]),
                              "v", 1)
                d32r[par] = (("v", vv),)
                bankfree[7] = (("v", vv),)

            def front_tail(i):
                par = i % 2
                pq, pk, vf, nm = pq2[par], pk2[par], pv2[par], numa2[par]

                def pqb(bb):
                    return pq.v4(B)[:, :, bb, :]
                def vfb4(bb, hlo):
                    return vf.all().rearrange(
                        "p (t b n) -> p t b n", t=DT, b=B)[:, hlo:hlo + 4, bb, :]
                def nm4(bb, hlo):
                    return nm.all().rearrange(
                        "p (t b n) -> p t b n", t=DT, b=B)[:, hlo:hlo + 4, bb, :]
                def tm4(bb, hlo):
                    return tm.all().rearrange(
                        "p (t b n) -> p t b n", t=DT, b=B)[:, hlo:hlo + 4, bb, :]

                for bb in range(B):
                    pqc = pq.v4(B)[:, :, bb:bb + 1, :].broadcast_to(
                        [128, DT, B, SE])
                    prog.need("vector", t_mer(pq.ready, pk.ready, dbig.guard))
                    vv = prog.inc("vector", lambda e,
                                  o=dbig.all().rearrange(
                                      "p (t b n) -> p t b n", t=DT, b=B),
                                  a=pqc, b_=pk.v4(B):
                                  e.tensor_tensor(out=o, in0=a, in1=b_,
                                                  op=ALU.mult), "v", 1)
                    dbig.ready = (("v", vv),)
                    pevA = 0
                    for bp in range(B):
                        bkA = (2, 3) if bp % 2 == 0 else (4, 5)
                        prog.need("tensor", t_mer(dbig.ready,
                                                  bankfree.get(bkA[0], ()),
                                                  bankfree.get(bkA[1], ())))
                        for h in range(DT):
                            bnk = bkA[0] if h < 4 else bkA[1]
                            cof = (h % 4) * SE
                            bld = lambda e, bnk=bnk, cof=cof, r_=dbsl(h, bp): \
                                e.matmul(pb[bnk][:, cof:cof + SE], allones,
                                         r_, start=True, stop=True)
                            if h == DT - 1:
                                pevA = prog.inc("tensor", bld, "pe", 1)
                            else:
                                prog.emit("tensor", bld)
                        prog.need("vector", t_mer((("pe", pevA),), vf.ready,
                                                  nm.guard if bp == 0 else (),
                                                  tm.guard if bp > 0 else ()))
                        for half in range(2):
                            bnk = bkA[half]
                            ps4 = pb[bnk][:, 0:4 * SE].rearrange(
                                "p (h n) -> p h n", h=4)
                            if bp == 0:
                                bld = (lambda e, o=nm4(bb, half * 4),
                                       i=vfb4(0, half * 4), ps4=ps4:
                                       e.tensor_mul(out=o, in0=i, in1=ps4))
                            else:
                                bld = (lambda e, o=tm4(bb, half * 4),
                                       i=vfb4(bp, half * 4), ps4=ps4:
                                       e.tensor_mul(out=o, in0=i, in1=ps4))
                            if half == 1:
                                vv = prog.inc("vector", bld, "v", 1)
                            else:
                                prog.emit("vector", bld)
                        bankfree[bkA[0]] = (("v", vv),)
                        bankfree[bkA[1]] = (("v", vv),)
                        if bp > 0:
                            prog.wait("gpsimd", "v", vv)
                            gv = prog.inc("gpsimd", lambda e, o=nm4(bb, 0),
                                          i=tm4(bb, 0): e.tensor_add(
                                              out=o, in0=o, in1=i), "g", 1)
                            gv = prog.inc("gpsimd", lambda e, o=nm4(bb, 4),
                                          i=tm4(bb, 4): e.tensor_add(
                                              out=o, in0=o, in1=i), "g", 1)
                            tm.guard = (("g", gv),)
                    dbig.guard = (("pe", pevA),)
                nm.ready = (("g", prog.cnt["g"]), ("v", prog.cnt["v"]))
                pq.guard = (("v", prog.cnt["v"]),)
                pk.guard = (("v", prog.cnt["v"]),)
                vf.guard = (("v", prog.cnt["v"]),)

            def back(i):
                s0 = i * SE
                par = i % 2
                nm = numa2[par]
                def nm4(bb, hlo):
                    return nm.all().rearrange(
                        "p (t b n) -> p t b n", t=DT, b=B)[:, hlo:hlo + 4, bb, :]
                prog.need("sync", x2f.guard)
                srcv = spv("sp_x2").rearrange("p t (b n) -> p t b n", b=B)
                _t = []
                for bb in range(B):
                    _t.append(prog.dma("sync", "ld", x2f.v4(B)[:, :, bb, :],
                                       srcv[:, :, bb, s0:s0 + SE]))
                x2f.ready = t_mer(*_t)
                prog.need("gpsimd", t_mer(d32r[par], rowsg[0]))
                rden_edge = prog.dma(
                    "gpsimd", "gld",
                    ROWS[0:1, rws["rden"]:rws["rden"] + 32 * SE]
                    .rearrange("o (r n) -> o r n", r=32),
                    d32.t[0:32, d32.off + par * SE:d32.off + (par + 1) * SE])
                d32g[par] = rden_edge

                RD = ROWS[0:1, rws["rden"]:rws["rden"] + 32 * SE].rearrange(
                    "o (h x n) -> o h x n", h=8, x=4)
                pevR = 0
                for bb in range(B):
                    bkR = (6, 7) if bb % 2 == 0 else (0, 1)
                    prog.need("tensor", t_mer(rden_edge,
                                              bankfree.get(bkR[0], ()),
                                              bankfree.get(bkR[1], ())))
                    for half in range(2):
                        bnk = bkR[half]
                        rhs = RD[:, half * 4:half * 4 + 4, bb, :]
                        bld = lambda e, bnk=bnk, rhs=rhs: e.matmul(
                            pb[bnk][:, 0:4 * SE], R("onesf", 0, 128), rhs,
                            start=True, stop=True)
                        if half == 1:
                            pevR = prog.inc("tensor", bld, "pe", 1)
                        else:
                            prog.emit("tensor", bld)
                    prog.need("vector", t_mer((("pe", pevR),), nm.ready,
                                              attb.guard))
                    for half in range(2):
                        bnk = bkR[half]
                        ps4 = pb[bnk][:, 0:4 * SE].rearrange("p (h n) -> p h n",
                                                             h=4)
                        ab4 = attb.all().rearrange(
                            "p (t b n) -> p t b n",
                            t=DT, b=B)[:, half * 4:half * 4 + 4, bb, :]
                        bld = lambda e, o=ab4, i=nm4(bb, half * 4), ps4=ps4: \
                            e.tensor_mul(out=o, in0=i, in1=ps4)
                        if half == 1:
                            vv = prog.inc("vector", bld, "v", 1)
                        else:
                            prog.emit("vector", bld)
                    bankfree[bkR[0]] = (("v", vv),)
                    bankfree[bkR[1]] = (("v", vv),)
                attb.ready = (("v", prog.cnt["v"]),)
                rowsg[0] = (("pe", pevR),)

                def epi_ob(ep, bk, pev, lo, w):
                    prog.need("vector", t_mer((("pe", pev),), tmpE.guard,
                                              pre_ld))
                    vv = prog.inc("vector", lambda eng, bk=bk, w=w, lo=lo,
                                  s1=M("bo", ep, 1): eng.tensor_scalar(
                                      out=tmpE.rg(lo, w), in0=pb[bk][:, 0:w],
                                      scalar1=s1, scalar2=None, op0=ALU.add),
                                  "v", 1)
                    return (("v", vv),)
                proj(3 * D * DT, attb, NE, DT, DT, epi_ob, banks=(0, 1),
                     single=True)
                tmpE.ready = (("v", prog.cnt["v"]),)
                attb.guard = (("pe", prog.cnt["pe"]),)
                prog.need("vector", t_mer(tmpE.ready, x2f.ready, attb.ready))
                vv = prog.inc("vector", lambda e, o=nm.all(), a=tmpE.all(),
                              b_=x2f.all(): e.tensor_add(out=o, in0=a, in1=b_),
                              "v", 1)
                nm.ready = (("v", vv),)
                x2f.guard = nm.ready
                prog.need("sync", nm.ready)
                dstv = spv("sp_x3").rearrange("p t (b n) -> p t b n", b=B)
                _t = []
                for bb in range(B):
                    _t.append(prog.dma("sync", "hst",
                                       dstv[:, :, bb, s0:s0 + SE],
                                       nm.v4(B)[:, :, bb, :]))
                stv = t_mer(*_t)

                prog.need("vector", t_mer(nm.ready, x3bf.guard))
                vv = prog.inc("vector", lambda e, o=x3bf.all(), i=nm.all():
                              e.tensor_copy(out=o, in_=i), "v", 1)
                x3bf.ready = (("v", vv),)
                pev = ln_stats(x3bf, attb, NE, x3bf.ready)
                x3bf.guard = (("pe", pev),)
                ln_apply(nm, tmpE, pev, NE)
                nm.guard = t_mer(stv, tmpE.ready)
                if i == C.NTE - 1:
                    prog.need("vector", tmpE.ready)
                    for d in range(DT):
                        for bb in range(B):
                            bld = lambda e, d=d, bb=bb: e.tensor_copy(
                                out=M("bnd3", d * B + bb, 1),
                                in_=tmpE.sl(d, bb * SE + SE - 1, 1))
                            if d == DT - 1 and bb == B - 1:
                                b3cell[0] = prog.inc("vector", bld, "v", 1)
                            else:
                                prog.emit("vector", bld)
                prog.need("sync", tmpE.ready)
                dstv = spv("sp_hf2").rearrange("p t (b n) -> p t b n", b=B)
                _t = []
                for bb in range(B):
                    _t.append(prog.dma("sync", "hst",
                                       dstv[:, :, bb, s0:s0 + SE],
                                       tmpE.v4(B)[:, :, bb, :]))
                tmpE.guard = t_mer(*_t, (("v", b3cell[0]),)
                                   if i == C.NTE - 1 else ())

            for i in range(C.NTE + 1):
                if i < C.NTE:
                    front_head(i)
                if i >= 1:
                    back(i - 1)
                if i < C.NTE:
                    front_tail(i)

            return b3cell[0]
        b3pub = pass_e()
        bnd3_edge = boundary_collective("bnd3", "B3_loc", "B3_all", "B3g",
                                        b3pub)
        prog.barrier()
        alloc_f.reset()
        alloc_b.reset()

        _wt.append(load_w("fWrT", 0 * D * DT, D, D))
        _wt.append(load_w("fWkT", 1 * D * DT, D, D))
        _wt.append(load_w("fWvT", 2 * D * DT, D, D))
        wready[0] = wdone()
        ffn_pass("sp_hf2", "sp_x3", "y_out", None, "bnd3", bnd3_edge, True)

        if taps:
            prog.barrier()
            for nm, tap in taps.items():
                prog.dma("sync", "hst", tap, ap[nm])
        for i in range(4):
            prog.wait("sync", f"st{i}", prog.cnt[f"st{i}"])
        for i in range(2):
            prog.wait("sync", f"hst{i}", prog.cnt[f"hst{i}"])
        for i in range(4):
            prog.wait("sync", f"ld{i}", prog.cnt[f"ld{i}"])
        for i in range(2):
            prog.wait("sync", f"gld{i}", prog.cnt[f"gld{i}"])

        with nc.Block() as block:
            @block.sync
            def _(eng):
                for f in prog.items["sync"]:
                    f(eng)

            @block.scalar
            def _(eng):
                for f in prog.items["scalar"]:
                    f(eng)

            @block.vector
            def _(eng):
                for f in prog.items["vector"]:
                    f(eng)

            @block.tensor
            def _(eng):
                for f in prog.items["tensor"]:
                    f(eng)

            @block.gpsimd
            def _(eng):
                for f in prog.items["gpsimd"]:
                    f(eng)

    return nc



def make_inputs(C, core, **inp):
    s_lo = core * C.SLOC
    s_hi = s_lo + C.SLOC
    x = inp["L0_attn_a_x"]
    xs = np.ascontiguousarray(
        x[:, s_lo:s_hi, :].transpose(2, 0, 1)).reshape(D, C.TOK)

    def wt(a):
        return np.ascontiguousarray(np.asarray(a).T)

    def pm(v):
        return np.ascontiguousarray(np.asarray(v).reshape(DT, 128).T)

    pos = np.arange(B, dtype=np.float32)[:, None]
    div = np.exp(np.arange(0, D, 2, dtype=np.float32)
                 * (-np.log(10000.0) / D))
    pe = np.zeros((B, D), dtype=np.float32)
    pe[:, 0::2] = np.sin(pos * div)
    pe[:, 1::2] = np.cos(pos * div)
    pe_pm = np.zeros((128, DT * B), np.float32)
    for t in range(DT):
        for b in range(B):
            pe_pm[:, t * B + b] = pe[b, t * 128:(t + 1) * 128]

    hmask = np.zeros((128, 8), np.float32)
    hexpo = np.zeros((128, 8), np.float32)
    bmask = np.zeros((128, 8), np.float32)
    for j in range(NCORES):
        if j < core:
            hmask[:, j] = 1.0
            hexpo[:, j] = 1024.0 * 0 + float(C.SLOC * (core - 1 - j))
        if j == core - 1:
            bmask[:, j] = 1.0

    m = dict(
        x0=xs,
        WrT=wt(inp["L0_attn_a_Wr"]), WwT=wt(inp["L0_attn_a_Ww"]),
        WkT=wt(inp["L0_attn_a_Wk"]), WvT=wt(inp["L0_attn_a_Wv"]),
        WoT=wt(inp["L0_attn_a_Wo"]), WrecT=wt(inp["L0_attn_a_Wrec"]),
        WkeyT=wt(inp["L0_attn_a_Wkey"]), WvalT=wt(inp["L0_attn_a_Wval"]),
        bWqT=wt(inp["L0_attn_b_Wq"]), bWkT=wt(inp["L0_attn_b_Wk"]),
        bWvT=wt(inp["L0_attn_b_Wv"]), bWoT=wt(inp["L0_attn_b_Wo"]),
        fWrT=wt(inp["L0_ffn_W_r"]), fWkT=wt(inp["L0_ffn_W_k"]),
        fWvT=wt(inp["L0_ffn_W_v"]),
        g1=pm(inp["L0_attn_a_ln1_g"]), b1=pm(inp["L0_attn_a_ln1_b"]),
        g2=pm(inp["L0_attn_a_ln2_g"]), b2=pm(inp["L0_attn_a_ln2_b"]),
        td=pm(inp["L0_attn_a_time_decay"]),
        mur=pm(inp["L0_ffn_mu_r"]), muk=pm(inp["L0_ffn_mu_k"]),
        bq=pm(inp["L0_attn_b_bq"]), bk=pm(inp["L0_attn_b_bk"]),
        bv=pm(inp["L0_attn_b_bv"]), bo=pm(inp["L0_attn_b_bo"]),
        pe=pe_pm, hmask=hmask, hexpo=hexpo, bmask=bmask,
    )
    return {k: np.ascontiguousarray(np.asarray(v, dtype=m[k].dtype
                                               if hasattr(v, "dtype")
                                               else np.float32))
            for k, v in m.items()}


def kernel(**inputs):
    C = CFG(sloc=1024, tca=256, se=128)
    nc = build_program(C)
    in_maps = [make_inputs(C, core, **inputs) for core in range(NCORES)]
    res = run_bass_kernel_spmd(nc, in_maps, core_ids=list(range(NCORES)))
    outs = []
    for core in range(NCORES):
        y = res.results[core]["y_out"]
        outs.append(y.reshape(D, B, C.SLOC).transpose(1, 2, 0))
    return np.concatenate(outs, axis=1).astype(np.float32)
